# version 05
# baseline (speedup 1.0000x reference)
import numpy as np
import ml_dtypes
from contextlib import ExitStack

import concourse.bass as bass
import concourse.mybir as mybir
from concourse.bass_utils import run_bass_kernel_spmd

F32 = mybir.dt.float32
BF16 = mybir.dt.bfloat16
I32 = mybir.dt.int32
U8 = mybir.dt.uint8
AF = mybir.ActivationFunctionType
ALU = mybir.AluOpType
AX = mybir.AxisListType
BF = ml_dtypes.bfloat16

B, S, D, H, DH = 4, 2048, 4096, 32, 128
INTER = 11008
INTER_PAD = 11264
NT = 1024
CLIP = 8.0
LN_EPS = 1e-5
QSCALE = float(DH) ** -0.5
NEG_BIG = -1.0e35

STRIPS = {0: [0, 3, 4, 7, 8, 11, 12, 15], 1: [1, 2, 5, 6, 9, 10, 13, 14]}
SLOT_EXT = [2, 4, 6, 8, 10, 12, 14, 16]
SLOT_CHUNKS = [(e * 128 + 511) // 512 for e in SLOT_EXT]
REL_BASE = np.cumsum([0] + SLOT_CHUNKS).tolist()
N_REL = REL_BASE[-1]
SLOPES = [2.0 ** (-(h + 1) * 0.25) for h in range(H)]
PERM_D = np.array([8 * r + s for s in range(8) for r in range(512)], dtype=np.int64)

TAPS = False
STOP_AFTER = ""
ENGINES = ("sync", "tensor", "vector", "scalar", "gpsimd")


class Sem:
    def __init__(self, h):
        self.h = h
        self.n = 0

    def inc(self, k=1):
        self.n += k
        return (self.h, k)

    def tgt(self):
        return (self.h, self.n)


class Plan:
    def __init__(self):
        self.ops = {e: [] for e in ENGINES}

    def add(self, eng, fn, waits=(), incs=(), drain=False):
        w = [x for x in waits if x is not None]
        self.ops[eng].append((w, fn, list(incs), drain))


def _run_ops(eng, ops, use_drain=False):
    for waits, fn, incs, drain in ops:
        best = {}
        for (h, v) in waits:
            key = id(h)
            if key not in best or best[key][1] < v:
                best[key] = (h, v)
        for (h, v) in best.values():
            eng.wait_ge(h, v)
        inst = fn(eng)
        if incs and (use_drain or drain) and "DMA" not in type(inst.ins).__name__:
            inst = eng.drain()
        for (h, k) in incs:
            inst.then_inc(h, k)


def emit(nc, plan):
    with nc.Block() as block:
        for e in ENGINES:
            ops = plan.ops[e]
            if ops:
                getattr(block, e)(
                    lambda eng, ops=ops: _run_ops(eng, ops, False))


def build_nc():
    nc = bass.Bass("TRN2", target_bir_lowering=False)

    def din(name, shape, dtype):
        return nc.dram_tensor(name, shape, dtype, kind="ExternalInput")

    def dscr(name, shape, dtype):
        return nc.dram_tensor(name, shape, dtype,
                              kind="ExternalOutput" if TAPS else "Internal")

    x_own = din("x_own", [D, NT], F32)
    x_full = din("x_full", [D, S], F32)
    wqkv_st = din("wqkv_st", [96, 128, 32 * 128], F32)
    wout_st = din("wout_st", [32, 128, 32 * 128], F32)
    qlnw = din("qlnw", [128, 32], F32)
    qlnb = din("qlnb", [128, 32], F32)
    klnw = din("klnw", [128, 32], F32)
    klnb = din("klnb", [128, 32], F32)
    qw_gate = din("qw_gate", [512, INTER], I32)
    qw_up = din("qw_up", [512, INTER], I32)
    qw_down = din("qw_down", [1408, D], I32)
    sc_guq = din("sc_guq", [128, INTER], F32)
    qz_gate = din("qz_gate", [32, 1376], I32)
    qz_up = din("qz_up", [32, 1376], I32)
    sc_down = din("sc_down", [88, D], F32)
    qz_down = din("qz_down", [88, 512], I32)
    rel_in = din("rel", [N_REL, 128, 512], F32)
    sel_gu = din("sel_gu", [64, 4 * 128], BF16)
    sel_dn = din("sel_dn", [88, 11 * 128], BF16)
    ident_u8 = din("ident_u8", [128, 128], U8)
    ident_bf = din("ident_bf", [128, 128], BF16)
    ones_bf = din("ones_bf", [128, 1], BF16)
    ones1f = din("ones1f", [1, 128], F32)

    out_t = nc.dram_tensor("out", [D, NT], F32, kind="ExternalOutput")

    q_pre = dscr("q_pre", [32, 128, NT], BF16)
    k_pre = dscr("k_pre", [32, 128, S], BF16)
    q_d = dscr("q_d", [32, 128, NT], BF16)
    kT_d = dscr("kT_d", [32, 128, S], BF16)
    v_d = dscr("v_d", [32, 16, 128, 128], BF16)
    r1_d = dscr("r1_d", [32, 128, NT], F32)
    x1_d = dscr("x1_d", [32, 128, NT], F32)
    h_d = dscr("h_d", [INTER_PAD, NT], BF16)
    wd_d = dscr("wd_d", [88, 128, D], BF16)
    ab_dram = nc.dram_tensor("ab_dram", [1, 4096], F32, kind="Internal")
    oT_tap = dscr("oT_tap", [128, 32 * NT], BF16) if TAPS else None
    rows_tap = dscr("rows_tap", [1, 6 * S], F32) if TAPS else None
    s1_tap = dscr("s1_tap", [1, 2 * S], F32) if TAPS else None
    r2_tap = dscr("r2_tap", [128, 32 * NT], F32) if TAPS else None

    root = ExitStack()
    with root:
        R = root.enter_context
        sb_id8 = R(nc.sbuf_tensor("sb_id8", [128, 128], U8))
        sb_idbf = R(nc.sbuf_tensor("sb_idbf", [128, 128], BF16))
        sb_1bf = R(nc.sbuf_tensor("sb_1bf", [128, 1], BF16))
        sb_11f = R(nc.sbuf_tensor("sb_11f", [1, 128], F32))
        sb_qlnw = R(nc.sbuf_tensor("sb_qlnw", [128, 32], F32))
        sb_qlnb = R(nc.sbuf_tensor("sb_qlnb", [128, 32], F32))
        sb_klnw = R(nc.sbuf_tensor("sb_klnw", [128, 32], F32))
        sb_klnb = R(nc.sbuf_tensor("sb_klnb", [128, 32], F32))
        guq_stack = ExitStack()
        guq_s = guq_stack.enter_context(nc.sbuf_tensor("guq_s", [64, INTER], BF16, side="right"))
        guq_z = guq_stack.enter_context(nc.sbuf_tensor("guq_z", [64, INTER], BF16, side="right"))
        sb_eps = R(nc.sbuf_tensor("sb_eps", [128, 1], F32))

        sems = {nm: Sem(R(nc.semaphore(nm))) for nm in
                ("s_pe", "s_dve", "s_act", "s_pool",
                 "ld0", "ld1", "ld2", "ld3", "w0", "w1", "w2", "w3",
                 "s0", "s1", "s2", "s3", "x0", "x1", "h0", "h1",
                 "xl", "ml", "bn", "v0")}
        s_pe, s_dve, s_act, s_pool = (sems[k] for k in
                                      ("s_pe", "s_dve", "s_act", "s_pool"))
        LD = [sems["ld0"], sems["ld1"], sems["ld2"], sems["ld3"]]
        W = [sems["w0"], sems["w1"], sems["w2"], sems["w3"]]
        ST = [sems["s0"], sems["s1"], sems["s2"], sems["s3"]]
        X = [sems["x0"], sems["x1"]]
        HS = [sems["h0"], sems["h1"]]
        xl, ml, bn, v0 = sems["xl"], sems["ml"], sems["bn"], sems["v0"]

        def ab_chain(P, rows, n, stats_wait, s1_src, s2_src):
            P.add("vector", lambda e: e.tensor_copy(rows[0:1, 0:n], s1_src),
                  waits=[stats_wait])
            P.add("vector", lambda e: e.tensor_copy(rows[0:1, n:2 * n], s2_src),
                  incs=[s_dve.inc()], drain=True)
            P.add("vector", lambda e: e.tensor_scalar_mul(
                rows[0:1, 2 * n:3 * n], rows[0:1, 0:n], 1.0 / D),
                waits=[s_dve.tgt()], incs=[s_dve.inc()], drain=True)
            P.add("scalar", lambda e: e.activation(
                out=rows[0:1, 3 * n:4 * n], in_=rows[0:1, 2 * n:3 * n],
                func=AF.Square), waits=[s_dve.tgt()], incs=[s_act.inc()])
            P.add("vector", lambda e: e.scalar_tensor_tensor(
                out=rows[0:1, 3 * n:4 * n], in0=rows[0:1, n:2 * n],
                scalar=1.0 / D, in1=rows[0:1, 3 * n:4 * n],
                op0=ALU.mult, op1=ALU.subtract),
                waits=[s_act.tgt()], incs=[s_dve.inc()], drain=True)
            P.add("scalar", lambda e: e.activation(
                out=rows[0:1, 4 * n:5 * n], in_=rows[0:1, 3 * n:4 * n],
                func=AF.Sqrt, bias=sb_eps[0:1, 0:1]),
                waits=[s_dve.tgt()], incs=[s_act.inc()])
            P.add("vector", lambda e: e.reciprocal(
                rows[0:1, 4 * n:5 * n], rows[0:1, 4 * n:5 * n]),
                waits=[s_act.tgt()], incs=[s_dve.inc()], drain=True)
            P.add("vector", lambda e: e.scalar_tensor_tensor(
                out=rows[0:1, 5 * n:6 * n], in0=rows[0:1, 2 * n:3 * n],
                scalar=-1.0, in1=rows[0:1, 4 * n:5 * n],
                op0=ALU.mult, op1=ALU.mult),
                waits=[s_dve.tgt()], incs=[s_dve.inc()], drain=True)
            return s_dve.tgt()

        def ab_bounce(P, rows, n, ab_done, ab_sb):
            P.add("sync", lambda e: e.dma_start(
                out=ab_dram.ap()[0:1, 0:2 * n], in_=rows[0:1, 4 * n:6 * n]),
                waits=[ab_done], incs=[bn.inc(16)])
            P.add("sync", lambda e: e.dma_start(
                out=ab_sb[:, 0:2 * n],
                in_=bass.AP(ab_dram, 0, [[0, 128], [1, 2 * n]])),
                waits=[bn.tgt()], incs=[bn.inc(16)])
            return bn.tgt()

        def ab_bcast(P, ps, rows, n, ab_done, a_off, b_off):
            nch = n // 512
            for c in range(nch):
                P.add("tensor", lambda e, c=c: e.matmul(
                    ps[:, a_off + 512 * c:a_off + 512 * (c + 1)], sb_11f[:, :],
                    rows[0:1, 4 * n + 512 * c:4 * n + 512 * (c + 1)],
                    start=True, stop=True),
                    waits=[ab_done] if c == 0 else [])
                P.add("tensor", lambda e, c=c: e.matmul(
                    ps[:, b_off + 512 * c:b_off + 512 * (c + 1)], sb_11f[:, :],
                    rows[0:1, 5 * n + 512 * c:5 * n + 512 * (c + 1)],
                    start=True, stop=True),
                    incs=[s_pe.inc()] if c == nch - 1 else [])
            return s_pe.tgt()

        with ExitStack() as st:
            E = st.enter_context
            guqf = E(nc.sbuf_tensor("guqf", [128, INTER], F32))
            zf_f = E(nc.sbuf_tensor("zf_f", [64, INTER], F32))
            zrows = E(nc.sbuf_tensor("zrows", [64, INTER + 1376], I32))
            P = Plan()
            for dst, src in ((sb_id8, ident_u8), (sb_idbf, ident_bf),
                             (sb_1bf, ones_bf), (sb_11f, ones1f),
                             (sb_qlnw, qlnw), (sb_qlnb, qlnb),
                             (sb_klnw, klnw), (sb_klnb, klnb)):
                P.add("sync", lambda e, d=dst, s=src: e.dma_start(out=d[:, :], in_=s.ap()),
                      incs=[ml.inc(16)])
            P.add("sync", lambda e: e.dma_start(out=guqf[:, :], in_=sc_guq.ap()),
                  incs=[ml.inc(16)])
            P.add("sync", lambda e: e.dma_start(
                out=zrows[0:32, INTER:INTER + 1376], in_=qz_gate.ap()),
                incs=[ml.inc(16)])
            P.add("sync", lambda e: e.dma_start(
                out=zrows[32:64, INTER:INTER + 1376], in_=qz_up.ap()),
                incs=[ml.inc(16)])
            P.add("vector", lambda e: e.memset(sb_eps[:, :], LN_EPS),
                  waits=[ml.tgt()])
            P.add("vector", lambda e: e.tensor_scalar_mul(
                sb_qlnw[:, :], sb_qlnw[:, :], QSCALE))
            P.add("vector", lambda e: e.tensor_scalar_mul(
                sb_qlnb[:, :], sb_qlnb[:, :], QSCALE))
            for s_ in range(8):
                P.add("vector", lambda e, s_=s_: e.tensor_scalar(
                    out=bass.AP(zrows, s_, [[INTER + 1376, 64], [8, 1376]]),
                    in0=zrows[0:64, INTER:INTER + 1376], scalar1=4 * s_, scalar2=15,
                    op0=ALU.logical_shift_right, op1=ALU.bitwise_and),
                    incs=[s_dve.inc()] if s_ == 7 else [])
            P.add("vector", lambda e: e.tensor_scalar_add(
                zf_f[0:64, 0:INTER], zrows[0:64, 0:INTER], 1.0),
                waits=[s_dve.tgt()], incs=[s_dve.inc()])
            P.add("vector", lambda e: e.tensor_mul(
                guq_z[0:64, :], zf_f[0:64, :], guqf[0:64, :]),
                waits=[s_dve.tgt()], incs=[s_dve.inc()])
            P.add("vector", lambda e: e.tensor_copy(guq_s[0:64, :], guqf[0:64, :]),
                  incs=[s_dve.inc()])
            emit(nc, P)

        with ExitStack() as st:
            E = st.enter_context
            x_own_bf = E(nc.sbuf_tensor("x_own_bf", [128, 32 * NT], BF16))
            wblk = [E(nc.sbuf_tensor(f"wblkA{i}", [128, 32 * 128], BF16)) for i in range(2)]
            stq = [E(nc.sbuf_tensor(f"stqA{i}", [128, NT], BF16)) for i in range(2)]
            ps = E(nc.psum_tensor("psA", [128, 2048], F32))
            P = Plan()
            for dt in range(32):
                P.add("gpsimd", lambda e, dt=dt: e.dma_start(
                    out=x_own_bf[:, NT * dt:NT * (dt + 1)],
                    in_=x_own.ap()[128 * dt:128 * (dt + 1), :]),
                    incs=[xl.inc(16)])
            x_rdy = xl.tgt()
            wfree = [None, None]
            evac_done = [None, None]
            stq_free = [None, None]
            for et in range(32):
                par = et % 2
                P.add("gpsimd", lambda e, et=et, par=par: e.dma_start(
                    out=wblk[par][:, :], in_=wqkv_st.ap()[et]),
                    waits=[wfree[par]], incs=[W[par].inc(16)])
                w_rdy = W[par].tgt()
                for dt in range(32):
                    for c in range(2):
                        P.add("tensor", lambda e, dt=dt, c=c, par=par: e.matmul(
                            ps[:, 1024 * par + 512 * c:1024 * par + 512 * (c + 1)],
                            wblk[par][:, 128 * dt:128 * (dt + 1)],
                            x_own_bf[:, NT * dt + 512 * c:NT * dt + 512 * (c + 1)],
                            start=(dt == 0), stop=(dt == 31)),
                            waits=[w_rdy, x_rdy, evac_done[par]] if (dt == 0 and c == 0) else [],
                            incs=[s_pe.inc()] if (dt == 31 and c == 1) else [],
                            drain=(dt == 31 and c == 1))
                mm_done = s_pe.tgt()
                wfree[par] = mm_done
                P.add("vector", lambda e, par=par: e.tensor_scalar(
                    out=stq[par][:, :], in0=ps[:, 1024 * par:1024 * (par + 1)],
                    scalar1=-CLIP, scalar2=CLIP, op0=ALU.max, op1=ALU.min),
                    waits=[mm_done, stq_free[par]], incs=[s_dve.inc()])
                evac_done[par] = s_dve.tgt()
                P.add("sync", lambda e, et=et, par=par: e.dma_start(
                    out=q_pre.ap()[et], in_=stq[par][:, :]),
                    waits=[evac_done[par]], incs=[ST[par].inc(16)])
                stq_free[par] = ST[par].tgt()
            P.add("sync", lambda e: e.nop(), waits=[ST[0].tgt(), ST[1].tgt()])
            P.add("gpsimd", lambda e: e.nop(),
                  waits=[xl.tgt(), W[0].tgt(), W[1].tgt()])
            emit(nc, P)

        with ExitStack() as st:
            E = st.enter_context
            x_full_bf = E(nc.sbuf_tensor("x_full_bf", [128, 32 * S], BF16))
            wblk = [E(nc.sbuf_tensor(f"wblkB{i}", [128, 32 * 128], BF16)) for i in range(2)]
            stkv = [E(nc.sbuf_tensor(f"stkvB{i}", [128, S], BF16)) for i in range(2)]
            stvt = E(nc.sbuf_tensor("stvtB", [128, S], BF16))
            ps = E(nc.psum_tensor("psB", [128, 2048], F32))
            vtr = E(nc.psum_tensor("psBv", [128, 1024], F32))
            P = Plan()
            for dt in range(32):
                P.add("gpsimd", lambda e, dt=dt: e.dma_start(
                    out=x_full_bf[:, S * dt:S * (dt + 1)],
                    in_=x_full.ap()[128 * dt:128 * (dt + 1), :]),
                    incs=[xl.inc(16)])
            x_rdy = xl.tgt()
            wfree = [None, None]
            evac_done = None
            st_free = [None, None]
            vt_free = None
            vt_dma = None
            for et in range(32, 96):
                par = et % 2
                is_k = et < 64
                P.add("gpsimd", lambda e, et=et, par=par: e.dma_start(
                    out=wblk[par][:, :], in_=wqkv_st.ap()[et]),
                    waits=[wfree[par]], incs=[W[par].inc(16)])
                w_rdy = W[par].tgt()
                for dt in range(32):
                    for c in range(4):
                        P.add("tensor", lambda e, dt=dt, c=c, par=par: e.matmul(
                            ps[:, 512 * c:512 * (c + 1)],
                            wblk[par][:, 128 * dt:128 * (dt + 1)],
                            x_full_bf[:, S * dt + 512 * c:S * dt + 512 * (c + 1)],
                            start=(dt == 0), stop=(dt == 31)),
                            waits=[w_rdy, x_rdy, evac_done] if (dt == 0 and c == 0) else [],
                            incs=[s_pe.inc()] if (dt == 31 and c == 3) else [],
                            drain=(dt == 31 and c == 3))
                mm_done = s_pe.tgt()
                wfree[par] = mm_done
                P.add("vector", lambda e, par=par: e.tensor_scalar(
                    out=stkv[par][:, :], in0=ps[:, :],
                    scalar1=-CLIP, scalar2=CLIP, op0=ALU.max, op1=ALU.min),
                    waits=[mm_done, st_free[par]], incs=[s_dve.inc()])
                evac_done = s_dve.tgt()
                if is_k:
                    P.add("sync", lambda e, et=et, par=par: e.dma_start(
                        out=k_pre.ap()[et - 32], in_=stkv[par][:, :]),
                        waits=[evac_done], incs=[ST[par].inc(16)])
                    st_free[par] = ST[par].tgt()
                else:
                    hh = et - 64
                    clip_done = evac_done
                    for half in range(2):
                        for j in range(8):
                            ts_ = 8 * half + j
                            P.add("tensor", lambda e, ts_=ts_, j=j, par=par: e.matmul(
                                vtr[:, 128 * j:128 * (j + 1)],
                                stkv[par][:, 128 * ts_:128 * (ts_ + 1)],
                                sb_idbf[:, :], start=True, stop=True),
                                waits=[clip_done, vt_free] if j == 0 else [],
                                incs=[s_pe.inc()] if j == 7 else [],
                                drain=(j == 7))
                        tr_done = s_pe.tgt()
                        P.add("vector", lambda e, half=half: e.tensor_copy(
                            stvt[:, 1024 * half:1024 * (half + 1)], vtr[:, :]),
                            waits=[tr_done, vt_dma], incs=[s_dve.inc()])
                        vt_free = s_dve.tgt()
                    st_free[par] = s_pe.tgt()
                    P.add("sync", lambda e, hh=hh: e.dma_start(
                        out=v_d.ap()[hh].transpose([1, 0, 2]),
                        in_=stvt[:, :].rearrange("p (a c) -> p a c", a=16, c=128)),
                        waits=[vt_free], incs=[v0.inc(16)])
                    vt_dma = v0.tgt()
            P.add("sync", lambda e: e.nop(),
                  waits=[ST[0].tgt(), ST[1].tgt(), v0.tgt()])
            P.add("gpsimd", lambda e: e.nop(),
                  waits=[xl.tgt(), W[0].tgt(), W[1].tgt()])
            emit(nc, P)

        def ln_qk(src_pre, dst_post, n_cols, w_col, b_col):
            nch = n_cols // 512
            with ExitStack() as st:
                E = st.enter_context
                qt = [E(nc.sbuf_tensor(f"p2t{i}_{n_cols}", [128, n_cols], BF16)) for i in range(2)]
                qsq = E(nc.sbuf_tensor(f"p2sq_{n_cols}", [128, n_cols], BF16))
                rows = E(nc.sbuf_tensor(f"p2rows_{n_cols}", [1, 6 * n_cols], F32))
                t1 = [E(nc.sbuf_tensor(f"p2a{i}_{n_cols}", [128, n_cols], F32)) for i in range(2)]
                ob = [E(nc.sbuf_tensor(f"p2o{i}_{n_cols}", [128, n_cols], BF16)) for i in range(2)]
                ab_sb = E(nc.sbuf_tensor(f"p2ab_{n_cols}", [128, 2 * n_cols], F32))
                ps = E(nc.psum_tensor(f"psP2_{n_cols}", [128, 4096], F32))
                P = Plan()
                ld_free = [None, None]
                sq_free = None
                for et in range(32):
                    par = et % 2
                    P.add("sync", lambda e, et=et, par=par: e.dma_start(
                        out=qt[par][:, :], in_=src_pre.ap()[et]),
                        waits=[ld_free[par]], incs=[LD[par].inc(16)])
                    t_rdy = LD[par].tgt()
                    for c in range(nch):
                        P.add("tensor", lambda e, c=c, par=par, et=et: e.matmul(
                            ps[0:1, 512 * c:512 * (c + 1)], sb_1bf[:, :],
                            qt[par][:, 512 * c:512 * (c + 1)],
                            start=(et == 0), stop=(et == 31)),
                            waits=[t_rdy] if c == 0 else [])
                    P.add("scalar", lambda e, par=par: e.activation(
                        out=qsq[:, :], in_=qt[par][:, :], func=AF.Square),
                        waits=[t_rdy, sq_free], incs=[s_act.inc()])
                    sq_rdy = s_act.tgt()
                    for c in range(nch):
                        P.add("tensor", lambda e, c=c, et=et: e.matmul(
                            ps[0:1, 2048 + 512 * c:2048 + 512 * (c + 1)], sb_1bf[:, :],
                            qsq[:, 512 * c:512 * (c + 1)],
                            start=(et == 0), stop=(et == 31)),
                            waits=[sq_rdy] if c == 0 else [],
                            incs=[s_pe.inc()] if c == nch - 1 else [],
                            drain=(c == nch - 1 and et == 31))
                    sq_free = s_pe.tgt()
                    ld_free[par] = s_pe.tgt()
                stats_done = s_pe.tgt()
                n = n_cols
                ab_done = ab_chain(P, rows, n, stats_done,
                                   ps[0:1, 0:n], ps[0:1, 2048:2048 + n])
                if TAPS and n_cols == S:
                    P.add("sync", lambda e: e.dma_start(
                        out=rows_tap.ap(), in_=rows[0:1, :]),
                        waits=[ab_done], incs=[ml.inc(16)])
                bdone = ab_bounce(P, rows, n, ab_done, ab_sb)
                ld_free = [[stats_done], [stats_done]]
                ap_done = [None, None]
                ob_free = [None, None]
                for et in range(32):
                    par = et % 2
                    P.add("sync", lambda e, et=et, par=par: e.dma_start(
                        out=qt[par][:, :], in_=src_pre.ap()[et]),
                        waits=ld_free[par], incs=[LD[par].inc(16)])
                    t_rdy = LD[par].tgt()
                    P.add("vector", lambda e, par=par: e.tensor_mul(
                        t1[par][:, :], qt[par][:, :], ab_sb[:, 0:n]),
                        waits=[t_rdy, bdone, ap_done[par]], incs=[s_dve.inc()])
                    P.add("vector", lambda e, par=par: e.tensor_add(
                        t1[par][:, :], t1[par][:, :], ab_sb[:, n:2 * n]),
                        waits=[s_dve.tgt()], incs=[s_dve.inc()])
                    tt_done = s_dve.tgt()
                    ld_free[par] = [tt_done]
                    P.add("scalar", lambda e, et=et, par=par: e.activation(
                        out=ob[par][:, :], in_=t1[par][:, :], func=AF.Identity,
                        bias=b_col[:, et:et + 1], scale=w_col[:, et:et + 1]),
                        waits=[tt_done, ob_free[par]], incs=[s_act.inc()])
                    ap_done[par] = s_act.tgt()
                    P.add("sync", lambda e, et=et, par=par: e.dma_start(
                        out=dst_post.ap()[et], in_=ob[par][:, :]),
                        waits=[ap_done[par]], incs=[ST[par].inc(16)])
                    ob_free[par] = ST[par].tgt()
                P.add("sync", lambda e: e.nop(),
                      waits=[ST[0].tgt(), ST[1].tgt(), ml.tgt()])
                emit(nc, P)

        ln_qk(q_pre, q_d, NT, sb_qlnw, sb_qlnb)
        ln_qk(k_pre, kT_d, S, sb_klnw, sb_klnb)
        if STOP_AFTER == "P2":
            return nc

        with ExitStack() as st_r:
            ab4_sb = st_r.enter_context(nc.sbuf_tensor("ab4_sb", [128, 2048], F32))

            with ExitStack() as stA:
                sb_oT = stA.enter_context(nc.sbuf_tensor("sb_oT", [128, 32 * NT], BF16))

                with ExitStack() as st:
                    E = st.enter_context
                    sb_rel = E(nc.sbuf_tensor("sb_rel", [128, N_REL * 512], F32))
                    kh = [E(nc.sbuf_tensor(f"kh{i}", [128, S], BF16)) for i in range(2)]
                    vh = [E(nc.sbuf_tensor(f"vh{i}", [128, S], BF16)) for i in range(2)]
                    qh = [E(nc.sbuf_tensor(f"qh{i}", [128, NT], BF16)) for i in range(2)]
                    Ssb = [E(nc.sbuf_tensor(f"Ssb{i}", [128, 2048], F32)) for i in range(2)]
                    Pbf = [E(nc.sbuf_tensor(f"Pbf{i}", [128, 2048], BF16)) for i in range(2)]
                    ptsb = [E(nc.sbuf_tensor(f"ptsb{i}", [128, 128], BF16)) for i in range(2)]
                    diag = [E(nc.sbuf_tensor(f"diag{i}", [128, 128], BF16)) for i in range(2)]
                    cols = [E(nc.sbuf_tensor(f"cols{i}", [128, 2], F32)) for i in range(2)]
                    ps = E(nc.psum_tensor("psP3", [128, 4096], F32))
                    P = Plan()
                    P.add("sync", lambda e: e.dma_start(
                        out=sb_rel[:, :].rearrange("p (n w) -> p n w", n=N_REL, w=512),
                        in_=rel_in.ap().transpose([1, 0, 2])),
                        incs=[ml.inc(16)])
                    rel_rdy = ml.tgt()

                    ss = {}
                    pv_last = {0: None, 1: None}
                    last_use = [None, None]

                    def plan_scores(h, i, hrdy):
                        base = 2048 * (i % 2)
                        Ei = SLOT_EXT[i]
                        nch = SLOT_CHUNKS[i]
                        hp = h % 2
                        w0 = [hrdy, rel_rdy, ss.get(("oev", i - 2)), ss.get(("sev", i - 2))]
                        for c in range(nch):
                            cw = min(512, Ei * 128 - 512 * c)
                            P.add("tensor", lambda e, i=i, c=c, cw=cw, base=base, hp=hp: e.matmul(
                                ps[:, base + 512 * c:base + 512 * c + cw],
                                qh[hp][:, 128 * i:128 * (i + 1)],
                                kh[hp][:, 512 * c:512 * c + cw],
                                start=True, stop=True),
                                waits=w0 if c == 0 else [],
                                incs=[s_pe.inc()] if c == nch - 1 else [],
                                drain=(c == nch - 1))
                        ss[("mm", i)] = s_pe.tgt()
                        for c in range(nch):
                            cw = min(512, Ei * 128 - 512 * c)
                            ri = REL_BASE[i] + c
                            P.add("vector", lambda e, i=i, c=c, cw=cw, base=base, ri=ri, h=h: e.scalar_tensor_tensor(
                                out=Ssb[i % 2][:, 512 * c:512 * c + cw],
                                in0=sb_rel[:, 512 * ri:512 * ri + cw],
                                scalar=SLOPES[h],
                                in1=ps[:, base + 512 * c:base + 512 * c + cw],
                                op0=ALU.mult, op1=ALU.add),
                                waits=[ss[("mm", i)], ss.get(("exp", i - 2))] if c == 0 else [],
                                incs=[s_dve.inc()] if c == nch - 1 else [])
                        ss[("sev", i)] = s_dve.tgt()
                        P.add("vector", lambda e, i=i, Ei=Ei: e.reduce_max(
                            cols[i % 2][:, 0:1], Ssb[i % 2][:, 0:Ei * 128],
                            axis=AX.X, negate=True),
                            waits=[ss.get(("diag", i - 2))],
                            incs=[s_dve.inc()])
                        mx = s_dve.tgt()
                        P.add("scalar", lambda e, i=i, Ei=Ei: e.activation(
                            out=Pbf[i % 2][:, 0:Ei * 128], in_=Ssb[i % 2][:, 0:Ei * 128],
                            func=AF.Exp, bias=cols[i % 2][:, 0:1],
                            accum_out=cols[i % 2][:, 1:2]),
                            waits=[mx, ss.get(("ptdone", i - 2))], incs=[s_act.inc()])
                        ss[("exp", i)] = s_act.tgt()
                        P.add("vector", lambda e, i=i: e.reciprocal(
                            cols[i % 2][:, 1:2], cols[i % 2][:, 1:2]),
                            waits=[ss[("exp", i)]], incs=[s_dve.inc()])
                        P.add("vector", lambda e, i=i: e.tensor_scalar_mul(
                            diag[i % 2][:, :], sb_id8[:, :], cols[i % 2][:, 1:2]),
                            waits=[s_dve.tgt(), ss.get(("ptdone", i - 2))],
                            incs=[s_dve.inc()])
                        ss[("diag", i)] = s_dve.tgt()

                    def plan_ptpv(h, i):
                        base = 2048 * (i % 2)
                        Ei = SLOT_EXT[i]
                        hp = h % 2
                        pt_off = [base, base + 512]
                        o_off = base + 1024
                        for kt in range(Ei):
                            pp = kt % 2
                            P.add("tensor", lambda e, i=i, kt=kt, pp=pp: e.matmul(
                                ps[:, pt_off[pp]:pt_off[pp] + 128],
                                Pbf[i % 2][:, 128 * kt:128 * (kt + 1)],
                                diag[i % 2][:, :], start=True, stop=True),
                                waits=([ss[("diag", i)]] if kt == 0 else []) +
                                      ([ss.get(("ptev", i, kt - 2))] if kt >= 2 else []),
                                incs=[s_pe.inc()], drain=True)
                            ptmm = s_pe.tgt()
                            P.add("vector", lambda e, pp=pp: e.tensor_copy(
                                ptsb[pp][:, :], ps[:, pt_off[pp]:pt_off[pp] + 128]),
                                waits=[ptmm, pv_last[pp]], incs=[s_dve.inc()])
                            ss[("ptev", i, kt)] = s_dve.tgt()
                            P.add("tensor", lambda e, i=i, kt=kt, pp=pp, hp=hp, Ei=Ei: e.matmul(
                                ps[:, o_off:o_off + 128],
                                vh[hp][:, 128 * kt:128 * (kt + 1)],
                                ptsb[pp][:, :],
                                start=(kt == 0), stop=(kt == Ei - 1)),
                                waits=[ss[("ptev", i, kt)]],
                                incs=[s_pe.inc()], drain=(kt == Ei - 1))
                            pv_last[pp] = s_pe.tgt()
                        ss[("ptdone", i)] = s_pe.tgt()
                        P.add("vector", lambda e, h=h, i=i, o_off=o_off: e.tensor_copy(
                            sb_oT[:, NT * h + 128 * i:NT * h + 128 * (i + 1)],
                            ps[:, o_off:o_off + 128]),
                            waits=[ss[("ptdone", i)]], incs=[s_dve.inc()])
                        ss[("oev", i)] = s_dve.tgt()

                    for h in range(H):
                        hp = h % 2
                        P.add("sync", lambda e, h=h, hp=hp: e.dma_start(
                            out=kh[hp][:, :], in_=kT_d.ap()[h]),
                            waits=[last_use[hp]], incs=[LD[hp].inc(16)])
                        P.add("sync", lambda e, h=h, hp=hp: e.dma_start(
                            out=vh[hp][:, :].rearrange("p (a c) -> p a c", a=16, c=128),
                            in_=v_d.ap()[h].transpose([1, 0, 2])),
                            incs=[LD[hp].inc(16)])
                        P.add("sync", lambda e, h=h, hp=hp: e.dma_start(
                            out=qh[hp][:, :], in_=q_d.ap()[h]),
                            incs=[LD[hp].inc(16)])
                        hrdy = LD[hp].tgt()
                        carry = {("oev", -2): ss.get(("oev", 6)),
                                 ("oev", -1): ss.get(("oev", 7)),
                                 ("sev", -2): ss.get(("sev", 6)),
                                 ("sev", -1): ss.get(("sev", 7)),
                                 ("exp", -2): ss.get(("exp", 6)),
                                 ("exp", -1): ss.get(("exp", 7)),
                                 ("diag", -2): ss.get(("diag", 6)),
                                 ("diag", -1): ss.get(("diag", 7)),
                                 ("ptdone", -2): ss.get(("ptdone", 6)),
                                 ("ptdone", -1): ss.get(("ptdone", 7))}
                        ss.clear()
                        ss.update(carry)
                        plan_scores(h, 0, hrdy)
                        plan_scores(h, 1, hrdy)
                        for i in range(8):
                            plan_ptpv(h, i)
                            if i + 2 < 8:
                                plan_scores(h, i + 2, hrdy)
                        last_use[hp] = s_pe.tgt()
                    P.add("sync", lambda e: e.nop(),
                          waits=[LD[0].tgt(), LD[1].tgt()])
                    emit(nc, P)

                if TAPS:
                    P = Plan()
                    P.add("sync", lambda e: e.dma_start(out=oT_tap.ap(), in_=sb_oT[:, :]),
                          incs=[ml.inc(16)])
                    P.add("sync", lambda e: e.nop(), waits=[ml.tgt()])
                    emit(nc, P)

                with ExitStack() as st:
                    E = st.enter_context
                    wblk = [E(nc.sbuf_tensor(f"wblkC{i}", [128, 32 * 128], BF16)) for i in range(2)]
                    xtile = [E(nc.sbuf_tensor(f"xtC{i}", [128, NT], F32)) for i in range(2)]
                    r1t = [E(nc.sbuf_tensor(f"r1C{i}", [128, NT], F32)) for i in range(2)]
                    r1b = E(nc.sbuf_tensor("r1bC", [128, NT], BF16))
                    rows4 = E(nc.sbuf_tensor("rows4", [1, 6 * NT], F32))
                    r1sq = E(nc.sbuf_tensor("r1sqC", [128, NT], BF16))
                    ps = E(nc.psum_tensor("psP4", [128, 4096], F32))
                    P = Plan()
                    wfree = [None, None]
                    xfree = [None, None]
                    r1free = [[], []]
                    mm_evd = [None, None]
                    sq_free = None
                    cast_free = None
                    for dt in range(32):
                        par = dt % 2
                        P.add("gpsimd", lambda e, dt=dt, par=par: e.dma_start(
                            out=wblk[par][:, :], in_=wout_st.ap()[dt]),
                            waits=[wfree[par]], incs=[W[par].inc(16)])
                        w_rdy = W[par].tgt()
                        P.add("sync", lambda e, dt=dt, par=par: e.dma_start(
                            out=xtile[par][:, :],
                            in_=x_own.ap()[128 * dt:128 * (dt + 1), :]),
                            waits=[xfree[par]], incs=[X[par].inc(16)])
                        x_rdy = X[par].tgt()
                        base = 1024 * par
                        for et in range(32):
                            for c in range(2):
                                P.add("tensor", lambda e, et=et, c=c, par=par, base=base: e.matmul(
                                    ps[:, base + 512 * c:base + 512 * (c + 1)],
                                    wblk[par][:, 128 * et:128 * (et + 1)],
                                    sb_oT[:, NT * et + 512 * c:NT * et + 512 * (c + 1)],
                                    start=(et == 0), stop=(et == 31)),
                                    waits=[w_rdy, mm_evd[par]] if (et == 0 and c == 0) else [],
                                    incs=[s_pe.inc()] if (et == 31 and c == 1) else [],
                                    drain=(et == 31 and c == 1))
                        mm_done = s_pe.tgt()
                        wfree[par] = mm_done
                        P.add("vector", lambda e, par=par, base=base: e.tensor_add(
                            r1t[par][:, :], xtile[par][:, :], ps[:, base:base + NT]),
                            waits=[mm_done, x_rdy] + r1free[par], incs=[s_dve.inc()])
                        r1_done = s_dve.tgt()
                        mm_evd[par] = r1_done
                        xfree[par] = r1_done
                        P.add("sync", lambda e, dt=dt, par=par: e.dma_start(
                            out=r1_d.ap()[dt], in_=r1t[par][:, :]),
                            waits=[r1_done], incs=[ST[par].inc(16)])
                        r1_dma = ST[par].tgt()
                        P.add("vector", lambda e, par=par: e.tensor_copy(
                            r1b[:, :], r1t[par][:, :]),
                            waits=[cast_free], incs=[s_dve.inc()])
                        b_rdy = s_dve.tgt()
                        P.add("scalar", lambda e, par=par: e.activation(
                            out=r1sq[:, :], in_=r1t[par][:, :], func=AF.Square),
                            waits=[r1_done, sq_free], incs=[s_act.inc()])
                        s_rdy = s_act.tgt()
                        for c in range(2):
                            P.add("tensor", lambda e, c=c, dt=dt: e.matmul(
                                ps[0:1, 2048 + 512 * c:2048 + 512 * (c + 1)], sb_1bf[:, :],
                                r1b[:, 512 * c:512 * (c + 1)],
                                start=(dt == 0), stop=(dt == 31)),
                                waits=[b_rdy] if c == 0 else [])
                            P.add("tensor", lambda e, c=c, dt=dt: e.matmul(
                                ps[0:1, 3072 + 512 * c:3072 + 512 * (c + 1)], sb_1bf[:, :],
                                r1sq[:, 512 * c:512 * (c + 1)],
                                start=(dt == 0), stop=(dt == 31)),
                                waits=[s_rdy] if c == 0 else [],
                                incs=[s_pe.inc()] if c == 1 else [],
                                drain=(c == 1 and dt == 31))
                        cast_free = s_pe.tgt()
                        sq_free = s_pe.tgt()
                        r1free[par] = [s_pe.tgt(), r1_dma]
                    stats_done = s_pe.tgt()
                    ab4 = ab_chain(P, rows4, NT, stats_done,
                                   ps[0:1, 2048:2048 + NT], ps[0:1, 3072:3072 + NT])
                    ab_bounce(P, rows4, NT, ab4, ab4_sb)
                    P.add("sync", lambda e: e.nop(),
                          waits=[ST[0].tgt(), ST[1].tgt(), bn.tgt()])
                    P.add("gpsimd", lambda e: e.nop(),
                          waits=[W[0].tgt(), W[1].tgt()])
                    emit(nc, P)

            if STOP_AFTER == "P4":
                return nc
            with ExitStack() as stX:
                sb_x1bf = stX.enter_context(nc.sbuf_tensor("sb_x1bf", [128, 32 * NT], BF16))

                with ExitStack() as st:
                    E = st.enter_context
                    r1t = [E(nc.sbuf_tensor(f"r1D{i}", [128, NT], F32)) for i in range(2)]
                    P = Plan()
                    bdone = None
                    ld_free = [[], []]
                    ap_done = [None, None]
                    for dt in range(32):
                        par = dt % 2
                        P.add("sync", lambda e, dt=dt, par=par: e.dma_start(
                            out=r1t[par][:, :], in_=r1_d.ap()[dt]),
                            waits=ld_free[par], incs=[LD[par].inc(16)])
                        t_rdy = LD[par].tgt()
                        P.add("vector", lambda e, par=par: e.tensor_mul(
                            r1t[par][:, :], r1t[par][:, :], ab4_sb[:, 0:NT]),
                            waits=[t_rdy, ap_done[par]], incs=[s_dve.inc()])
                        P.add("vector", lambda e, par=par: e.tensor_add(
                            r1t[par][:, :], r1t[par][:, :], ab4_sb[:, NT:2 * NT]),
                            waits=[s_dve.tgt()], incs=[s_dve.inc()])
                        x1_done = s_dve.tgt()
                        P.add("sync", lambda e, dt=dt, par=par: e.dma_start(
                            out=x1_d.ap()[dt], in_=r1t[par][:, :]),
                            waits=[x1_done], incs=[ST[par].inc(16)])
                        x1_dma = ST[par].tgt()
                        P.add("scalar", lambda e, dt=dt, par=par: e.activation(
                            out=sb_x1bf[:, NT * dt:NT * (dt + 1)], in_=r1t[par][:, :],
                            func=AF.Copy), waits=[x1_done], incs=[s_act.inc()])
                        ap_done[par] = s_act.tgt()
                        ld_free[par] = [x1_dma, ap_done[par]]
                    P.add("sync", lambda e: e.nop(),
                          waits=[ST[0].tgt(), ST[1].tgt()])
                    emit(nc, P)

                with ExitStack() as st:
                    E = st.enter_context
                    sb_selgu = E(nc.sbuf_tensor("sb_selgu", [64, 4 * 128], BF16))
                    qst = [E(nc.sbuf_tensor(f"qst{i}", [128, 256], I32)) for i in range(2)]
                    w4i = [E(nc.sbuf_tensor(f"w4i{i}", [128, 256], I32)) for i in range(2)]
                    wc = [[E(nc.sbuf_tensor(f"wc{s}{i}", [128, 32 * 256], BF16))
                           for i in range(2)] for s in range(2)]
                    silt = E(nc.sbuf_tensor("silt", [128, 512], F32))
                    hst = [E(nc.sbuf_tensor(f"hst{i}", [128, NT], BF16)) for i in range(2)]
                    ps = E(nc.psum_tensor("psP5", [128, 4096], F32))
                    P = Plan()
                    P.add("sync", lambda e: e.dma_start(out=sb_selgu[:, :], in_=sel_gu.ap()),
                          incs=[ml.inc(16)])
                    sel_rdy = ml.tgt()
                    P.add("tensor", lambda e: e.nop(), waits=[sel_rdy])

                    NBLK = 43
                    dq = {}
                    regfree = {}

                    def plan_bcast(r):
                        if r >= NBLK * 8:
                            return
                        blk, rem = divmod(r, 8)
                        src_, c = divmod(rem, 4)
                        n0 = 256 * blk
                        bb = 3072 + 512 * (r % 2)
                        P.add("tensor", lambda e, src_=src_, c=c, n0=n0, bb=bb: e.matmul(
                            ps[:, bb:bb + 256],
                            sb_selgu[32 * src_:32 * (src_ + 1), 128 * c:128 * (c + 1)],
                            guq_s[32 * src_:32 * (src_ + 1), n0:n0 + 256],
                            start=True, stop=True),
                            waits=[dq.get(("bfree", r % 2))])
                        P.add("tensor", lambda e, src_=src_, c=c, n0=n0, bb=bb: e.matmul(
                            ps[:, bb + 256:bb + 512],
                            sb_selgu[32 * src_:32 * (src_ + 1), 128 * c:128 * (c + 1)],
                            guq_z[32 * src_:32 * (src_ + 1), n0:n0 + 256],
                            start=True, stop=True), incs=[s_pe.inc()])
                        dq[("bc", r)] = s_pe.tgt()

                    def dequant_round(r):
                        blk, rem = divmod(r, 8)
                        src, c = divmod(rem, 4)
                        n0 = 256 * blk
                        par = blk % 2
                        rp = r % 2
                        bb = 3072 + 512 * (r % 2)
                        qsrc = qw_gate if src == 0 else qw_up
                        P.add("sync", lambda e, qsrc=qsrc, c=c, n0=n0, rp=rp: e.dma_start(
                            out=qst[rp][:, :],
                            in_=bass.AP(qsrc, 128 * c * INTER + n0,
                                        [[INTER, 128], [1, 256]])),
                            waits=[dq.get(("qst", rp))], incs=[LD[rp].inc(16)])
                        q_rdy = LD[rp].tgt()
                        bc_done = dq[("bc", r)]
                        for s_ in range(8):
                            kt = 4 * s_ + c
                            P.add("vector", lambda e, s_=s_, rp=rp: e.tensor_scalar(
                                out=w4i[rp][:, :], in0=qst[rp][:, :],
                                scalar1=4 * s_, scalar2=15,
                                op0=ALU.logical_shift_right, op1=ALU.bitwise_and),
                                waits=[q_rdy, dq.get(("w4", rp))], incs=[s_dve.inc()])
                            u_done = s_dve.tgt()
                            P.add("vector", lambda e, src=src, par=par, kt=kt, rp=rp, bb=bb: e.tensor_mul(
                                wc[src][par][:, 256 * kt:256 * (kt + 1)],
                                w4i[rp][:, :], ps[:, bb:bb + 256]),
                                waits=[u_done, bc_done, dq.get(("wcuse", src, par))],
                                incs=[s_dve.inc()])
                            P.add("vector", lambda e, src=src, par=par, kt=kt, bb=bb: e.tensor_sub(
                                wc[src][par][:, 256 * kt:256 * (kt + 1)],
                                wc[src][par][:, 256 * kt:256 * (kt + 1)],
                                ps[:, bb + 256:bb + 512]),
                                waits=[s_dve.tgt()], incs=[s_dve.inc()])
                            dq[("w4", rp)] = s_dve.tgt()
                            dq[("qst", rp)] = s_dve.tgt()
                        dq[("bfree", rp)] = s_dve.tgt()
                        dq[("dqdone", blk)] = s_dve.tgt()
                        plan_bcast(r + 2)

                    def mm_group(blk, j, gate_req):
                        nt, tc, src = j // 4, (j // 2) % 2, j % 2
                        par = blk % 2
                        n0 = 256 * blk
                        gidx = blk * 8 + j
                        reg = gidx % 6
                        rb = 512 * reg
                        hpar = (2 * blk + nt) % 2
                        w0 = [dq[("dqdone", blk)], regfree.get(reg)]
                        if j == 0:
                            w0.append(dq.get(("hfree", hpar)))
                        first_gate = None
                        for kt in range(32):
                            inc = []
                            if kt == 0 and gate_req:
                                inc = [s_pe.inc()]
                            elif kt == 31:
                                inc = [s_pe.inc()]
                            P.add("tensor", lambda e, src=src, par=par, kt=kt, nt=nt, tc=tc, rb=rb: e.matmul(
                                ps[:, rb:rb + 512],
                                wc[src][par][:, 256 * kt + 128 * nt:256 * kt + 128 * (nt + 1)],
                                sb_x1bf[:, NT * kt + 512 * tc:NT * kt + 512 * (tc + 1)],
                                start=(kt == 0), stop=(kt == 31)),
                                waits=w0 if kt == 0 else [],
                                incs=inc)
                            if kt == 0 and gate_req:
                                first_gate = s_pe.tgt()
                        dq[("mm", j)] = s_pe.tgt()
                        dq[("wcuse", 0, par)] = s_pe.tgt()
                        dq[("wcuse", 1, par)] = s_pe.tgt()
                        return first_gate

                    def plan_evac(blk, j, gate):
                        nt, tc = j // 4, (j // 2) % 2
                        gidx = blk * 8 + j
                        g_reg = 512 * ((gidx - 1) % 6)
                        u_reg = 512 * (gidx % 6)
                        n0 = 256 * blk
                        hpar = (2 * blk + nt) % 2
                        P.add("scalar", lambda e, g_reg=g_reg: e.activation(
                            out=silt[:, :], in_=ps[:, g_reg:g_reg + 512],
                            func=AF.Silu),
                            waits=[gate, dq.get("sil_free")],
                            incs=[s_act.inc()])
                        sil_rdy = s_act.tgt()
                        hw = [sil_rdy]
                        if tc == 0:
                            hw.append(dq.get(("hfree", hpar)))
                        P.add("vector", lambda e, u_reg=u_reg, tc=tc, hpar=hpar: e.tensor_mul(
                            hst[hpar][:, 512 * tc:512 * (tc + 1)],
                            silt[:, :], ps[:, u_reg:u_reg + 512]),
                            waits=hw, incs=[s_dve.inc()])
                        dq["sil_free"] = s_dve.tgt()
                        regfree[(gidx - 1) % 6] = s_dve.tgt()
                        regfree[gidx % 6] = s_dve.tgt()
                        if tc == 1:
                            hdone = s_dve.tgt()
                            P.add("sync", lambda e, n0=n0, nt=nt, hpar=hpar: e.dma_start(
                                out=bass.AP(h_d, (n0 + 128 * nt) * NT,
                                            [[NT, 128], [1, NT]]),
                                in_=hst[hpar][:, :]),
                                waits=[hdone], incs=[HS[hpar].inc(16)])
                            dq[("hfree", hpar)] = HS[hpar].tgt()

                    plan_bcast(0)
                    plan_bcast(1)
                    for r in range(8):
                        dequant_round(r)
                    pending = None
                    for blk in range(NBLK):
                        for j in range(8):
                            r = (blk + 1) * 8 + (j if blk + 1 < NBLK else 0)
                            if blk + 1 < NBLK:
                                dequant_round((blk + 1) * 8 + j)
                            gate = mm_group(blk, j, pending is not None)
                            if pending is not None:
                                plan_evac(*pending, gate)
                                pending = None
                            if j % 2 == 1:
                                pending = (blk, j)
                    P.add("tensor", lambda e: e.nop(), incs=[s_pe.inc()], drain=True)
                    plan_evac(*pending, s_pe.tgt())
                    pending = None
                    P.add("vector", lambda e: e.memset(hst[0][:, :], 0.0),
                          waits=[dq.get(("hfree", 0))], incs=[s_dve.inc()])
                    zt = s_dve.tgt()
                    for pr in range(2):
                        P.add("sync", lambda e, pr=pr: e.dma_start(
                            out=bass.AP(h_d, (INTER + 128 * pr) * NT,
                                        [[NT, 128], [1, NT]]),
                            in_=hst[0][:, :]),
                            waits=[zt], incs=[HS[0].inc(16)])
                    P.add("sync", lambda e: e.nop(),
                          waits=[HS[0].tgt(), HS[1].tgt()])
                    emit(nc, P)

            with ExitStack() as st:
                E = st.enter_context
                sb_seldn = E(nc.sbuf_tensor("sb_seldn", [88, 11 * 128], BF16))
                scd_f = E(nc.sbuf_tensor("scd_f", [88, D], F32))
                zd_i = E(nc.sbuf_tensor("zd_i", [88, 4096 + 512], I32))
                scd_bf = E(nc.sbuf_tensor("scd_bf", [88, D], BF16))
                zsd_bf = E(nc.sbuf_tensor("zsd_bf", [88, D], BF16))
                zd_f = E(nc.sbuf_tensor("zd_f", [88, D], F32))
                qst = [E(nc.sbuf_tensor(f"qstD{i}", [128, 512], I32)) for i in range(2)]
                w4i = [E(nc.sbuf_tensor(f"w4iD{i}", [128, 512], I32)) for i in range(2)]
                wds = [E(nc.sbuf_tensor(f"wdsD{i}", [128, 512], BF16)) for i in range(4)]
                ps = E(nc.psum_tensor("psP6", [128, 4096], F32))
                P = Plan()
                P.add("sync", lambda e: e.dma_start(out=sb_seldn[:, :], in_=sel_dn.ap()),
                      incs=[ml.inc(16)])
                P.add("sync", lambda e: e.dma_start(out=scd_f[:, :], in_=sc_down.ap()),
                      incs=[ml.inc(16)])
                P.add("sync", lambda e: e.dma_start(out=zd_i[:, 4096:4096 + 512], in_=qz_down.ap()),
                      incs=[ml.inc(16)])
                P.add("vector", lambda e: e.nop(), waits=[ml.tgt()])
                for s_ in range(8):
                    P.add("vector", lambda e, s_=s_: e.tensor_scalar(
                        out=bass.AP(zd_i, s_, [[4096 + 512, 88], [8, 512]]),
                        in0=zd_i[:, 4096:4096 + 512], scalar1=4 * s_, scalar2=15,
                        op0=ALU.logical_shift_right, op1=ALU.bitwise_and),
                        incs=[s_dve.inc()] if s_ == 7 else [])
                P.add("vector", lambda e: e.tensor_scalar_add(
                    zd_f[:, :],
                    bass.AP(zd_i, 0, [[4096 + 512, 88], [1, 8], [8, 512]]),
                    1.0), waits=[s_dve.tgt()], incs=[s_dve.inc()])
                P.add("vector", lambda e: e.tensor_mul(
                    zsd_bf[:, :], zd_f[:, :], scd_f[:, :]),
                    waits=[s_dve.tgt()], incs=[s_dve.inc()])
                P.add("vector", lambda e: e.tensor_copy(scd_bf[:, :], scd_f[:, :]),
                      incs=[s_dve.inc()])
                pre6 = s_dve.tgt()
                P.add("tensor", lambda e: e.nop(), waits=[pre6])
                P.add("gpsimd", lambda e: e.nop(), waits=[pre6])
                st6 = {}
                for cp in range(11):
                    for dch in range(8):
                        rp = (cp * 8 + dch) % 2
                        P.add("sync", lambda e, cp=cp, dch=dch, rp=rp: e.dma_start(
                            out=qst[rp][:, :],
                            in_=bass.AP(qw_down, 128 * cp * D + 512 * dch,
                                        [[D, 128], [1, 512]])),
                            waits=[st6.get(("qst", rp))], incs=[LD[rp].inc(16)])
                        q_rdy = LD[rp].tgt()
                        P.add("tensor", lambda e, cp=cp, dch=dch: e.matmul(
                            ps[:, 3072:3072 + 512],
                            sb_seldn[:, 128 * cp:128 * (cp + 1)],
                            scd_bf[:, 512 * dch:512 * (dch + 1)],
                            start=True, stop=True),
                            waits=[q_rdy, st6.get("aff_done")])
                        P.add("tensor", lambda e, cp=cp, dch=dch: e.matmul(
                            ps[:, 3584:3584 + 512],
                            sb_seldn[:, 128 * cp:128 * (cp + 1)],
                            zsd_bf[:, 512 * dch:512 * (dch + 1)],
                            start=True, stop=True), incs=[s_pe.inc()], drain=True)
                        bc_done = s_pe.tgt()
                        for s_ in range(8):
                            jt = s_ * 11 + cp
                            wp = (cp * 64 + dch * 8 + s_) % 4
                            P.add("vector", lambda e, s_=s_, rp=rp: e.tensor_scalar(
                                out=w4i[rp][:, :], in0=qst[rp][:, :],
                                scalar1=4 * s_, scalar2=15,
                                op0=ALU.logical_shift_right, op1=ALU.bitwise_and),
                                waits=[q_rdy, st6.get(("w4", rp))], incs=[s_dve.inc()])
                            u_done = s_dve.tgt()
                            P.add("vector", lambda e, rp=rp, wp=wp: e.tensor_mul(
                                wds[wp][:, :], w4i[rp][:, :], ps[:, 3072:3072 + 512]),
                                waits=[u_done, bc_done, st6.get(("wds", wp))],
                                incs=[s_dve.inc()])
                            P.add("vector", lambda e, wp=wp: e.tensor_sub(
                                wds[wp][:, :], wds[wp][:, :], ps[:, 3584:3584 + 512]),
                                waits=[s_dve.tgt()], incs=[s_dve.inc()])
                            st6[("w4", rp)] = s_dve.tgt()
                            st6[("qst", rp)] = s_dve.tgt()
                            aff = s_dve.tgt()
                            P.add("sync", lambda e, jt=jt, dch=dch, wp=wp: e.dma_start(
                                out=wd_d.ap()[jt][:, 512 * dch:512 * (dch + 1)],
                                in_=wds[wp][:, :]),
                                waits=[aff], incs=[ST[wp].inc(16)])
                            st6[("wds", wp)] = ST[wp].tgt()
                        st6["aff_done"] = s_dve.tgt()
                P.add("sync", lambda e: e.nop(),
                      waits=[ST[0].tgt(), ST[1].tgt(), ST[2].tgt(), ST[3].tgt()])
                emit(nc, P)

        guq_stack.close()

        with ExitStack() as st78:
            sb_r2 = st78.enter_context(nc.sbuf_tensor("sb_r2", [128, 32 * NT], F32))
            with ExitStack() as st:
                E = st.enter_context
                wdt = [E(nc.sbuf_tensor(f"wdt{i}", [128, 512], BF16)) for i in range(4)]
                htl = [E(nc.sbuf_tensor(f"htl{i}", [128, 512], BF16)) for i in range(4)]
                x1t = [E(nc.sbuf_tensor(f"x1t7{i}", [128, 512], F32)) for i in range(2)]
                ps = E(nc.psum_tensor("psP7", [128, 4096], F32))
                P = Plan()
                st7 = {}
                for th in range(2):
                    for ds in range(8):
                        par = (th * 8 + ds) % 2
                        base = 2048 * par
                        for jt in range(88):
                            bp = jt % 4
                            P.add("sync", lambda e, jt=jt, ds=ds, bp=bp: e.dma_start(
                                out=wdt[bp][:, :],
                                in_=wd_d.ap()[jt][:, 512 * ds:512 * (ds + 1)]),
                                waits=[st7.get(("wdt", bp))], incs=[W[bp].inc(16)])
                            s_ = jt // 11
                            cp = jt % 11
                            P.add("sync", lambda e, s_=s_, cp=cp, th=th, bp=bp: e.dma_start(
                                out=htl[bp][:, :],
                                in_=bass.AP(h_d, (1024 * cp + s_) * NT + 512 * th,
                                            [[8 * NT, 128], [1, 512]])),
                                waits=[st7.get(("wdt", bp))], incs=[LD[bp].inc(16)])
                            t_rdy = (W[bp].tgt(), LD[bp].tgt())
                            mw = [t_rdy[0], t_rdy[1]]
                            if jt == 0:
                                mw.append(st7.get(("evd", par)))
                            for dd in range(4):
                                P.add("tensor", lambda e, dd=dd, bp=bp, base=base, jt=jt: e.matmul(
                                    ps[:, base + 512 * dd:base + 512 * (dd + 1)],
                                    wdt[bp][:, 128 * dd:128 * (dd + 1)],
                                    htl[bp][:, :],
                                    start=(jt == 0), stop=(jt == 87)),
                                    waits=mw if dd == 0 else [],
                                    incs=[s_pe.inc()] if dd == 3 else [],
                                    drain=(dd == 3 and jt == 87))
                            st7[("wdt", bp)] = s_pe.tgt()
                        mm_done = s_pe.tgt()
                        for dd in range(4):
                            dt = 4 * ds + dd
                            xp = dd % 2
                            P.add("sync", lambda e, dt=dt, th=th, xp=xp: e.dma_start(
                                out=x1t[xp][:, :],
                                in_=x1_d.ap()[dt][:, 512 * th:512 * (th + 1)]),
                                waits=[st7.get(("x1t", xp))], incs=[X[xp].inc(16)])
                            x_rdy = X[xp].tgt()
                            P.add("vector", lambda e, dt=dt, th=th, dd=dd, xp=xp, base=base: e.tensor_add(
                                sb_r2[:, NT * dt + 512 * th:NT * dt + 512 * (th + 1)],
                                x1t[xp][:, :],
                                ps[:, base + 512 * dd:base + 512 * (dd + 1)]),
                                waits=[mm_done, x_rdy], incs=[s_dve.inc()])
                            st7[("x1t", xp)] = s_dve.tgt()
                        st7[("evd", par)] = s_dve.tgt()
                P.add("sync", lambda e: e.nop(),
                      waits=[W[0].tgt(), W[1].tgt(), W[2].tgt(), W[3].tgt(),
                             LD[0].tgt(), LD[1].tgt(), LD[2].tgt(), LD[3].tgt(),
                             X[0].tgt(), X[1].tgt()])
                emit(nc, P)

            if TAPS:
                P = Plan()
                P.add("sync", lambda e: e.dma_start(out=r2_tap.ap(), in_=sb_r2[:, :]),
                      incs=[ml.inc(16)])
                P.add("sync", lambda e: e.nop(), waits=[ml.tgt()])
                emit(nc, P)

            with ExitStack() as st:
                E = st.enter_context
                r2b = E(nc.sbuf_tensor("r2b8", [128, NT], BF16))
                r2sq = E(nc.sbuf_tensor("r2sq8", [128, NT], BF16))
                rows8 = E(nc.sbuf_tensor("rows8", [1, 6 * NT], F32))
                outst = [E(nc.sbuf_tensor(f"outst{i}", [128, NT], F32)) for i in range(2)]
                ab8_sb = E(nc.sbuf_tensor("ab8_sb", [128, 2 * NT], F32))
                ps = E(nc.psum_tensor("psP8", [128, 4096], F32))
                P = Plan()
                cast_free = None
                sq_free = None
                for dt in range(32):
                    P.add("vector", lambda e, dt=dt: e.tensor_copy(
                        r2b[:, :], sb_r2[:, NT * dt:NT * (dt + 1)]),
                        waits=[cast_free], incs=[s_dve.inc()])
                    b_rdy = s_dve.tgt()
                    P.add("scalar", lambda e, dt=dt: e.activation(
                        out=r2sq[:, :], in_=sb_r2[:, NT * dt:NT * (dt + 1)],
                        func=AF.Square), waits=[sq_free], incs=[s_act.inc()])
                    s_rdy = s_act.tgt()
                    for c in range(2):
                        P.add("tensor", lambda e, c=c, dt=dt: e.matmul(
                            ps[0:1, 2048 + 512 * c:2048 + 512 * (c + 1)], sb_1bf[:, :],
                            r2b[:, 512 * c:512 * (c + 1)],
                            start=(dt == 0), stop=(dt == 31)),
                            waits=[b_rdy] if c == 0 else [])
                        P.add("tensor", lambda e, c=c, dt=dt: e.matmul(
                            ps[0:1, 3072 + 512 * c:3072 + 512 * (c + 1)], sb_1bf[:, :],
                            r2sq[:, 512 * c:512 * (c + 1)],
                            start=(dt == 0), stop=(dt == 31)),
                            waits=[s_rdy] if c == 0 else [],
                            incs=[s_pe.inc()] if c == 1 else [],
                            drain=(c == 1 and dt == 31))
                    cast_free = s_pe.tgt()
                    sq_free = s_pe.tgt()
                stats_done = s_pe.tgt()
                ab8 = ab_chain(P, rows8, NT, stats_done,
                               ps[0:1, 2048:2048 + NT], ps[0:1, 3072:3072 + NT])
                b8done = ab_bounce(P, rows8, NT, ab8, ab8_sb)
                ap_done = [None, None]
                of_free = [None, None]
                for dt in range(32):
                    par = dt % 2
                    P.add("vector", lambda e, dt=dt, par=par: e.tensor_mul(
                        outst[par][:, :], sb_r2[:, NT * dt:NT * (dt + 1)],
                        ab8_sb[:, 0:NT]),
                        waits=[b8done, ap_done[par], of_free[par]], incs=[s_dve.inc()])
                    P.add("vector", lambda e, par=par: e.tensor_add(
                        outst[par][:, :], outst[par][:, :], ab8_sb[:, NT:2 * NT]),
                        waits=[s_dve.tgt()], incs=[s_dve.inc()])
                    ap_done[par] = s_dve.tgt()
                    P.add("sync", lambda e, dt=dt, par=par: e.dma_start(
                        out=out_t.ap()[128 * dt:128 * (dt + 1), :], in_=outst[par][:, :]),
                        waits=[ap_done[par]], incs=[ST[par].inc(16)])
                    of_free[par] = ST[par].tgt()
                P.add("sync", lambda e: e.nop(),
                      waits=[ST[0].tgt(), ST[1].tgt()])
                emit(nc, P)

    return nc



def _staircase_sel(n_rows, n_c):
    sel = np.zeros((n_rows, n_c * 128), dtype=BF)
    for c in range(n_c):
        for p in range(128):
            g = 8 * c + p // 16
            if g < n_rows:
                sel[g, 128 * c + p] = 1.0
    return sel


def _rel_tiles(m):
    rel = np.full((N_REL, 128, 512), NEG_BIG, dtype=np.float32)
    strips = STRIPS[m]
    for i in range(8):
        q0 = 128 * strips[i]
        for c in range(SLOT_CHUNKS[i]):
            k0 = 512 * c
            qq = np.arange(128)[:, None]
            kk = np.arange(512)[None, :]
            kabs = k0 + kk
            qabs = q0 + qq
            r = (kabs - qabs).astype(np.float32)
            r[kabs > qabs] = NEG_BIG
            rel[REL_BASE[i] + c] = r
    return rel


def _prep_inputs(inputs):
    ins = {k: np.asarray(v) for k, v in inputs.items()}
    gi = np.arange(4096) // 128
    assert np.array_equal(ins["L0_ffn_gate_g_idx"], gi)
    assert np.array_equal(ins["L0_ffn_up_g_idx"], gi)
    assert np.array_equal(ins["L0_ffn_down_g_idx"], np.arange(INTER) // 128)

    x = np.asarray(ins["L0_attn_hidden_states"], dtype=np.float32)
    wqkvT = np.ascontiguousarray(ins["L0_attn_Wqkv"].T)[PERM_D]
    woutT = np.ascontiguousarray(ins["L0_attn_Wout"].T)[:, PERM_D]

    w1 = wqkvT.reshape(32, 128, 96, 128)
    wqkv_st = np.ascontiguousarray(w1.transpose(2, 1, 0, 3).reshape(96, 128, 32 * 128))
    w2 = woutT.reshape(32, 128, 32, 128)
    wout_st = np.ascontiguousarray(w2.transpose(2, 1, 0, 3).reshape(32, 128, 32 * 128))

    def col32(v):
        return np.ascontiguousarray(np.asarray(v, dtype=np.float32).reshape(32, 128).T)

    sc_guq = np.zeros((128, INTER), dtype=np.float32)
    sc_guq[0:32] = ins["L0_ffn_gate_scales"]
    sc_guq[32:64] = ins["L0_ffn_up_scales"]

    qwd = np.zeros((1408, D), dtype=np.int32)
    qwd[:1376] = ins["L0_ffn_down_qweight"]
    qwd = np.ascontiguousarray(qwd[:, PERM_D])
    scd = np.zeros((88, D), dtype=np.float32)
    scd[:86] = ins["L0_ffn_down_scales"]
    scd = np.ascontiguousarray(scd[:, PERM_D])
    qzd = np.zeros((88, 512), dtype=np.int32)
    qzd[:86] = ins["L0_ffn_down_qzeros"]

    shared = dict(
        wqkv_st=wqkv_st.astype(np.float32),
        wout_st=wout_st.astype(np.float32),
        qlnw=col32(ins["L0_attn_q_ln_w"]), qlnb=col32(ins["L0_attn_q_ln_b"]),
        klnw=col32(ins["L0_attn_k_ln_w"]), klnb=col32(ins["L0_attn_k_ln_b"]),
        qw_gate=np.ascontiguousarray(ins["L0_ffn_gate_qweight"], dtype=np.int32),
        qw_up=np.ascontiguousarray(ins["L0_ffn_up_qweight"], dtype=np.int32),
        qw_down=qwd, sc_guq=sc_guq, sc_down=scd,
        qz_gate=np.ascontiguousarray(ins["L0_ffn_gate_qzeros"], dtype=np.int32),
        qz_up=np.ascontiguousarray(ins["L0_ffn_up_qzeros"], dtype=np.int32),
        qz_down=qzd,
        sel_gu=np.tile(_staircase_sel(32, 4), (2, 1)),
        sel_dn=_staircase_sel(88, 11),
        ident_u8=np.eye(128, dtype=np.uint8),
        ident_bf=np.eye(128, dtype=BF),
        ones_bf=np.ones((128, 1), dtype=BF),
        ones1f=np.ones((1, 128), dtype=np.float32),
    )
    rel = {m: _rel_tiles(m) for m in (0, 1)}

    in_maps, metas = [], []
    for core in range(8):
        b, m = core // 2, core % 2
        strips = STRIPS[m]
        tok = np.concatenate([np.arange(128 * s_, 128 * s_ + 128) for s_ in strips])
        xb = x[b]
        im = dict(shared)
        im["x_own"] = np.ascontiguousarray(xb[tok].T[PERM_D])
        im["x_full"] = np.ascontiguousarray(xb.T[PERM_D])
        im["rel"] = rel[m]
        in_maps.append(im)
        metas.append((b, tok))
    return in_maps, metas


def kernel(**inputs):
    in_maps, metas = _prep_inputs(inputs)
    nc = build_nc()
    res = run_bass_kernel_spmd(nc, in_maps, core_ids=list(range(8)))
    out = np.zeros((B, S, D), dtype=np.float32)
    inv = np.empty(D, dtype=np.int64)
    inv[PERM_D] = np.arange(D)
    for core in range(8):
        b, tok = metas[core]
        o = np.asarray(res.results[core]["out"], dtype=np.float32)
        out[b, tok, :] = o[inv, :].T
    return out


# version 06
# speedup vs baseline: 1.0466x; 1.0466x over previous
import numpy as np
import ml_dtypes
from contextlib import ExitStack

import concourse.bass as bass
import concourse.mybir as mybir
from concourse.bass_utils import run_bass_kernel_spmd

F32 = mybir.dt.float32
BF16 = mybir.dt.bfloat16
I32 = mybir.dt.int32
U8 = mybir.dt.uint8
AF = mybir.ActivationFunctionType
ALU = mybir.AluOpType
AX = mybir.AxisListType
BF = ml_dtypes.bfloat16

B, S, D, H, DH = 4, 2048, 4096, 32, 128
INTER = 11008
INTER_PAD = 11264
NT = 1024
CLIP = 8.0
LN_EPS = 1e-5
QSCALE = float(DH) ** -0.5
NEG_BIG = -1.0e35

STRIPS = {0: [0, 3, 4, 7, 8, 11, 12, 15], 1: [1, 2, 5, 6, 9, 10, 13, 14]}
SLOT_EXT = [2, 4, 6, 8, 10, 12, 14, 16]
SLOT_CHUNKS = [(e * 128 + 511) // 512 for e in SLOT_EXT]
REL_BASE = np.cumsum([0] + SLOT_CHUNKS).tolist()
N_REL = REL_BASE[-1]
SLOPES = [2.0 ** (-(h + 1) * 0.25) for h in range(H)]
PERM_D = np.array([8 * r + s for s in range(8) for r in range(512)], dtype=np.int64)

TAPS = False
STOP_AFTER = ""
ENGINES = ("sync", "tensor", "vector", "scalar", "gpsimd")


class Sem:
    def __init__(self, h):
        self.h = h
        self.n = 0

    def inc(self, k=1):
        self.n += k
        return (self.h, k)

    def tgt(self):
        return (self.h, self.n)


class Plan:
    def __init__(self):
        self.ops = {e: [] for e in ENGINES}

    def add(self, eng, fn, waits=(), incs=(), drain=False):
        w = [x for x in waits if x is not None]
        self.ops[eng].append((w, fn, list(incs), drain))


def _run_ops(eng, ops, use_drain=False):
    for waits, fn, incs, drain in ops:
        best = {}
        for (h, v) in waits:
            key = id(h)
            if key not in best or best[key][1] < v:
                best[key] = (h, v)
        for (h, v) in best.values():
            eng.wait_ge(h, v)
        inst = fn(eng)
        if incs and (use_drain or drain) and "DMA" not in type(inst.ins).__name__:
            inst = eng.drain()
        for (h, k) in incs:
            inst.then_inc(h, k)


def emit(nc, plan):
    with nc.Block() as block:
        for e in ENGINES:
            ops = plan.ops[e]
            if ops:
                getattr(block, e)(
                    lambda eng, ops=ops: _run_ops(eng, ops, False))


def build_nc():
    nc = bass.Bass("TRN2", target_bir_lowering=False)

    def din(name, shape, dtype):
        return nc.dram_tensor(name, shape, dtype, kind="ExternalInput")

    def dscr(name, shape, dtype):
        return nc.dram_tensor(name, shape, dtype,
                              kind="ExternalOutput" if TAPS else "Internal")

    x_own = din("x_own", [D, NT], F32)
    x_full = din("x_full", [D, S], F32)
    wqkv_st = din("wqkv_st", [96, 128, 32 * 128], F32)
    wout_st = din("wout_st", [32, 128, 32 * 128], F32)
    qlnw = din("qlnw", [128, 32], F32)
    qlnb = din("qlnb", [128, 32], F32)
    klnw = din("klnw", [128, 32], F32)
    klnb = din("klnb", [128, 32], F32)
    qw_gate = din("qw_gate", [512, INTER], I32)
    qw_up = din("qw_up", [512, INTER], I32)
    qw_down = din("qw_down", [1408, D], I32)
    sc_guq = din("sc_guq", [128, INTER], F32)
    qz_gate = din("qz_gate", [32, 1376], I32)
    qz_up = din("qz_up", [32, 1376], I32)
    sc_down = din("sc_down", [88, D], F32)
    qz_down = din("qz_down", [88, 512], I32)
    rel_in = din("rel", [N_REL, 128, 512], F32)
    sel_gu = din("sel_gu", [64, 4 * 128], BF16)
    sel_dn = din("sel_dn", [88, 11 * 128], BF16)
    ident_u8 = din("ident_u8", [128, 128], U8)
    ident_bf = din("ident_bf", [128, 128], BF16)
    ones_bf = din("ones_bf", [128, 1], BF16)
    ones1f = din("ones1f", [1, 128], F32)

    out_t = nc.dram_tensor("out", [D, NT], F32, kind="ExternalOutput")

    q_pre = dscr("q_pre", [32, 128, NT], BF16)
    k_pre = dscr("k_pre", [32, 128, S], BF16)
    q_d = dscr("q_d", [32, 128, NT], BF16)
    kT_d = dscr("kT_d", [32, 128, S], BF16)
    v_d = dscr("v_d", [32, 16, 128, 128], BF16)
    r1_d = dscr("r1_d", [32, 128, NT], F32)
    x1_d = dscr("x1_d", [32, 128, NT], F32)
    h_d = dscr("h_d", [INTER_PAD, NT], BF16)
    wd_d = dscr("wd_d", [88, 128, D], BF16)
    ab_dram = nc.dram_tensor("ab_dram", [1, 4096], F32, kind="Internal")
    oT_tap = dscr("oT_tap", [128, 32 * NT], BF16) if TAPS else None
    rows_tap = dscr("rows_tap", [1, 6 * S], F32) if TAPS else None
    s1_tap = dscr("s1_tap", [1, 2 * S], F32) if TAPS else None
    r2_tap = dscr("r2_tap", [128, 32 * NT], F32) if TAPS else None

    root = ExitStack()
    with root:
        R = root.enter_context
        sb_id8 = R(nc.sbuf_tensor("sb_id8", [128, 128], U8))
        sb_idbf = R(nc.sbuf_tensor("sb_idbf", [128, 128], BF16))
        sb_1bf = R(nc.sbuf_tensor("sb_1bf", [128, 1], BF16))
        sb_11f = R(nc.sbuf_tensor("sb_11f", [1, 128], F32))
        sb_qlnw = R(nc.sbuf_tensor("sb_qlnw", [128, 32], F32))
        sb_qlnb = R(nc.sbuf_tensor("sb_qlnb", [128, 32], F32))
        sb_klnw = R(nc.sbuf_tensor("sb_klnw", [128, 32], F32))
        sb_klnb = R(nc.sbuf_tensor("sb_klnb", [128, 32], F32))
        guq_stack = ExitStack()
        guq_s = guq_stack.enter_context(nc.sbuf_tensor("guq_s", [64, INTER], BF16, side="right"))
        guq_z = guq_stack.enter_context(nc.sbuf_tensor("guq_z", [64, INTER], BF16, side="right"))
        sb_eps = R(nc.sbuf_tensor("sb_eps", [128, 1], F32))

        sems = {nm: Sem(R(nc.semaphore(nm))) for nm in
                ("s_pe", "s_dve", "s_act", "s_pool",
                 "ld0", "ld1", "ld2", "ld3", "w0", "w1", "w2", "w3",
                 "s0", "s1", "s2", "s3", "x0", "x1", "h0", "h1",
                 "xl", "ml", "bn", "v0")}
        s_pe, s_dve, s_act, s_pool = (sems[k] for k in
                                      ("s_pe", "s_dve", "s_act", "s_pool"))
        LD = [sems["ld0"], sems["ld1"], sems["ld2"], sems["ld3"]]
        W = [sems["w0"], sems["w1"], sems["w2"], sems["w3"]]
        ST = [sems["s0"], sems["s1"], sems["s2"], sems["s3"]]
        X = [sems["x0"], sems["x1"]]
        HS = [sems["h0"], sems["h1"]]
        xl, ml, bn, v0 = sems["xl"], sems["ml"], sems["bn"], sems["v0"]

        def ab_chain(P, rows, n, stats_wait, s1_src, s2_src):
            P.add("vector", lambda e: e.tensor_copy(rows[0:1, 0:n], s1_src),
                  waits=[stats_wait])
            P.add("vector", lambda e: e.tensor_copy(rows[0:1, n:2 * n], s2_src),
                  incs=[s_dve.inc()], drain=True)
            P.add("vector", lambda e: e.tensor_scalar_mul(
                rows[0:1, 2 * n:3 * n], rows[0:1, 0:n], 1.0 / D),
                waits=[s_dve.tgt()], incs=[s_dve.inc()], drain=True)
            P.add("scalar", lambda e: e.activation(
                out=rows[0:1, 3 * n:4 * n], in_=rows[0:1, 2 * n:3 * n],
                func=AF.Square), waits=[s_dve.tgt()], incs=[s_act.inc()])
            P.add("vector", lambda e: e.scalar_tensor_tensor(
                out=rows[0:1, 3 * n:4 * n], in0=rows[0:1, n:2 * n],
                scalar=1.0 / D, in1=rows[0:1, 3 * n:4 * n],
                op0=ALU.mult, op1=ALU.subtract),
                waits=[s_act.tgt()], incs=[s_dve.inc()], drain=True)
            P.add("scalar", lambda e: e.activation(
                out=rows[0:1, 4 * n:5 * n], in_=rows[0:1, 3 * n:4 * n],
                func=AF.Sqrt, bias=sb_eps[0:1, 0:1]),
                waits=[s_dve.tgt()], incs=[s_act.inc()])
            P.add("vector", lambda e: e.reciprocal(
                rows[0:1, 4 * n:5 * n], rows[0:1, 4 * n:5 * n]),
                waits=[s_act.tgt()], incs=[s_dve.inc()], drain=True)
            P.add("vector", lambda e: e.scalar_tensor_tensor(
                out=rows[0:1, 5 * n:6 * n], in0=rows[0:1, 2 * n:3 * n],
                scalar=-1.0, in1=rows[0:1, 4 * n:5 * n],
                op0=ALU.mult, op1=ALU.mult),
                waits=[s_dve.tgt()], incs=[s_dve.inc()], drain=True)
            return s_dve.tgt()

        def ab_bounce(P, rows, n, ab_done, ab_sb):
            P.add("sync", lambda e: e.dma_start(
                out=ab_dram.ap()[0:1, 0:2 * n], in_=rows[0:1, 4 * n:6 * n]),
                waits=[ab_done], incs=[bn.inc(16)])
            P.add("sync", lambda e: e.dma_start(
                out=ab_sb[:, 0:2 * n],
                in_=bass.AP(ab_dram, 0, [[0, 128], [1, 2 * n]])),
                waits=[bn.tgt()], incs=[bn.inc(16)])
            return bn.tgt()

        def ab_bcast(P, ps, rows, n, ab_done, a_off, b_off):
            nch = n // 512
            for c in range(nch):
                P.add("tensor", lambda e, c=c: e.matmul(
                    ps[:, a_off + 512 * c:a_off + 512 * (c + 1)], sb_11f[:, :],
                    rows[0:1, 4 * n + 512 * c:4 * n + 512 * (c + 1)],
                    start=True, stop=True),
                    waits=[ab_done] if c == 0 else [])
                P.add("tensor", lambda e, c=c: e.matmul(
                    ps[:, b_off + 512 * c:b_off + 512 * (c + 1)], sb_11f[:, :],
                    rows[0:1, 5 * n + 512 * c:5 * n + 512 * (c + 1)],
                    start=True, stop=True),
                    incs=[s_pe.inc()] if c == nch - 1 else [])
            return s_pe.tgt()

        with ExitStack() as st:
            E = st.enter_context
            guqf = E(nc.sbuf_tensor("guqf", [128, INTER], F32))
            zf_f = E(nc.sbuf_tensor("zf_f", [64, INTER], F32))
            zrows = E(nc.sbuf_tensor("zrows", [64, INTER + 1376], I32))
            P = Plan()
            for dst, src in ((sb_id8, ident_u8), (sb_idbf, ident_bf),
                             (sb_1bf, ones_bf), (sb_11f, ones1f),
                             (sb_qlnw, qlnw), (sb_qlnb, qlnb),
                             (sb_klnw, klnw), (sb_klnb, klnb)):
                P.add("sync", lambda e, d=dst, s=src: e.dma_start(out=d[:, :], in_=s.ap()),
                      incs=[ml.inc(16)])
            P.add("sync", lambda e: e.dma_start(out=guqf[:, :], in_=sc_guq.ap()),
                  incs=[ml.inc(16)])
            P.add("sync", lambda e: e.dma_start(
                out=zrows[0:32, INTER:INTER + 1376], in_=qz_gate.ap()),
                incs=[ml.inc(16)])
            P.add("sync", lambda e: e.dma_start(
                out=zrows[32:64, INTER:INTER + 1376], in_=qz_up.ap()),
                incs=[ml.inc(16)])
            P.add("vector", lambda e: e.memset(sb_eps[:, :], LN_EPS),
                  waits=[ml.tgt()])
            P.add("vector", lambda e: e.tensor_scalar_mul(
                sb_qlnw[:, :], sb_qlnw[:, :], QSCALE))
            P.add("vector", lambda e: e.tensor_scalar_mul(
                sb_qlnb[:, :], sb_qlnb[:, :], QSCALE))
            for s_ in range(8):
                P.add("vector", lambda e, s_=s_: e.tensor_scalar(
                    out=bass.AP(zrows, s_, [[INTER + 1376, 64], [8, 1376]]),
                    in0=zrows[0:64, INTER:INTER + 1376], scalar1=4 * s_, scalar2=15,
                    op0=ALU.logical_shift_right, op1=ALU.bitwise_and),
                    incs=[s_dve.inc()] if s_ == 7 else [])
            P.add("vector", lambda e: e.tensor_scalar_add(
                zf_f[0:64, 0:INTER], zrows[0:64, 0:INTER], 1.0),
                waits=[s_dve.tgt()], incs=[s_dve.inc()])
            P.add("vector", lambda e: e.tensor_mul(
                guq_z[0:64, :], zf_f[0:64, :], guqf[0:64, :]),
                waits=[s_dve.tgt()], incs=[s_dve.inc()])
            P.add("vector", lambda e: e.tensor_copy(guq_s[0:64, :], guqf[0:64, :]),
                  incs=[s_dve.inc()])
            emit(nc, P)

        with ExitStack() as st:
            E = st.enter_context
            x_own_bf = E(nc.sbuf_tensor("x_own_bf", [128, 32 * NT], BF16))
            wblk = [E(nc.sbuf_tensor(f"wblkA{i}", [128, 32 * 128], BF16)) for i in range(2)]
            stq = [E(nc.sbuf_tensor(f"stqA{i}", [128, NT], BF16)) for i in range(2)]
            ps = E(nc.psum_tensor("psA", [128, 2048], F32))
            P = Plan()
            for dt in range(32):
                P.add("gpsimd", lambda e, dt=dt: e.dma_start(
                    out=x_own_bf[:, NT * dt:NT * (dt + 1)],
                    in_=x_own.ap()[128 * dt:128 * (dt + 1), :]),
                    incs=[xl.inc(16)])
            x_rdy = xl.tgt()
            wfree = [None, None]
            evac_done = [None, None]
            stq_free = [None, None]
            for et in range(32):
                par = et % 2
                P.add("gpsimd", lambda e, et=et, par=par: e.dma_start(
                    out=wblk[par][:, :], in_=wqkv_st.ap()[et]),
                    waits=[wfree[par]], incs=[W[par].inc(16)])
                w_rdy = W[par].tgt()
                for dt in range(32):
                    for c in range(2):
                        P.add("tensor", lambda e, dt=dt, c=c, par=par: e.matmul(
                            ps[:, 1024 * par + 512 * c:1024 * par + 512 * (c + 1)],
                            wblk[par][:, 128 * dt:128 * (dt + 1)],
                            x_own_bf[:, NT * dt + 512 * c:NT * dt + 512 * (c + 1)],
                            start=(dt == 0), stop=(dt == 31)),
                            waits=[w_rdy, x_rdy, evac_done[par]] if (dt == 0 and c == 0) else [],
                            incs=[s_pe.inc()] if (dt == 31 and c == 1) else [],
                            drain=(dt == 31 and c == 1))
                mm_done = s_pe.tgt()
                wfree[par] = mm_done
                P.add("vector", lambda e, par=par: e.tensor_scalar(
                    out=stq[par][:, :], in0=ps[:, 1024 * par:1024 * (par + 1)],
                    scalar1=-CLIP, scalar2=CLIP, op0=ALU.max, op1=ALU.min),
                    waits=[mm_done, stq_free[par]], incs=[s_dve.inc()])
                evac_done[par] = s_dve.tgt()
                P.add("sync", lambda e, et=et, par=par: e.dma_start(
                    out=q_pre.ap()[et], in_=stq[par][:, :]),
                    waits=[evac_done[par]], incs=[ST[par].inc(16)])
                stq_free[par] = ST[par].tgt()
            P.add("sync", lambda e: e.nop(), waits=[ST[0].tgt(), ST[1].tgt()])
            P.add("gpsimd", lambda e: e.nop(),
                  waits=[xl.tgt(), W[0].tgt(), W[1].tgt()])
            emit(nc, P)

        with ExitStack() as st:
            E = st.enter_context
            x_full_bf = E(nc.sbuf_tensor("x_full_bf", [128, 32 * S], BF16))
            wblk = [E(nc.sbuf_tensor(f"wblkB{i}", [128, 32 * 128], BF16)) for i in range(2)]
            stkv = [E(nc.sbuf_tensor(f"stkvB{i}", [128, S], BF16)) for i in range(2)]
            stvt = E(nc.sbuf_tensor("stvtB", [128, S], BF16))
            ps = E(nc.psum_tensor("psB", [128, 2048], F32))
            vtr = E(nc.psum_tensor("psBv", [128, 1024], F32))
            P = Plan()
            for dt in range(32):
                P.add("gpsimd", lambda e, dt=dt: e.dma_start(
                    out=x_full_bf[:, S * dt:S * (dt + 1)],
                    in_=x_full.ap()[128 * dt:128 * (dt + 1), :]),
                    incs=[xl.inc(16)])
            x_rdy = xl.tgt()
            wfree = [None, None]
            evac_done = None
            st_free = [None, None]
            vt_free = None
            vt_dma = None
            for et in range(32, 96):
                par = et % 2
                is_k = et < 64
                P.add("gpsimd", lambda e, et=et, par=par: e.dma_start(
                    out=wblk[par][:, :], in_=wqkv_st.ap()[et]),
                    waits=[wfree[par]], incs=[W[par].inc(16)])
                w_rdy = W[par].tgt()
                for dt in range(32):
                    for c in range(4):
                        P.add("tensor", lambda e, dt=dt, c=c, par=par: e.matmul(
                            ps[:, 512 * c:512 * (c + 1)],
                            wblk[par][:, 128 * dt:128 * (dt + 1)],
                            x_full_bf[:, S * dt + 512 * c:S * dt + 512 * (c + 1)],
                            start=(dt == 0), stop=(dt == 31)),
                            waits=[w_rdy, x_rdy, evac_done] if (dt == 0 and c == 0) else [],
                            incs=[s_pe.inc()] if (dt == 31 and c == 3) else [],
                            drain=(dt == 31 and c == 3))
                mm_done = s_pe.tgt()
                wfree[par] = mm_done
                P.add("vector", lambda e, par=par: e.tensor_scalar(
                    out=stkv[par][:, :], in0=ps[:, :],
                    scalar1=-CLIP, scalar2=CLIP, op0=ALU.max, op1=ALU.min),
                    waits=[mm_done, st_free[par]], incs=[s_dve.inc()])
                evac_done = s_dve.tgt()
                if is_k:
                    P.add("sync", lambda e, et=et, par=par: e.dma_start(
                        out=k_pre.ap()[et - 32], in_=stkv[par][:, :]),
                        waits=[evac_done], incs=[ST[par].inc(16)])
                    st_free[par] = ST[par].tgt()
                else:
                    hh = et - 64
                    clip_done = evac_done
                    for half in range(2):
                        for j in range(8):
                            ts_ = 8 * half + j
                            P.add("tensor", lambda e, ts_=ts_, j=j, par=par: e.matmul(
                                vtr[:, 128 * j:128 * (j + 1)],
                                stkv[par][:, 128 * ts_:128 * (ts_ + 1)],
                                sb_idbf[:, :], start=True, stop=True),
                                waits=[clip_done, vt_free] if j == 0 else [],
                                incs=[s_pe.inc()] if j == 7 else [],
                                drain=(j == 7))
                        tr_done = s_pe.tgt()
                        P.add("vector", lambda e, half=half: e.tensor_copy(
                            stvt[:, 1024 * half:1024 * (half + 1)], vtr[:, :]),
                            waits=[tr_done, vt_dma], incs=[s_dve.inc()])
                        vt_free = s_dve.tgt()
                    st_free[par] = s_pe.tgt()
                    P.add("sync", lambda e, hh=hh: e.dma_start(
                        out=v_d.ap()[hh].transpose([1, 0, 2]),
                        in_=stvt[:, :].rearrange("p (a c) -> p a c", a=16, c=128)),
                        waits=[vt_free], incs=[v0.inc(16)])
                    vt_dma = v0.tgt()
            P.add("sync", lambda e: e.nop(),
                  waits=[ST[0].tgt(), ST[1].tgt(), v0.tgt()])
            P.add("gpsimd", lambda e: e.nop(),
                  waits=[xl.tgt(), W[0].tgt(), W[1].tgt()])
            emit(nc, P)

        def ln_qk(src_pre, dst_post, n_cols, w_col, b_col):
            nch = n_cols // 512
            with ExitStack() as st:
                E = st.enter_context
                qt = [E(nc.sbuf_tensor(f"p2t{i}_{n_cols}", [128, n_cols], BF16)) for i in range(2)]
                qsq = E(nc.sbuf_tensor(f"p2sq_{n_cols}", [128, n_cols], BF16))
                rows = E(nc.sbuf_tensor(f"p2rows_{n_cols}", [1, 6 * n_cols], F32))
                t1 = [E(nc.sbuf_tensor(f"p2a{i}_{n_cols}", [128, n_cols], F32)) for i in range(2)]
                ob = [E(nc.sbuf_tensor(f"p2o{i}_{n_cols}", [128, n_cols], BF16)) for i in range(2)]
                ab_sb = E(nc.sbuf_tensor(f"p2ab_{n_cols}", [128, 2 * n_cols], F32))
                ps = E(nc.psum_tensor(f"psP2_{n_cols}", [128, 4096], F32))
                P = Plan()
                ld_free = [None, None]
                sq_free = None
                for et in range(32):
                    par = et % 2
                    P.add("sync", lambda e, et=et, par=par: e.dma_start(
                        out=qt[par][:, :], in_=src_pre.ap()[et]),
                        waits=[ld_free[par]], incs=[LD[par].inc(16)])
                    t_rdy = LD[par].tgt()
                    for c in range(nch):
                        P.add("tensor", lambda e, c=c, par=par, et=et: e.matmul(
                            ps[0:1, 512 * c:512 * (c + 1)], sb_1bf[:, :],
                            qt[par][:, 512 * c:512 * (c + 1)],
                            start=(et == 0), stop=(et == 31)),
                            waits=[t_rdy] if c == 0 else [])
                    P.add("scalar", lambda e, par=par: e.activation(
                        out=qsq[:, :], in_=qt[par][:, :], func=AF.Square),
                        waits=[t_rdy, sq_free], incs=[s_act.inc()])
                    sq_rdy = s_act.tgt()
                    for c in range(nch):
                        P.add("tensor", lambda e, c=c, et=et: e.matmul(
                            ps[0:1, 2048 + 512 * c:2048 + 512 * (c + 1)], sb_1bf[:, :],
                            qsq[:, 512 * c:512 * (c + 1)],
                            start=(et == 0), stop=(et == 31)),
                            waits=[sq_rdy] if c == 0 else [],
                            incs=[s_pe.inc()] if c == nch - 1 else [],
                            drain=(c == nch - 1 and et == 31))
                    sq_free = s_pe.tgt()
                    ld_free[par] = s_pe.tgt()
                stats_done = s_pe.tgt()
                n = n_cols
                ab_done = ab_chain(P, rows, n, stats_done,
                                   ps[0:1, 0:n], ps[0:1, 2048:2048 + n])
                if TAPS and n_cols == S:
                    P.add("sync", lambda e: e.dma_start(
                        out=rows_tap.ap(), in_=rows[0:1, :]),
                        waits=[ab_done], incs=[ml.inc(16)])
                bdone = ab_bounce(P, rows, n, ab_done, ab_sb)
                ld_free = [[stats_done], [stats_done]]
                ap_done = [None, None]
                ob_free = [None, None]
                for et in range(32):
                    par = et % 2
                    P.add("sync", lambda e, et=et, par=par: e.dma_start(
                        out=qt[par][:, :], in_=src_pre.ap()[et]),
                        waits=ld_free[par], incs=[LD[par].inc(16)])
                    t_rdy = LD[par].tgt()
                    P.add("vector", lambda e, par=par: e.tensor_mul(
                        t1[par][:, :], qt[par][:, :], ab_sb[:, 0:n]),
                        waits=[t_rdy, bdone, ap_done[par]], incs=[s_dve.inc()])
                    P.add("vector", lambda e, par=par: e.tensor_add(
                        t1[par][:, :], t1[par][:, :], ab_sb[:, n:2 * n]),
                        waits=[s_dve.tgt()], incs=[s_dve.inc()])
                    tt_done = s_dve.tgt()
                    ld_free[par] = [tt_done]
                    P.add("scalar", lambda e, et=et, par=par: e.activation(
                        out=ob[par][:, :], in_=t1[par][:, :], func=AF.Identity,
                        bias=b_col[:, et:et + 1], scale=w_col[:, et:et + 1]),
                        waits=[tt_done, ob_free[par]], incs=[s_act.inc()])
                    ap_done[par] = s_act.tgt()
                    P.add("sync", lambda e, et=et, par=par: e.dma_start(
                        out=dst_post.ap()[et], in_=ob[par][:, :]),
                        waits=[ap_done[par]], incs=[ST[par].inc(16)])
                    ob_free[par] = ST[par].tgt()
                P.add("sync", lambda e: e.nop(),
                      waits=[ST[0].tgt(), ST[1].tgt(), ml.tgt()])
                emit(nc, P)

        ln_qk(q_pre, q_d, NT, sb_qlnw, sb_qlnb)
        ln_qk(k_pre, kT_d, S, sb_klnw, sb_klnb)
        if STOP_AFTER == "P2":
            return nc

        with ExitStack() as st_r:
            ab4_sb = st_r.enter_context(nc.sbuf_tensor("ab4_sb", [128, 2048], F32))

            with ExitStack() as stA:
                sb_oT = stA.enter_context(nc.sbuf_tensor("sb_oT", [128, 32 * NT], BF16))

                with ExitStack() as st:
                    E = st.enter_context
                    sb_rel = E(nc.sbuf_tensor("sb_rel", [128, N_REL * 512], F32))
                    kh = [E(nc.sbuf_tensor(f"kh{i}", [128, S], BF16)) for i in range(2)]
                    vh = [E(nc.sbuf_tensor(f"vh{i}", [128, S], BF16)) for i in range(2)]
                    qh = [E(nc.sbuf_tensor(f"qh{i}", [128, NT], BF16)) for i in range(2)]
                    Ssb = [E(nc.sbuf_tensor(f"Ssb{i}", [128, 2048], F32)) for i in range(2)]
                    Pbf = [E(nc.sbuf_tensor(f"Pbf{i}", [128, 2048], BF16)) for i in range(2)]
                    ptsb = [E(nc.sbuf_tensor(f"ptsb{i}", [128, 128], BF16)) for i in range(2)]
                    diag = [E(nc.sbuf_tensor(f"diag{i}", [128, 128], BF16)) for i in range(2)]
                    cols = [E(nc.sbuf_tensor(f"cols{i}", [128, 2], F32)) for i in range(2)]
                    ps = E(nc.psum_tensor("psP3", [128, 4096], F32))
                    P = Plan()
                    P.add("sync", lambda e: e.dma_start(
                        out=sb_rel[:, :].rearrange("p (n w) -> p n w", n=N_REL, w=512),
                        in_=rel_in.ap().transpose([1, 0, 2])),
                        incs=[ml.inc(16)])
                    rel_rdy = ml.tgt()

                    ss = {}
                    pv_last = {0: None, 1: None}
                    last_use = [None, None]

                    def plan_scores(h, i, hrdy):
                        base = 2048 * (i % 2)
                        Ei = SLOT_EXT[i]
                        nch = SLOT_CHUNKS[i]
                        hp = h % 2
                        w0 = [hrdy, rel_rdy, ss.get(("oev", i - 2)), ss.get(("sev", i - 2))]
                        for c in range(nch):
                            cw = min(512, Ei * 128 - 512 * c)
                            P.add("tensor", lambda e, i=i, c=c, cw=cw, base=base, hp=hp: e.matmul(
                                ps[:, base + 512 * c:base + 512 * c + cw],
                                qh[hp][:, 128 * i:128 * (i + 1)],
                                kh[hp][:, 512 * c:512 * c + cw],
                                start=True, stop=True),
                                waits=w0 if c == 0 else [],
                                incs=[s_pe.inc()] if c == nch - 1 else [],
                                drain=(c == nch - 1))
                        ss[("mm", i)] = s_pe.tgt()
                        for c in range(nch):
                            cw = min(512, Ei * 128 - 512 * c)
                            ri = REL_BASE[i] + c
                            P.add("vector", lambda e, i=i, c=c, cw=cw, base=base, ri=ri, h=h: e.scalar_tensor_tensor(
                                out=Ssb[i % 2][:, 512 * c:512 * c + cw],
                                in0=sb_rel[:, 512 * ri:512 * ri + cw],
                                scalar=SLOPES[h],
                                in1=ps[:, base + 512 * c:base + 512 * c + cw],
                                op0=ALU.mult, op1=ALU.add),
                                waits=[ss[("mm", i)], ss.get(("exp", i - 2))] if c == 0 else [],
                                incs=[s_dve.inc()] if c == nch - 1 else [])
                        ss[("sev", i)] = s_dve.tgt()
                        P.add("vector", lambda e, i=i, Ei=Ei: e.reduce_max(
                            cols[i % 2][:, 0:1], Ssb[i % 2][:, 0:Ei * 128],
                            axis=AX.X, negate=True),
                            waits=[ss.get(("diag", i - 2))],
                            incs=[s_dve.inc()])
                        mx = s_dve.tgt()
                        P.add("scalar", lambda e, i=i, Ei=Ei: e.activation(
                            out=Pbf[i % 2][:, 0:Ei * 128], in_=Ssb[i % 2][:, 0:Ei * 128],
                            func=AF.Exp, bias=cols[i % 2][:, 0:1],
                            accum_out=cols[i % 2][:, 1:2]),
                            waits=[mx, ss.get(("ptdone", i - 2))], incs=[s_act.inc()])
                        ss[("exp", i)] = s_act.tgt()
                        P.add("vector", lambda e, i=i: e.reciprocal(
                            cols[i % 2][:, 1:2], cols[i % 2][:, 1:2]),
                            waits=[ss[("exp", i)]], incs=[s_dve.inc()])
                        P.add("vector", lambda e, i=i: e.tensor_scalar_mul(
                            diag[i % 2][:, :], sb_id8[:, :], cols[i % 2][:, 1:2]),
                            waits=[s_dve.tgt(), ss.get(("ptdone", i - 2))],
                            incs=[s_dve.inc()])
                        ss[("diag", i)] = s_dve.tgt()

                    def plan_ptpv(h, i):
                        base = 2048 * (i % 2)
                        Ei = SLOT_EXT[i]
                        hp = h % 2
                        pt_off = [base, base + 512]
                        o_off = base + 1024
                        for kt in range(Ei):
                            pp = kt % 2
                            P.add("tensor", lambda e, i=i, kt=kt, pp=pp: e.matmul(
                                ps[:, pt_off[pp]:pt_off[pp] + 128],
                                Pbf[i % 2][:, 128 * kt:128 * (kt + 1)],
                                diag[i % 2][:, :], start=True, stop=True),
                                waits=([ss[("diag", i)]] if kt == 0 else []) +
                                      ([ss.get(("ptev", i, kt - 2))] if kt >= 2 else []),
                                incs=[s_pe.inc()], drain=True)
                            ptmm = s_pe.tgt()
                            P.add("vector", lambda e, pp=pp: e.tensor_copy(
                                ptsb[pp][:, :], ps[:, pt_off[pp]:pt_off[pp] + 128]),
                                waits=[ptmm, pv_last[pp]], incs=[s_dve.inc()])
                            ss[("ptev", i, kt)] = s_dve.tgt()
                            P.add("tensor", lambda e, i=i, kt=kt, pp=pp, hp=hp, Ei=Ei: e.matmul(
                                ps[:, o_off:o_off + 128],
                                vh[hp][:, 128 * kt:128 * (kt + 1)],
                                ptsb[pp][:, :],
                                start=(kt == 0), stop=(kt == Ei - 1)),
                                waits=[ss[("ptev", i, kt)]],
                                incs=[s_pe.inc()], drain=(kt == Ei - 1))
                            pv_last[pp] = s_pe.tgt()
                        ss[("ptdone", i)] = s_pe.tgt()
                        P.add("vector", lambda e, h=h, i=i, o_off=o_off: e.tensor_copy(
                            sb_oT[:, NT * h + 128 * i:NT * h + 128 * (i + 1)],
                            ps[:, o_off:o_off + 128]),
                            waits=[ss[("ptdone", i)]], incs=[s_dve.inc()])
                        ss[("oev", i)] = s_dve.tgt()

                    for h in range(H):
                        hp = h % 2
                        P.add("sync", lambda e, h=h, hp=hp: e.dma_start(
                            out=kh[hp][:, :], in_=kT_d.ap()[h]),
                            waits=[last_use[hp]], incs=[LD[hp].inc(16)])
                        P.add("sync", lambda e, h=h, hp=hp: e.dma_start(
                            out=vh[hp][:, :].rearrange("p (a c) -> p a c", a=16, c=128),
                            in_=v_d.ap()[h].transpose([1, 0, 2])),
                            incs=[LD[hp].inc(16)])
                        P.add("sync", lambda e, h=h, hp=hp: e.dma_start(
                            out=qh[hp][:, :], in_=q_d.ap()[h]),
                            incs=[LD[hp].inc(16)])
                        hrdy = LD[hp].tgt()
                        carry = {("oev", -2): ss.get(("oev", 6)),
                                 ("oev", -1): ss.get(("oev", 7)),
                                 ("sev", -2): ss.get(("sev", 6)),
                                 ("sev", -1): ss.get(("sev", 7)),
                                 ("exp", -2): ss.get(("exp", 6)),
                                 ("exp", -1): ss.get(("exp", 7)),
                                 ("diag", -2): ss.get(("diag", 6)),
                                 ("diag", -1): ss.get(("diag", 7)),
                                 ("ptdone", -2): ss.get(("ptdone", 6)),
                                 ("ptdone", -1): ss.get(("ptdone", 7))}
                        ss.clear()
                        ss.update(carry)
                        plan_scores(h, 0, hrdy)
                        plan_scores(h, 1, hrdy)
                        for i in range(8):
                            plan_ptpv(h, i)
                            if i + 2 < 8:
                                plan_scores(h, i + 2, hrdy)
                        last_use[hp] = s_pe.tgt()
                    P.add("sync", lambda e: e.nop(),
                          waits=[LD[0].tgt(), LD[1].tgt()])
                    emit(nc, P)

                if TAPS:
                    P = Plan()
                    P.add("sync", lambda e: e.dma_start(out=oT_tap.ap(), in_=sb_oT[:, :]),
                          incs=[ml.inc(16)])
                    P.add("sync", lambda e: e.nop(), waits=[ml.tgt()])
                    emit(nc, P)

                with ExitStack() as st:
                    E = st.enter_context
                    wblk = [E(nc.sbuf_tensor(f"wblkC{i}", [128, 32 * 128], BF16)) for i in range(2)]
                    xtile = [E(nc.sbuf_tensor(f"xtC{i}", [128, NT], F32)) for i in range(2)]
                    r1t = [E(nc.sbuf_tensor(f"r1C{i}", [128, NT], F32)) for i in range(2)]
                    r1b = E(nc.sbuf_tensor("r1bC", [128, NT], BF16))
                    rows4 = E(nc.sbuf_tensor("rows4", [1, 6 * NT], F32))
                    r1sq = E(nc.sbuf_tensor("r1sqC", [128, NT], BF16))
                    ps = E(nc.psum_tensor("psP4", [128, 4096], F32))
                    P = Plan()
                    wfree = [None, None]
                    xfree = [None, None]
                    r1free = [[], []]
                    mm_evd = [None, None]
                    sq_free = None
                    cast_free = None
                    for dt in range(32):
                        par = dt % 2
                        P.add("gpsimd", lambda e, dt=dt, par=par: e.dma_start(
                            out=wblk[par][:, :], in_=wout_st.ap()[dt]),
                            waits=[wfree[par]], incs=[W[par].inc(16)])
                        w_rdy = W[par].tgt()
                        P.add("sync", lambda e, dt=dt, par=par: e.dma_start(
                            out=xtile[par][:, :],
                            in_=x_own.ap()[128 * dt:128 * (dt + 1), :]),
                            waits=[xfree[par]], incs=[X[par].inc(16)])
                        x_rdy = X[par].tgt()
                        base = 1024 * par
                        for et in range(32):
                            for c in range(2):
                                P.add("tensor", lambda e, et=et, c=c, par=par, base=base: e.matmul(
                                    ps[:, base + 512 * c:base + 512 * (c + 1)],
                                    wblk[par][:, 128 * et:128 * (et + 1)],
                                    sb_oT[:, NT * et + 512 * c:NT * et + 512 * (c + 1)],
                                    start=(et == 0), stop=(et == 31)),
                                    waits=[w_rdy, mm_evd[par]] if (et == 0 and c == 0) else [],
                                    incs=[s_pe.inc()] if (et == 31 and c == 1) else [],
                                    drain=(et == 31 and c == 1))
                        mm_done = s_pe.tgt()
                        wfree[par] = mm_done
                        P.add("vector", lambda e, par=par, base=base: e.tensor_add(
                            r1t[par][:, :], xtile[par][:, :], ps[:, base:base + NT]),
                            waits=[mm_done, x_rdy] + r1free[par], incs=[s_dve.inc()])
                        r1_done = s_dve.tgt()
                        mm_evd[par] = r1_done
                        xfree[par] = r1_done
                        P.add("sync", lambda e, dt=dt, par=par: e.dma_start(
                            out=r1_d.ap()[dt], in_=r1t[par][:, :]),
                            waits=[r1_done], incs=[ST[par].inc(16)])
                        r1_dma = ST[par].tgt()
                        P.add("vector", lambda e, par=par: e.tensor_copy(
                            r1b[:, :], r1t[par][:, :]),
                            waits=[cast_free], incs=[s_dve.inc()])
                        b_rdy = s_dve.tgt()
                        P.add("scalar", lambda e, par=par: e.activation(
                            out=r1sq[:, :], in_=r1t[par][:, :], func=AF.Square),
                            waits=[r1_done, sq_free], incs=[s_act.inc()])
                        s_rdy = s_act.tgt()
                        for c in range(2):
                            P.add("tensor", lambda e, c=c, dt=dt: e.matmul(
                                ps[0:1, 2048 + 512 * c:2048 + 512 * (c + 1)], sb_1bf[:, :],
                                r1b[:, 512 * c:512 * (c + 1)],
                                start=(dt == 0), stop=(dt == 31)),
                                waits=[b_rdy] if c == 0 else [])
                            P.add("tensor", lambda e, c=c, dt=dt: e.matmul(
                                ps[0:1, 3072 + 512 * c:3072 + 512 * (c + 1)], sb_1bf[:, :],
                                r1sq[:, 512 * c:512 * (c + 1)],
                                start=(dt == 0), stop=(dt == 31)),
                                waits=[s_rdy] if c == 0 else [],
                                incs=[s_pe.inc()] if c == 1 else [],
                                drain=(c == 1 and dt == 31))
                        cast_free = s_pe.tgt()
                        sq_free = s_pe.tgt()
                        r1free[par] = [s_pe.tgt(), r1_dma]
                    stats_done = s_pe.tgt()
                    ab4 = ab_chain(P, rows4, NT, stats_done,
                                   ps[0:1, 2048:2048 + NT], ps[0:1, 3072:3072 + NT])
                    ab_bounce(P, rows4, NT, ab4, ab4_sb)
                    P.add("sync", lambda e: e.nop(),
                          waits=[ST[0].tgt(), ST[1].tgt(), bn.tgt()])
                    P.add("gpsimd", lambda e: e.nop(),
                          waits=[W[0].tgt(), W[1].tgt()])
                    emit(nc, P)

            if STOP_AFTER == "P4":
                return nc
            with ExitStack() as stX:
                sb_x1bf = stX.enter_context(nc.sbuf_tensor("sb_x1bf", [128, 32 * NT], BF16))

                with ExitStack() as st:
                    E = st.enter_context
                    r1t = [E(nc.sbuf_tensor(f"r1D{i}", [128, NT], F32)) for i in range(2)]
                    P = Plan()
                    bdone = None
                    ld_free = [[], []]
                    ap_done = [None, None]
                    for dt in range(32):
                        par = dt % 2
                        P.add("sync", lambda e, dt=dt, par=par: e.dma_start(
                            out=r1t[par][:, :], in_=r1_d.ap()[dt]),
                            waits=ld_free[par], incs=[LD[par].inc(16)])
                        t_rdy = LD[par].tgt()
                        P.add("vector", lambda e, par=par: e.tensor_mul(
                            r1t[par][:, :], r1t[par][:, :], ab4_sb[:, 0:NT]),
                            waits=[t_rdy, ap_done[par]], incs=[s_dve.inc()])
                        P.add("vector", lambda e, par=par: e.tensor_add(
                            r1t[par][:, :], r1t[par][:, :], ab4_sb[:, NT:2 * NT]),
                            waits=[s_dve.tgt()], incs=[s_dve.inc()])
                        x1_done = s_dve.tgt()
                        P.add("sync", lambda e, dt=dt, par=par: e.dma_start(
                            out=x1_d.ap()[dt], in_=r1t[par][:, :]),
                            waits=[x1_done], incs=[ST[par].inc(16)])
                        x1_dma = ST[par].tgt()
                        P.add("scalar", lambda e, dt=dt, par=par: e.activation(
                            out=sb_x1bf[:, NT * dt:NT * (dt + 1)], in_=r1t[par][:, :],
                            func=AF.Copy), waits=[x1_done], incs=[s_act.inc()])
                        ap_done[par] = s_act.tgt()
                        ld_free[par] = [x1_dma, ap_done[par]]
                    P.add("sync", lambda e: e.nop(),
                          waits=[ST[0].tgt(), ST[1].tgt()])
                    emit(nc, P)

                with ExitStack() as st:
                    E = st.enter_context
                    sb_selgu = E(nc.sbuf_tensor("sb_selgu", [64, 4 * 128], BF16))
                    qst = [E(nc.sbuf_tensor(f"qst{i}", [128, 256], I32)) for i in range(2)]
                    w4i = [E(nc.sbuf_tensor(f"w4i{i}", [128, 256], I32)) for i in range(2)]
                    wc = [[E(nc.sbuf_tensor(f"wc{s}{i}", [128, 32 * 256], BF16))
                           for i in range(2)] for s in range(2)]
                    silt = E(nc.sbuf_tensor("silt", [128, 512], F32))
                    hst = [E(nc.sbuf_tensor(f"hst{i}", [128, NT], BF16)) for i in range(2)]
                    ps = E(nc.psum_tensor("psP5", [128, 4096], F32))
                    P = Plan()
                    P.add("sync", lambda e: e.dma_start(out=sb_selgu[:, :], in_=sel_gu.ap()),
                          incs=[ml.inc(16)])
                    sel_rdy = ml.tgt()
                    P.add("tensor", lambda e: e.nop(), waits=[sel_rdy])

                    NBLK = 43
                    dq = {}
                    regfree = {}

                    def plan_bcast(r):
                        if r >= NBLK * 8:
                            return
                        blk, rem = divmod(r, 8)
                        src_, c = divmod(rem, 4)
                        n0 = 256 * blk
                        bb = 3072 + 512 * (r % 2)
                        P.add("tensor", lambda e, src_=src_, c=c, n0=n0, bb=bb: e.matmul(
                            ps[:, bb:bb + 256],
                            sb_selgu[32 * src_:32 * (src_ + 1), 128 * c:128 * (c + 1)],
                            guq_s[32 * src_:32 * (src_ + 1), n0:n0 + 256],
                            start=True, stop=True),
                            waits=[dq.get(("bfree", r % 2))])
                        P.add("tensor", lambda e, src_=src_, c=c, n0=n0, bb=bb: e.matmul(
                            ps[:, bb + 256:bb + 512],
                            sb_selgu[32 * src_:32 * (src_ + 1), 128 * c:128 * (c + 1)],
                            guq_z[32 * src_:32 * (src_ + 1), n0:n0 + 256],
                            start=True, stop=True), incs=[s_pe.inc()])
                        dq[("bc", r)] = s_pe.tgt()

                    def dequant_round(r):
                        blk, rem = divmod(r, 8)
                        src, c = divmod(rem, 4)
                        n0 = 256 * blk
                        par = blk % 2
                        rp = r % 2
                        bb = 3072 + 512 * (r % 2)
                        qsrc = qw_gate if src == 0 else qw_up
                        P.add("sync", lambda e, qsrc=qsrc, c=c, n0=n0, rp=rp: e.dma_start(
                            out=qst[rp][:, :],
                            in_=bass.AP(qsrc, 128 * c * INTER + n0,
                                        [[INTER, 128], [1, 256]])),
                            waits=[dq.get(("qst", rp))], incs=[LD[rp].inc(16)])
                        q_rdy = LD[rp].tgt()
                        bc_done = dq[("bc", r)]
                        for s_ in range(8):
                            kt = 4 * s_ + c
                            P.add("vector", lambda e, s_=s_, rp=rp: e.tensor_scalar(
                                out=w4i[rp][:, :], in0=qst[rp][:, :],
                                scalar1=4 * s_, scalar2=15,
                                op0=ALU.logical_shift_right, op1=ALU.bitwise_and),
                                waits=[q_rdy, dq.get(("w4", rp))], incs=[s_dve.inc()])
                            u_done = s_dve.tgt()
                            P.add("vector", lambda e, src=src, par=par, kt=kt, rp=rp, bb=bb: e.tensor_mul(
                                wc[src][par][:, 256 * kt:256 * (kt + 1)],
                                w4i[rp][:, :], ps[:, bb:bb + 256]),
                                waits=[u_done, bc_done, dq.get(("wcuse", src, par))],
                                incs=[s_dve.inc()])
                            P.add("vector", lambda e, src=src, par=par, kt=kt, bb=bb: e.tensor_sub(
                                wc[src][par][:, 256 * kt:256 * (kt + 1)],
                                wc[src][par][:, 256 * kt:256 * (kt + 1)],
                                ps[:, bb + 256:bb + 512]),
                                waits=[s_dve.tgt()], incs=[s_dve.inc()])
                            dq[("w4", rp)] = s_dve.tgt()
                            dq[("qst", rp)] = s_dve.tgt()
                        dq[("bfree", rp)] = s_dve.tgt()
                        dq[("dqdone", blk)] = s_dve.tgt()

                    def mm_group(blk, j, gate_req):
                        nt, tc, src = j // 4, (j // 2) % 2, j % 2
                        par = blk % 2
                        n0 = 256 * blk
                        gidx = blk * 8 + j
                        reg = gidx % 6
                        rb = 512 * reg
                        hpar = (2 * blk + nt) % 2
                        w0 = [dq[("dqdone", blk)], regfree.get(reg)]
                        if j == 0:
                            w0.append(dq.get(("hfree", hpar)))
                        first_gate = None
                        for kt in range(32):
                            inc = []
                            if kt == 0 and gate_req:
                                inc = [s_pe.inc()]
                            elif kt == 31:
                                inc = [s_pe.inc()]
                            P.add("tensor", lambda e, src=src, par=par, kt=kt, nt=nt, tc=tc, rb=rb: e.matmul(
                                ps[:, rb:rb + 512],
                                wc[src][par][:, 256 * kt + 128 * nt:256 * kt + 128 * (nt + 1)],
                                sb_x1bf[:, NT * kt + 512 * tc:NT * kt + 512 * (tc + 1)],
                                start=(kt == 0), stop=(kt == 31)),
                                waits=w0 if kt == 0 else [],
                                incs=inc)
                            if kt == 0 and gate_req:
                                first_gate = s_pe.tgt()
                        dq[("mm", j)] = s_pe.tgt()
                        dq[("wcuse", 0, par)] = s_pe.tgt()
                        dq[("wcuse", 1, par)] = s_pe.tgt()
                        return first_gate

                    def plan_evac(blk, j, gate):
                        nt, tc = j // 4, (j // 2) % 2
                        gidx = blk * 8 + j
                        g_reg = 512 * ((gidx - 1) % 6)
                        u_reg = 512 * (gidx % 6)
                        n0 = 256 * blk
                        hpar = (2 * blk + nt) % 2
                        P.add("scalar", lambda e, g_reg=g_reg: e.activation(
                            out=silt[:, :], in_=ps[:, g_reg:g_reg + 512],
                            func=AF.Silu),
                            waits=[gate, dq.get("sil_free")],
                            incs=[s_act.inc()])
                        sil_rdy = s_act.tgt()
                        hw = [sil_rdy]
                        if tc == 0:
                            hw.append(dq.get(("hfree", hpar)))
                        P.add("vector", lambda e, u_reg=u_reg, tc=tc, hpar=hpar: e.tensor_mul(
                            hst[hpar][:, 512 * tc:512 * (tc + 1)],
                            silt[:, :], ps[:, u_reg:u_reg + 512]),
                            waits=hw, incs=[s_dve.inc()])
                        dq["sil_free"] = s_dve.tgt()
                        regfree[(gidx - 1) % 6] = s_dve.tgt()
                        regfree[gidx % 6] = s_dve.tgt()
                        if tc == 1:
                            hdone = s_dve.tgt()
                            P.add("sync", lambda e, n0=n0, nt=nt, hpar=hpar: e.dma_start(
                                out=bass.AP(h_d, (n0 + 128 * nt) * NT,
                                            [[NT, 128], [1, NT]]),
                                in_=hst[hpar][:, :]),
                                waits=[hdone], incs=[HS[hpar].inc(16)])
                            dq[("hfree", hpar)] = HS[hpar].tgt()

                    plan_bcast(0)
                    plan_bcast(1)
                    for r in range(8):
                        dequant_round(r)
                        plan_bcast(r + 2)
                    pending = None
                    for blk in range(NBLK):
                        for j in range(8):
                            if blk + 1 < NBLK:
                                dequant_round((blk + 1) * 8 + j)
                            gate = mm_group(blk, j, pending is not None)
                            if blk + 1 < NBLK:
                                plan_bcast((blk + 1) * 8 + j + 2)
                            if pending is not None:
                                plan_evac(*pending, gate)
                                pending = None
                            if j % 2 == 1:
                                pending = (blk, j)
                    P.add("tensor", lambda e: e.nop(), incs=[s_pe.inc()], drain=True)
                    plan_evac(*pending, s_pe.tgt())
                    pending = None
                    P.add("vector", lambda e: e.memset(hst[0][:, :], 0.0),
                          waits=[dq.get(("hfree", 0))], incs=[s_dve.inc()])
                    zt = s_dve.tgt()
                    for pr in range(2):
                        P.add("sync", lambda e, pr=pr: e.dma_start(
                            out=bass.AP(h_d, (INTER + 128 * pr) * NT,
                                        [[NT, 128], [1, NT]]),
                            in_=hst[0][:, :]),
                            waits=[zt], incs=[HS[0].inc(16)])
                    P.add("sync", lambda e: e.nop(),
                          waits=[HS[0].tgt(), HS[1].tgt()])
                    emit(nc, P)

            with ExitStack() as st:
                E = st.enter_context
                sb_seldn = E(nc.sbuf_tensor("sb_seldn", [88, 11 * 128], BF16))
                scd_f = E(nc.sbuf_tensor("scd_f", [88, D], F32))
                zd_i = E(nc.sbuf_tensor("zd_i", [88, 4096 + 512], I32))
                scd_bf = E(nc.sbuf_tensor("scd_bf", [88, D], BF16))
                zsd_bf = E(nc.sbuf_tensor("zsd_bf", [88, D], BF16))
                zd_f = E(nc.sbuf_tensor("zd_f", [88, D], F32))
                qst = [E(nc.sbuf_tensor(f"qstD{i}", [128, 512], I32)) for i in range(2)]
                w4i = [E(nc.sbuf_tensor(f"w4iD{i}", [128, 512], I32)) for i in range(2)]
                wds = [E(nc.sbuf_tensor(f"wdsD{i}", [128, 512], BF16)) for i in range(4)]
                ps = E(nc.psum_tensor("psP6", [128, 4096], F32))
                P = Plan()
                P.add("sync", lambda e: e.dma_start(out=sb_seldn[:, :], in_=sel_dn.ap()),
                      incs=[ml.inc(16)])
                P.add("sync", lambda e: e.dma_start(out=scd_f[:, :], in_=sc_down.ap()),
                      incs=[ml.inc(16)])
                P.add("sync", lambda e: e.dma_start(out=zd_i[:, 4096:4096 + 512], in_=qz_down.ap()),
                      incs=[ml.inc(16)])
                P.add("vector", lambda e: e.nop(), waits=[ml.tgt()])
                for s_ in range(8):
                    P.add("vector", lambda e, s_=s_: e.tensor_scalar(
                        out=bass.AP(zd_i, s_, [[4096 + 512, 88], [8, 512]]),
                        in0=zd_i[:, 4096:4096 + 512], scalar1=4 * s_, scalar2=15,
                        op0=ALU.logical_shift_right, op1=ALU.bitwise_and),
                        incs=[s_dve.inc()] if s_ == 7 else [])
                P.add("vector", lambda e: e.tensor_scalar_add(
                    zd_f[:, :],
                    bass.AP(zd_i, 0, [[4096 + 512, 88], [1, 8], [8, 512]]),
                    1.0), waits=[s_dve.tgt()], incs=[s_dve.inc()])
                P.add("vector", lambda e: e.tensor_mul(
                    zsd_bf[:, :], zd_f[:, :], scd_f[:, :]),
                    waits=[s_dve.tgt()], incs=[s_dve.inc()])
                P.add("vector", lambda e: e.tensor_copy(scd_bf[:, :], scd_f[:, :]),
                      incs=[s_dve.inc()])
                pre6 = s_dve.tgt()
                P.add("tensor", lambda e: e.nop(), waits=[pre6])
                P.add("gpsimd", lambda e: e.nop(), waits=[pre6])
                st6 = {}
                for cp in range(11):
                    for dch in range(8):
                        rp = (cp * 8 + dch) % 2
                        P.add("sync", lambda e, cp=cp, dch=dch, rp=rp: e.dma_start(
                            out=qst[rp][:, :],
                            in_=bass.AP(qw_down, 128 * cp * D + 512 * dch,
                                        [[D, 128], [1, 512]])),
                            waits=[st6.get(("qst", rp))], incs=[LD[rp].inc(16)])
                        q_rdy = LD[rp].tgt()
                        P.add("tensor", lambda e, cp=cp, dch=dch: e.matmul(
                            ps[:, 3072:3072 + 512],
                            sb_seldn[:, 128 * cp:128 * (cp + 1)],
                            scd_bf[:, 512 * dch:512 * (dch + 1)],
                            start=True, stop=True),
                            waits=[q_rdy, st6.get("aff_done")])
                        P.add("tensor", lambda e, cp=cp, dch=dch: e.matmul(
                            ps[:, 3584:3584 + 512],
                            sb_seldn[:, 128 * cp:128 * (cp + 1)],
                            zsd_bf[:, 512 * dch:512 * (dch + 1)],
                            start=True, stop=True), incs=[s_pe.inc()], drain=True)
                        bc_done = s_pe.tgt()
                        for s_ in range(8):
                            jt = s_ * 11 + cp
                            wp = (cp * 64 + dch * 8 + s_) % 4
                            P.add("vector", lambda e, s_=s_, rp=rp: e.tensor_scalar(
                                out=w4i[rp][:, :], in0=qst[rp][:, :],
                                scalar1=4 * s_, scalar2=15,
                                op0=ALU.logical_shift_right, op1=ALU.bitwise_and),
                                waits=[q_rdy, st6.get(("w4", rp))], incs=[s_dve.inc()])
                            u_done = s_dve.tgt()
                            P.add("vector", lambda e, rp=rp, wp=wp: e.tensor_mul(
                                wds[wp][:, :], w4i[rp][:, :], ps[:, 3072:3072 + 512]),
                                waits=[u_done, bc_done, st6.get(("wds", wp))],
                                incs=[s_dve.inc()])
                            P.add("vector", lambda e, wp=wp: e.tensor_sub(
                                wds[wp][:, :], wds[wp][:, :], ps[:, 3584:3584 + 512]),
                                waits=[s_dve.tgt()], incs=[s_dve.inc()])
                            st6[("w4", rp)] = s_dve.tgt()
                            st6[("qst", rp)] = s_dve.tgt()
                            aff = s_dve.tgt()
                            P.add("sync", lambda e, jt=jt, dch=dch, wp=wp: e.dma_start(
                                out=wd_d.ap()[jt][:, 512 * dch:512 * (dch + 1)],
                                in_=wds[wp][:, :]),
                                waits=[aff], incs=[ST[wp].inc(16)])
                            st6[("wds", wp)] = ST[wp].tgt()
                        st6["aff_done"] = s_dve.tgt()
                P.add("sync", lambda e: e.nop(),
                      waits=[ST[0].tgt(), ST[1].tgt(), ST[2].tgt(), ST[3].tgt()])
                emit(nc, P)

        guq_stack.close()

        with ExitStack() as st78:
            sb_r2 = st78.enter_context(nc.sbuf_tensor("sb_r2", [128, 32 * NT], F32))
            with ExitStack() as st:
                E = st.enter_context
                wdt = [E(nc.sbuf_tensor(f"wdt{i}", [128, 512], BF16)) for i in range(4)]
                htl = [E(nc.sbuf_tensor(f"htl{i}", [128, 512], BF16)) for i in range(4)]
                x1t = [E(nc.sbuf_tensor(f"x1t7{i}", [128, 512], F32)) for i in range(2)]
                ps = E(nc.psum_tensor("psP7", [128, 4096], F32))
                P = Plan()
                st7 = {}
                for th in range(2):
                    for ds in range(8):
                        par = (th * 8 + ds) % 2
                        base = 2048 * par
                        for jt in range(88):
                            bp = jt % 4
                            P.add("sync", lambda e, jt=jt, ds=ds, bp=bp: e.dma_start(
                                out=wdt[bp][:, :],
                                in_=wd_d.ap()[jt][:, 512 * ds:512 * (ds + 1)]),
                                waits=[st7.get(("wdt", bp))], incs=[W[bp].inc(16)])
                            s_ = jt // 11
                            cp = jt % 11
                            P.add("sync", lambda e, s_=s_, cp=cp, th=th, bp=bp: e.dma_start(
                                out=htl[bp][:, :],
                                in_=bass.AP(h_d, (1024 * cp + s_) * NT + 512 * th,
                                            [[8 * NT, 128], [1, 512]])),
                                waits=[st7.get(("wdt", bp))], incs=[LD[bp].inc(16)])
                            t_rdy = (W[bp].tgt(), LD[bp].tgt())
                            mw = [t_rdy[0], t_rdy[1]]
                            if jt == 0:
                                mw.append(st7.get(("evd", par)))
                            for dd in range(4):
                                P.add("tensor", lambda e, dd=dd, bp=bp, base=base, jt=jt: e.matmul(
                                    ps[:, base + 512 * dd:base + 512 * (dd + 1)],
                                    wdt[bp][:, 128 * dd:128 * (dd + 1)],
                                    htl[bp][:, :],
                                    start=(jt == 0), stop=(jt == 87)),
                                    waits=mw if dd == 0 else [],
                                    incs=[s_pe.inc()] if dd == 3 else [],
                                    drain=(dd == 3 and jt == 87))
                            st7[("wdt", bp)] = s_pe.tgt()
                        mm_done = s_pe.tgt()
                        for dd in range(4):
                            dt = 4 * ds + dd
                            xp = dd % 2
                            P.add("sync", lambda e, dt=dt, th=th, xp=xp: e.dma_start(
                                out=x1t[xp][:, :],
                                in_=x1_d.ap()[dt][:, 512 * th:512 * (th + 1)]),
                                waits=[st7.get(("x1t", xp))], incs=[X[xp].inc(16)])
                            x_rdy = X[xp].tgt()
                            P.add("vector", lambda e, dt=dt, th=th, dd=dd, xp=xp, base=base: e.tensor_add(
                                sb_r2[:, NT * dt + 512 * th:NT * dt + 512 * (th + 1)],
                                x1t[xp][:, :],
                                ps[:, base + 512 * dd:base + 512 * (dd + 1)]),
                                waits=[mm_done, x_rdy], incs=[s_dve.inc()])
                            st7[("x1t", xp)] = s_dve.tgt()
                        st7[("evd", par)] = s_dve.tgt()
                P.add("sync", lambda e: e.nop(),
                      waits=[W[0].tgt(), W[1].tgt(), W[2].tgt(), W[3].tgt(),
                             LD[0].tgt(), LD[1].tgt(), LD[2].tgt(), LD[3].tgt(),
                             X[0].tgt(), X[1].tgt()])
                emit(nc, P)

            if TAPS:
                P = Plan()
                P.add("sync", lambda e: e.dma_start(out=r2_tap.ap(), in_=sb_r2[:, :]),
                      incs=[ml.inc(16)])
                P.add("sync", lambda e: e.nop(), waits=[ml.tgt()])
                emit(nc, P)

            with ExitStack() as st:
                E = st.enter_context
                r2b = E(nc.sbuf_tensor("r2b8", [128, NT], BF16))
                r2sq = E(nc.sbuf_tensor("r2sq8", [128, NT], BF16))
                rows8 = E(nc.sbuf_tensor("rows8", [1, 6 * NT], F32))
                outst = [E(nc.sbuf_tensor(f"outst{i}", [128, NT], F32)) for i in range(2)]
                ab8_sb = E(nc.sbuf_tensor("ab8_sb", [128, 2 * NT], F32))
                ps = E(nc.psum_tensor("psP8", [128, 4096], F32))
                P = Plan()
                cast_free = None
                sq_free = None
                for dt in range(32):
                    P.add("vector", lambda e, dt=dt: e.tensor_copy(
                        r2b[:, :], sb_r2[:, NT * dt:NT * (dt + 1)]),
                        waits=[cast_free], incs=[s_dve.inc()])
                    b_rdy = s_dve.tgt()
                    P.add("scalar", lambda e, dt=dt: e.activation(
                        out=r2sq[:, :], in_=sb_r2[:, NT * dt:NT * (dt + 1)],
                        func=AF.Square), waits=[sq_free], incs=[s_act.inc()])
                    s_rdy = s_act.tgt()
                    for c in range(2):
                        P.add("tensor", lambda e, c=c, dt=dt: e.matmul(
                            ps[0:1, 2048 + 512 * c:2048 + 512 * (c + 1)], sb_1bf[:, :],
                            r2b[:, 512 * c:512 * (c + 1)],
                            start=(dt == 0), stop=(dt == 31)),
                            waits=[b_rdy] if c == 0 else [])
                        P.add("tensor", lambda e, c=c, dt=dt: e.matmul(
                            ps[0:1, 3072 + 512 * c:3072 + 512 * (c + 1)], sb_1bf[:, :],
                            r2sq[:, 512 * c:512 * (c + 1)],
                            start=(dt == 0), stop=(dt == 31)),
                            waits=[s_rdy] if c == 0 else [],
                            incs=[s_pe.inc()] if c == 1 else [],
                            drain=(c == 1 and dt == 31))
                    cast_free = s_pe.tgt()
                    sq_free = s_pe.tgt()
                stats_done = s_pe.tgt()
                ab8 = ab_chain(P, rows8, NT, stats_done,
                               ps[0:1, 2048:2048 + NT], ps[0:1, 3072:3072 + NT])
                b8done = ab_bounce(P, rows8, NT, ab8, ab8_sb)
                ap_done = [None, None]
                of_free = [None, None]
                for dt in range(32):
                    par = dt % 2
                    P.add("vector", lambda e, dt=dt, par=par: e.tensor_mul(
                        outst[par][:, :], sb_r2[:, NT * dt:NT * (dt + 1)],
                        ab8_sb[:, 0:NT]),
                        waits=[b8done, ap_done[par], of_free[par]], incs=[s_dve.inc()])
                    P.add("vector", lambda e, par=par: e.tensor_add(
                        outst[par][:, :], outst[par][:, :], ab8_sb[:, NT:2 * NT]),
                        waits=[s_dve.tgt()], incs=[s_dve.inc()])
                    ap_done[par] = s_dve.tgt()
                    P.add("sync", lambda e, dt=dt, par=par: e.dma_start(
                        out=out_t.ap()[128 * dt:128 * (dt + 1), :], in_=outst[par][:, :]),
                        waits=[ap_done[par]], incs=[ST[par].inc(16)])
                    of_free[par] = ST[par].tgt()
                P.add("sync", lambda e: e.nop(),
                      waits=[ST[0].tgt(), ST[1].tgt()])
                emit(nc, P)

    return nc



def _staircase_sel(n_rows, n_c):
    sel = np.zeros((n_rows, n_c * 128), dtype=BF)
    for c in range(n_c):
        for p in range(128):
            g = 8 * c + p // 16
            if g < n_rows:
                sel[g, 128 * c + p] = 1.0
    return sel


def _rel_tiles(m):
    rel = np.full((N_REL, 128, 512), NEG_BIG, dtype=np.float32)
    strips = STRIPS[m]
    for i in range(8):
        q0 = 128 * strips[i]
        for c in range(SLOT_CHUNKS[i]):
            k0 = 512 * c
            qq = np.arange(128)[:, None]
            kk = np.arange(512)[None, :]
            kabs = k0 + kk
            qabs = q0 + qq
            r = (kabs - qabs).astype(np.float32)
            r[kabs > qabs] = NEG_BIG
            rel[REL_BASE[i] + c] = r
    return rel


def _prep_inputs(inputs):
    ins = {k: np.asarray(v) for k, v in inputs.items()}
    gi = np.arange(4096) // 128
    assert np.array_equal(ins["L0_ffn_gate_g_idx"], gi)
    assert np.array_equal(ins["L0_ffn_up_g_idx"], gi)
    assert np.array_equal(ins["L0_ffn_down_g_idx"], np.arange(INTER) // 128)

    x = np.asarray(ins["L0_attn_hidden_states"], dtype=np.float32)
    wqkvT = np.ascontiguousarray(ins["L0_attn_Wqkv"].T)[PERM_D]
    woutT = np.ascontiguousarray(ins["L0_attn_Wout"].T)[:, PERM_D]

    w1 = wqkvT.reshape(32, 128, 96, 128)
    wqkv_st = np.ascontiguousarray(w1.transpose(2, 1, 0, 3).reshape(96, 128, 32 * 128))
    w2 = woutT.reshape(32, 128, 32, 128)
    wout_st = np.ascontiguousarray(w2.transpose(2, 1, 0, 3).reshape(32, 128, 32 * 128))

    def col32(v):
        return np.ascontiguousarray(np.asarray(v, dtype=np.float32).reshape(32, 128).T)

    sc_guq = np.zeros((128, INTER), dtype=np.float32)
    sc_guq[0:32] = ins["L0_ffn_gate_scales"]
    sc_guq[32:64] = ins["L0_ffn_up_scales"]

    qwd = np.zeros((1408, D), dtype=np.int32)
    qwd[:1376] = ins["L0_ffn_down_qweight"]
    qwd = np.ascontiguousarray(qwd[:, PERM_D])
    scd = np.zeros((88, D), dtype=np.float32)
    scd[:86] = ins["L0_ffn_down_scales"]
    scd = np.ascontiguousarray(scd[:, PERM_D])
    qzd = np.zeros((88, 512), dtype=np.int32)
    qzd[:86] = ins["L0_ffn_down_qzeros"]

    shared = dict(
        wqkv_st=wqkv_st.astype(np.float32),
        wout_st=wout_st.astype(np.float32),
        qlnw=col32(ins["L0_attn_q_ln_w"]), qlnb=col32(ins["L0_attn_q_ln_b"]),
        klnw=col32(ins["L0_attn_k_ln_w"]), klnb=col32(ins["L0_attn_k_ln_b"]),
        qw_gate=np.ascontiguousarray(ins["L0_ffn_gate_qweight"], dtype=np.int32),
        qw_up=np.ascontiguousarray(ins["L0_ffn_up_qweight"], dtype=np.int32),
        qw_down=qwd, sc_guq=sc_guq, sc_down=scd,
        qz_gate=np.ascontiguousarray(ins["L0_ffn_gate_qzeros"], dtype=np.int32),
        qz_up=np.ascontiguousarray(ins["L0_ffn_up_qzeros"], dtype=np.int32),
        qz_down=qzd,
        sel_gu=np.tile(_staircase_sel(32, 4), (2, 1)),
        sel_dn=_staircase_sel(88, 11),
        ident_u8=np.eye(128, dtype=np.uint8),
        ident_bf=np.eye(128, dtype=BF),
        ones_bf=np.ones((128, 1), dtype=BF),
        ones1f=np.ones((1, 128), dtype=np.float32),
    )
    rel = {m: _rel_tiles(m) for m in (0, 1)}

    in_maps, metas = [], []
    for core in range(8):
        b, m = core // 2, core % 2
        strips = STRIPS[m]
        tok = np.concatenate([np.arange(128 * s_, 128 * s_ + 128) for s_ in strips])
        xb = x[b]
        im = dict(shared)
        im["x_own"] = np.ascontiguousarray(xb[tok].T[PERM_D])
        im["x_full"] = np.ascontiguousarray(xb.T[PERM_D])
        im["rel"] = rel[m]
        in_maps.append(im)
        metas.append((b, tok))
    return in_maps, metas


def kernel(**inputs):
    in_maps, metas = _prep_inputs(inputs)
    nc = build_nc()
    res = run_bass_kernel_spmd(nc, in_maps, core_ids=list(range(8)))
    out = np.zeros((B, S, D), dtype=np.float32)
    inv = np.empty(D, dtype=np.int64)
    inv[PERM_D] = np.arange(D)
    for core in range(8):
        b, tok = metas[core]
        o = np.asarray(res.results[core]["out"], dtype=np.float32)
        out[b, tok, :] = o[inv, :].T
    return out


# version 07
# speedup vs baseline: 1.2548x; 1.1990x over previous
import numpy as np
import ml_dtypes
from contextlib import ExitStack

import concourse.bass as bass
import concourse.mybir as mybir
from concourse.bass_utils import run_bass_kernel_spmd

F32 = mybir.dt.float32
BF16 = mybir.dt.bfloat16
I32 = mybir.dt.int32
U8 = mybir.dt.uint8
AF = mybir.ActivationFunctionType
ALU = mybir.AluOpType
AX = mybir.AxisListType
BF = ml_dtypes.bfloat16

B, S, D, H, DH = 4, 2048, 4096, 32, 128
INTER = 11008
INTER_PAD = 11264
NT = 1024
CLIP = 8.0
LN_EPS = 1e-5
QSCALE = float(DH) ** -0.5
NEG_BIG = -1.0e35

STRIPS = {0: [0, 3, 4, 7, 8, 11, 12, 15], 1: [1, 2, 5, 6, 9, 10, 13, 14]}
SLOT_EXT = [2, 4, 6, 8, 10, 12, 14, 16]
SLOT_CHUNKS = [(e * 128 + 511) // 512 for e in SLOT_EXT]
REL_BASE = np.cumsum([0] + SLOT_CHUNKS).tolist()
N_REL = REL_BASE[-1]
SLOPES = [2.0 ** (-(h + 1) * 0.25) for h in range(H)]
PERM_D = np.array([8 * r + s for s in range(8) for r in range(512)], dtype=np.int64)

TAPS = False
STOP_AFTER = ""
ENGINES = ("sync", "tensor", "vector", "scalar", "gpsimd")


class Sem:
    def __init__(self, h):
        self.h = h
        self.n = 0

    def inc(self, k=1):
        self.n += k
        return (self.h, k)

    def tgt(self):
        return (self.h, self.n)


class Plan:
    def __init__(self):
        self.ops = {e: [] for e in ENGINES}

    def add(self, eng, fn, waits=(), incs=(), drain=False):
        w = [x for x in waits if x is not None]
        self.ops[eng].append((w, fn, list(incs), drain))


def _run_ops(eng, ops, use_drain=False):
    for waits, fn, incs, drain in ops:
        best = {}
        for (h, v) in waits:
            key = id(h)
            if key not in best or best[key][1] < v:
                best[key] = (h, v)
        for (h, v) in best.values():
            eng.wait_ge(h, v)
        inst = fn(eng)
        if incs and (use_drain or drain) and "DMA" not in type(inst.ins).__name__:
            inst = eng.drain()
        for (h, k) in incs:
            inst.then_inc(h, k)


def emit(nc, plan):
    with nc.Block() as block:
        for e in ENGINES:
            ops = plan.ops[e]
            if ops:
                getattr(block, e)(
                    lambda eng, ops=ops: _run_ops(eng, ops, False))


def build_nc():
    nc = bass.Bass("TRN2", target_bir_lowering=False)

    def din(name, shape, dtype):
        return nc.dram_tensor(name, shape, dtype, kind="ExternalInput")

    def dscr(name, shape, dtype):
        return nc.dram_tensor(name, shape, dtype,
                              kind="ExternalOutput" if TAPS else "Internal")

    x_own = din("x_own", [D, NT], F32)
    x_full = din("x_full", [D, S], F32)
    wqkv_st = din("wqkv_st", [96, 128, 32 * 128], F32)
    wout_st = din("wout_st", [32, 128, 32 * 128], F32)
    qlnw = din("qlnw", [128, 32], F32)
    qlnb = din("qlnb", [128, 32], F32)
    klnw = din("klnw", [128, 32], F32)
    klnb = din("klnb", [128, 32], F32)
    qw_gate = din("qw_gate", [512, INTER], I32)
    qw_up = din("qw_up", [512, INTER], I32)
    qw_down = din("qw_down", [1408, D], I32)
    sc_guq = din("sc_guq", [128, INTER], F32)
    qz_gate = din("qz_gate", [32, 1376], I32)
    qz_up = din("qz_up", [32, 1376], I32)
    sc_down = din("sc_down", [88, D], F32)
    qz_down = din("qz_down", [88, 512], I32)
    rel_in = din("rel", [N_REL, 128, 512], F32)
    sel_gu = din("sel_gu", [64, 4 * 128], BF16)
    sel_dn = din("sel_dn", [88, 11 * 128], BF16)
    selT_gu = din("selT_gu", [128, 4 * 64], BF16)
    selT_dn = din("selT_dn", [128, 11 * 88], BF16)
    ident_u8 = din("ident_u8", [128, 128], U8)
    ident_bf = din("ident_bf", [128, 128], BF16)
    ones_bf = din("ones_bf", [128, 1], BF16)
    ones1f = din("ones1f", [1, 128], F32)

    out_t = nc.dram_tensor("out", [D, NT], F32, kind="ExternalOutput")

    q_pre = dscr("q_pre", [32, 128, NT], BF16)
    k_pre = dscr("k_pre", [32, 128, S], BF16)
    q_d = dscr("q_d", [32, 128, NT], BF16)
    kT_d = dscr("kT_d", [32, 128, S], BF16)
    v_d = dscr("v_d", [32, 16, 128, 128], BF16)
    r1_d = dscr("r1_d", [32, 128, NT], F32)
    x1_d = dscr("x1_d", [32, 128, NT], F32)
    h_d = dscr("h_d", [INTER_PAD, NT], BF16)
    wd_d = dscr("wd_d", [88, 128, D], BF16)
    ab_dram = nc.dram_tensor("ab_dram", [1, 4096], F32, kind="Internal")
    oT_tap = dscr("oT_tap", [128, 32 * NT], BF16) if TAPS else None
    rows_tap = dscr("rows_tap", [1, 6 * S], F32) if TAPS else None
    s1_tap = dscr("s1_tap", [1, 2 * S], F32) if TAPS else None
    r2_tap = dscr("r2_tap", [128, 32 * NT], F32) if TAPS else None

    root = ExitStack()
    with root:
        R = root.enter_context
        sb_id8 = R(nc.sbuf_tensor("sb_id8", [128, 128], U8))
        sb_idbf = R(nc.sbuf_tensor("sb_idbf", [128, 128], BF16))
        sb_1bf = R(nc.sbuf_tensor("sb_1bf", [128, 1], BF16))
        sb_11f = R(nc.sbuf_tensor("sb_11f", [1, 128], F32))
        sb_qlnw = R(nc.sbuf_tensor("sb_qlnw", [128, 32], F32))
        sb_qlnb = R(nc.sbuf_tensor("sb_qlnb", [128, 32], F32))
        sb_klnw = R(nc.sbuf_tensor("sb_klnw", [128, 32], F32))
        sb_klnb = R(nc.sbuf_tensor("sb_klnb", [128, 32], F32))
        guq_stack = ExitStack()
        guq_s = guq_stack.enter_context(nc.sbuf_tensor("guq_s", [64, INTER], BF16, side="right"))
        guq_z = guq_stack.enter_context(nc.sbuf_tensor("guq_z", [64, INTER], BF16, side="right"))
        sb_eps = R(nc.sbuf_tensor("sb_eps", [128, 1], F32))

        sems = {nm: Sem(R(nc.semaphore(nm))) for nm in
                ("s_pe", "s_dve", "s_act", "s_pool",
                 "ld0", "ld1", "ld2", "ld3", "w0", "w1", "w2", "w3",
                 "s0", "s1", "s2", "s3", "x0", "x1", "h0", "h1",
                 "xl", "ml", "bn", "v0")}
        s_pe, s_dve, s_act, s_pool = (sems[k] for k in
                                      ("s_pe", "s_dve", "s_act", "s_pool"))
        LD = [sems["ld0"], sems["ld1"], sems["ld2"], sems["ld3"]]
        W = [sems["w0"], sems["w1"], sems["w2"], sems["w3"]]
        ST = [sems["s0"], sems["s1"], sems["s2"], sems["s3"]]
        X = [sems["x0"], sems["x1"]]
        HS = [sems["h0"], sems["h1"]]
        xl, ml, bn, v0 = sems["xl"], sems["ml"], sems["bn"], sems["v0"]

        def ab_chain(P, rows, n, stats_wait, s1_src, s2_src):
            P.add("vector", lambda e: e.tensor_copy(rows[0:1, 0:n], s1_src),
                  waits=[stats_wait])
            P.add("vector", lambda e: e.tensor_copy(rows[0:1, n:2 * n], s2_src),
                  incs=[s_dve.inc()], drain=True)
            P.add("vector", lambda e: e.tensor_scalar_mul(
                rows[0:1, 2 * n:3 * n], rows[0:1, 0:n], 1.0 / D),
                waits=[s_dve.tgt()], incs=[s_dve.inc()], drain=True)
            P.add("scalar", lambda e: e.activation(
                out=rows[0:1, 3 * n:4 * n], in_=rows[0:1, 2 * n:3 * n],
                func=AF.Square), waits=[s_dve.tgt()], incs=[s_act.inc()])
            P.add("vector", lambda e: e.scalar_tensor_tensor(
                out=rows[0:1, 3 * n:4 * n], in0=rows[0:1, n:2 * n],
                scalar=1.0 / D, in1=rows[0:1, 3 * n:4 * n],
                op0=ALU.mult, op1=ALU.subtract),
                waits=[s_act.tgt()], incs=[s_dve.inc()], drain=True)
            P.add("scalar", lambda e: e.activation(
                out=rows[0:1, 4 * n:5 * n], in_=rows[0:1, 3 * n:4 * n],
                func=AF.Sqrt, bias=sb_eps[0:1, 0:1]),
                waits=[s_dve.tgt()], incs=[s_act.inc()])
            P.add("vector", lambda e: e.reciprocal(
                rows[0:1, 4 * n:5 * n], rows[0:1, 4 * n:5 * n]),
                waits=[s_act.tgt()], incs=[s_dve.inc()], drain=True)
            P.add("vector", lambda e: e.scalar_tensor_tensor(
                out=rows[0:1, 5 * n:6 * n], in0=rows[0:1, 2 * n:3 * n],
                scalar=-1.0, in1=rows[0:1, 4 * n:5 * n],
                op0=ALU.mult, op1=ALU.mult),
                waits=[s_dve.tgt()], incs=[s_dve.inc()], drain=True)
            return s_dve.tgt()

        def ab_bounce(P, rows, n, ab_done, ab_sb):
            P.add("sync", lambda e: e.dma_start(
                out=ab_dram.ap()[0:1, 0:2 * n], in_=rows[0:1, 4 * n:6 * n]),
                waits=[ab_done], incs=[bn.inc(16)])
            P.add("sync", lambda e: e.dma_start(
                out=ab_sb[:, 0:2 * n],
                in_=bass.AP(ab_dram, 0, [[0, 128], [1, 2 * n]])),
                waits=[bn.tgt()], incs=[bn.inc(16)])
            return bn.tgt()

        def ab_bcast(P, ps, rows, n, ab_done, a_off, b_off):
            nch = n // 512
            for c in range(nch):
                P.add("tensor", lambda e, c=c: e.matmul(
                    ps[:, a_off + 512 * c:a_off + 512 * (c + 1)], sb_11f[:, :],
                    rows[0:1, 4 * n + 512 * c:4 * n + 512 * (c + 1)],
                    start=True, stop=True),
                    waits=[ab_done] if c == 0 else [])
                P.add("tensor", lambda e, c=c: e.matmul(
                    ps[:, b_off + 512 * c:b_off + 512 * (c + 1)], sb_11f[:, :],
                    rows[0:1, 5 * n + 512 * c:5 * n + 512 * (c + 1)],
                    start=True, stop=True),
                    incs=[s_pe.inc()] if c == nch - 1 else [])
            return s_pe.tgt()

        with ExitStack() as st:
            E = st.enter_context
            guqf = E(nc.sbuf_tensor("guqf", [128, INTER], F32))
            zf_f = E(nc.sbuf_tensor("zf_f", [64, INTER], F32))
            zrows = E(nc.sbuf_tensor("zrows", [64, INTER + 1376], I32))
            P = Plan()
            for dst, src in ((sb_id8, ident_u8), (sb_idbf, ident_bf),
                             (sb_1bf, ones_bf), (sb_11f, ones1f),
                             (sb_qlnw, qlnw), (sb_qlnb, qlnb),
                             (sb_klnw, klnw), (sb_klnb, klnb)):
                P.add("sync", lambda e, d=dst, s=src: e.dma_start(out=d[:, :], in_=s.ap()),
                      incs=[ml.inc(16)])
            P.add("sync", lambda e: e.dma_start(out=guqf[:, :], in_=sc_guq.ap()),
                  incs=[ml.inc(16)])
            P.add("sync", lambda e: e.dma_start(
                out=zrows[0:32, INTER:INTER + 1376], in_=qz_gate.ap()),
                incs=[ml.inc(16)])
            P.add("sync", lambda e: e.dma_start(
                out=zrows[32:64, INTER:INTER + 1376], in_=qz_up.ap()),
                incs=[ml.inc(16)])
            P.add("vector", lambda e: e.memset(sb_eps[:, :], LN_EPS),
                  waits=[ml.tgt()])
            P.add("vector", lambda e: e.tensor_scalar_mul(
                sb_qlnw[:, :], sb_qlnw[:, :], QSCALE))
            P.add("vector", lambda e: e.tensor_scalar_mul(
                sb_qlnb[:, :], sb_qlnb[:, :], QSCALE))
            for s_ in range(8):
                P.add("vector", lambda e, s_=s_: e.tensor_scalar(
                    out=bass.AP(zrows, s_, [[INTER + 1376, 64], [8, 1376]]),
                    in0=zrows[0:64, INTER:INTER + 1376], scalar1=4 * s_, scalar2=15,
                    op0=ALU.logical_shift_right, op1=ALU.bitwise_and),
                    incs=[s_dve.inc()] if s_ == 7 else [])
            P.add("vector", lambda e: e.tensor_scalar_add(
                zf_f[0:64, 0:INTER], zrows[0:64, 0:INTER], 1.0),
                waits=[s_dve.tgt()], incs=[s_dve.inc()])
            P.add("vector", lambda e: e.tensor_mul(
                guq_z[0:64, :], zf_f[0:64, :], guqf[0:64, :]),
                waits=[s_dve.tgt()], incs=[s_dve.inc()])
            P.add("vector", lambda e: e.tensor_copy(guq_s[0:64, :], guqf[0:64, :]),
                  incs=[s_dve.inc()])
            emit(nc, P)

        with ExitStack() as st:
            E = st.enter_context
            x_own_bf = E(nc.sbuf_tensor("x_own_bf", [128, 32 * NT], BF16))
            wblk = [E(nc.sbuf_tensor(f"wblkA{i}", [128, 32 * 128], BF16)) for i in range(2)]
            stq = [E(nc.sbuf_tensor(f"stqA{i}", [128, NT], BF16)) for i in range(2)]
            ps = E(nc.psum_tensor("psA", [128, 2048], F32))
            P = Plan()
            for dt in range(32):
                P.add("gpsimd", lambda e, dt=dt: e.dma_start(
                    out=x_own_bf[:, NT * dt:NT * (dt + 1)],
                    in_=x_own.ap()[128 * dt:128 * (dt + 1), :]),
                    incs=[xl.inc(16)])
            x_rdy = xl.tgt()
            wfree = [None, None]
            evac_done = [None, None]
            stq_free = [None, None]
            for et in range(32):
                par = et % 2
                P.add("gpsimd", lambda e, et=et, par=par: e.dma_start(
                    out=wblk[par][:, :], in_=wqkv_st.ap()[et]),
                    waits=[wfree[par]], incs=[W[par].inc(16)])
                w_rdy = W[par].tgt()
                for dt in range(32):
                    for c in range(2):
                        P.add("tensor", lambda e, dt=dt, c=c, par=par: e.matmul(
                            ps[:, 1024 * par + 512 * c:1024 * par + 512 * (c + 1)],
                            wblk[par][:, 128 * dt:128 * (dt + 1)],
                            x_own_bf[:, NT * dt + 512 * c:NT * dt + 512 * (c + 1)],
                            start=(dt == 0), stop=(dt == 31)),
                            waits=[w_rdy, x_rdy, evac_done[par]] if (dt == 0 and c == 0) else [],
                            incs=[s_pe.inc()] if (dt == 31 and c == 1) else [],
                            drain=(dt == 31 and c == 1))
                mm_done = s_pe.tgt()
                wfree[par] = mm_done
                P.add("vector", lambda e, par=par: e.tensor_scalar(
                    out=stq[par][:, :], in0=ps[:, 1024 * par:1024 * (par + 1)],
                    scalar1=-CLIP, scalar2=CLIP, op0=ALU.max, op1=ALU.min),
                    waits=[mm_done, stq_free[par]], incs=[s_dve.inc()])
                evac_done[par] = s_dve.tgt()
                P.add("sync", lambda e, et=et, par=par: e.dma_start(
                    out=q_pre.ap()[et], in_=stq[par][:, :]),
                    waits=[evac_done[par]], incs=[ST[par].inc(16)])
                stq_free[par] = ST[par].tgt()
            P.add("sync", lambda e: e.nop(), waits=[ST[0].tgt(), ST[1].tgt()])
            P.add("gpsimd", lambda e: e.nop(),
                  waits=[xl.tgt(), W[0].tgt(), W[1].tgt()])
            emit(nc, P)

        with ExitStack() as st:
            E = st.enter_context
            x_full_bf = E(nc.sbuf_tensor("x_full_bf", [128, 32 * S], BF16))
            wblk = [E(nc.sbuf_tensor(f"wblkB{i}", [128, 32 * 128], BF16)) for i in range(2)]
            stkv = [E(nc.sbuf_tensor(f"stkvB{i}", [128, S], BF16)) for i in range(2)]
            stvt = E(nc.sbuf_tensor("stvtB", [128, S], BF16))
            ps = E(nc.psum_tensor("psB", [128, 2048], F32))
            vtr = E(nc.psum_tensor("psBv", [128, 1024], F32))
            P = Plan()
            for dt in range(32):
                P.add("gpsimd", lambda e, dt=dt: e.dma_start(
                    out=x_full_bf[:, S * dt:S * (dt + 1)],
                    in_=x_full.ap()[128 * dt:128 * (dt + 1), :]),
                    incs=[xl.inc(16)])
            x_rdy = xl.tgt()
            wfree = [None, None]
            evac_done = None
            st_free = [None, None]
            vt_free = None
            vt_dma = None
            for et in range(32, 96):
                par = et % 2
                is_k = et < 64
                P.add("gpsimd", lambda e, et=et, par=par: e.dma_start(
                    out=wblk[par][:, :], in_=wqkv_st.ap()[et]),
                    waits=[wfree[par]], incs=[W[par].inc(16)])
                w_rdy = W[par].tgt()
                for dt in range(32):
                    for c in range(4):
                        P.add("tensor", lambda e, dt=dt, c=c, par=par: e.matmul(
                            ps[:, 512 * c:512 * (c + 1)],
                            wblk[par][:, 128 * dt:128 * (dt + 1)],
                            x_full_bf[:, S * dt + 512 * c:S * dt + 512 * (c + 1)],
                            start=(dt == 0), stop=(dt == 31)),
                            waits=[w_rdy, x_rdy, evac_done] if (dt == 0 and c == 0) else [],
                            incs=[s_pe.inc()] if (dt == 31 and c == 3) else [],
                            drain=(dt == 31 and c == 3))
                mm_done = s_pe.tgt()
                wfree[par] = mm_done
                P.add("vector", lambda e, par=par: e.tensor_scalar(
                    out=stkv[par][:, :], in0=ps[:, :],
                    scalar1=-CLIP, scalar2=CLIP, op0=ALU.max, op1=ALU.min),
                    waits=[mm_done, st_free[par]], incs=[s_dve.inc()])
                evac_done = s_dve.tgt()
                if is_k:
                    P.add("sync", lambda e, et=et, par=par: e.dma_start(
                        out=k_pre.ap()[et - 32], in_=stkv[par][:, :]),
                        waits=[evac_done], incs=[ST[par].inc(16)])
                    st_free[par] = ST[par].tgt()
                else:
                    hh = et - 64
                    clip_done = evac_done
                    for half in range(2):
                        for j in range(8):
                            ts_ = 8 * half + j
                            P.add("tensor", lambda e, ts_=ts_, j=j, par=par: e.matmul(
                                vtr[:, 128 * j:128 * (j + 1)],
                                stkv[par][:, 128 * ts_:128 * (ts_ + 1)],
                                sb_idbf[:, :], start=True, stop=True),
                                waits=[clip_done, vt_free] if j == 0 else [],
                                incs=[s_pe.inc()] if j == 7 else [],
                                drain=(j == 7))
                        tr_done = s_pe.tgt()
                        P.add("vector", lambda e, half=half: e.tensor_copy(
                            stvt[:, 1024 * half:1024 * (half + 1)], vtr[:, :]),
                            waits=[tr_done, vt_dma], incs=[s_dve.inc()])
                        vt_free = s_dve.tgt()
                    st_free[par] = s_pe.tgt()
                    P.add("sync", lambda e, hh=hh: e.dma_start(
                        out=v_d.ap()[hh].transpose([1, 0, 2]),
                        in_=stvt[:, :].rearrange("p (a c) -> p a c", a=16, c=128)),
                        waits=[vt_free], incs=[v0.inc(16)])
                    vt_dma = v0.tgt()
            P.add("sync", lambda e: e.nop(),
                  waits=[ST[0].tgt(), ST[1].tgt(), v0.tgt()])
            P.add("gpsimd", lambda e: e.nop(),
                  waits=[xl.tgt(), W[0].tgt(), W[1].tgt()])
            emit(nc, P)

        def ln_qk(src_pre, dst_post, n_cols, w_col, b_col):
            nch = n_cols // 512
            with ExitStack() as st:
                E = st.enter_context
                qt = [E(nc.sbuf_tensor(f"p2t{i}_{n_cols}", [128, n_cols], BF16)) for i in range(2)]
                qsq = E(nc.sbuf_tensor(f"p2sq_{n_cols}", [128, n_cols], BF16))
                rows = E(nc.sbuf_tensor(f"p2rows_{n_cols}", [1, 6 * n_cols], F32))
                t1 = [E(nc.sbuf_tensor(f"p2a{i}_{n_cols}", [128, n_cols], F32)) for i in range(2)]
                ob = [E(nc.sbuf_tensor(f"p2o{i}_{n_cols}", [128, n_cols], BF16)) for i in range(2)]
                ab_sb = E(nc.sbuf_tensor(f"p2ab_{n_cols}", [128, 2 * n_cols], F32))
                ps = E(nc.psum_tensor(f"psP2_{n_cols}", [128, 4096], F32))
                P = Plan()
                ld_free = [None, None]
                sq_free = None
                for et in range(32):
                    par = et % 2
                    P.add("sync", lambda e, et=et, par=par: e.dma_start(
                        out=qt[par][:, :], in_=src_pre.ap()[et]),
                        waits=[ld_free[par]], incs=[LD[par].inc(16)])
                    t_rdy = LD[par].tgt()
                    for c in range(nch):
                        P.add("tensor", lambda e, c=c, par=par, et=et: e.matmul(
                            ps[0:1, 512 * c:512 * (c + 1)], sb_1bf[:, :],
                            qt[par][:, 512 * c:512 * (c + 1)],
                            start=(et == 0), stop=(et == 31)),
                            waits=[t_rdy] if c == 0 else [])
                    P.add("scalar", lambda e, par=par: e.activation(
                        out=qsq[:, :], in_=qt[par][:, :], func=AF.Square),
                        waits=[t_rdy, sq_free], incs=[s_act.inc()])
                    sq_rdy = s_act.tgt()
                    for c in range(nch):
                        P.add("tensor", lambda e, c=c, et=et: e.matmul(
                            ps[0:1, 2048 + 512 * c:2048 + 512 * (c + 1)], sb_1bf[:, :],
                            qsq[:, 512 * c:512 * (c + 1)],
                            start=(et == 0), stop=(et == 31)),
                            waits=[sq_rdy] if c == 0 else [],
                            incs=[s_pe.inc()] if c == nch - 1 else [],
                            drain=(c == nch - 1 and et == 31))
                    sq_free = s_pe.tgt()
                    ld_free[par] = s_pe.tgt()
                stats_done = s_pe.tgt()
                n = n_cols
                ab_done = ab_chain(P, rows, n, stats_done,
                                   ps[0:1, 0:n], ps[0:1, 2048:2048 + n])
                if TAPS and n_cols == S:
                    P.add("sync", lambda e: e.dma_start(
                        out=rows_tap.ap(), in_=rows[0:1, :]),
                        waits=[ab_done], incs=[ml.inc(16)])
                bdone = ab_bounce(P, rows, n, ab_done, ab_sb)
                ld_free = [[stats_done], [stats_done]]
                ap_done = [None, None]
                ob_free = [None, None]
                for et in range(32):
                    par = et % 2
                    P.add("sync", lambda e, et=et, par=par: e.dma_start(
                        out=qt[par][:, :], in_=src_pre.ap()[et]),
                        waits=ld_free[par], incs=[LD[par].inc(16)])
                    t_rdy = LD[par].tgt()
                    P.add("vector", lambda e, par=par: e.tensor_mul(
                        t1[par][:, :], qt[par][:, :], ab_sb[:, 0:n]),
                        waits=[t_rdy, bdone, ap_done[par]], incs=[s_dve.inc()])
                    P.add("vector", lambda e, par=par: e.tensor_add(
                        t1[par][:, :], t1[par][:, :], ab_sb[:, n:2 * n]),
                        waits=[s_dve.tgt()], incs=[s_dve.inc()])
                    tt_done = s_dve.tgt()
                    ld_free[par] = [tt_done]
                    P.add("scalar", lambda e, et=et, par=par: e.activation(
                        out=ob[par][:, :], in_=t1[par][:, :], func=AF.Identity,
                        bias=b_col[:, et:et + 1], scale=w_col[:, et:et + 1]),
                        waits=[tt_done, ob_free[par]], incs=[s_act.inc()])
                    ap_done[par] = s_act.tgt()
                    P.add("sync", lambda e, et=et, par=par: e.dma_start(
                        out=dst_post.ap()[et], in_=ob[par][:, :]),
                        waits=[ap_done[par]], incs=[ST[par].inc(16)])
                    ob_free[par] = ST[par].tgt()
                P.add("sync", lambda e: e.nop(),
                      waits=[ST[0].tgt(), ST[1].tgt(), ml.tgt()])
                emit(nc, P)

        ln_qk(q_pre, q_d, NT, sb_qlnw, sb_qlnb)
        ln_qk(k_pre, kT_d, S, sb_klnw, sb_klnb)
        if STOP_AFTER == "P2":
            return nc

        with ExitStack() as st_r:
            ab4_sb = st_r.enter_context(nc.sbuf_tensor("ab4_sb", [128, 2048], F32))

            with ExitStack() as stA:
                sb_oT = stA.enter_context(nc.sbuf_tensor("sb_oT", [128, 32 * NT], BF16))

                with ExitStack() as st:
                    E = st.enter_context
                    sb_rel = E(nc.sbuf_tensor("sb_rel", [128, N_REL * 512], F32))
                    kh = [E(nc.sbuf_tensor(f"kh{i}", [128, S], BF16)) for i in range(2)]
                    vh = [E(nc.sbuf_tensor(f"vh{i}", [128, S], BF16)) for i in range(2)]
                    qh = [E(nc.sbuf_tensor(f"qh{i}", [128, NT], BF16)) for i in range(2)]
                    Ssb = [E(nc.sbuf_tensor(f"Ssb{i}", [128, 2048], F32)) for i in range(2)]
                    Pbf = [E(nc.sbuf_tensor(f"Pbf{i}", [128, 2048], BF16)) for i in range(2)]
                    ptsb = [E(nc.sbuf_tensor(f"ptsb{i}", [128, 512], BF16)) for i in range(2)]
                    diag = [E(nc.sbuf_tensor(f"diag{i}", [128, 128], BF16)) for i in range(2)]
                    cols = [E(nc.sbuf_tensor(f"cols{i}", [128, 2], F32)) for i in range(2)]
                    ps = E(nc.psum_tensor("psP3", [128, 4096], F32))
                    P = Plan()
                    P.add("sync", lambda e: e.dma_start(
                        out=sb_rel[:, :].rearrange("p (n w) -> p n w", n=N_REL, w=512),
                        in_=rel_in.ap().transpose([1, 0, 2])),
                        incs=[ml.inc(16)])
                    rel_rdy = ml.tgt()

                    ss = {}
                    pv_last = {0: None, 1: None}
                    last_use = [None, None]

                    def plan_scores(h, i, hrdy):
                        base = 2048 * (i % 2)
                        Ei = SLOT_EXT[i]
                        nch = SLOT_CHUNKS[i]
                        hp = h % 2
                        w0 = [hrdy, rel_rdy, ss.get(("oev", i - 2)), ss.get(("sev", i - 2))]
                        for c in range(nch):
                            cw = min(512, Ei * 128 - 512 * c)
                            P.add("tensor", lambda e, i=i, c=c, cw=cw, base=base, hp=hp: e.matmul(
                                ps[:, base + 512 * c:base + 512 * c + cw],
                                qh[hp][:, 128 * i:128 * (i + 1)],
                                kh[hp][:, 512 * c:512 * c + cw],
                                start=True, stop=True),
                                waits=w0 if c == 0 else [],
                                incs=[s_pe.inc()] if c == nch - 1 else [],
                                drain=(c == nch - 1))
                        ss[("mm", i)] = s_pe.tgt()
                        for c in range(nch):
                            cw = min(512, Ei * 128 - 512 * c)
                            ri = REL_BASE[i] + c
                            P.add("vector", lambda e, i=i, c=c, cw=cw, base=base, ri=ri, h=h: e.scalar_tensor_tensor(
                                out=Ssb[i % 2][:, 512 * c:512 * c + cw],
                                in0=sb_rel[:, 512 * ri:512 * ri + cw],
                                scalar=SLOPES[h],
                                in1=ps[:, base + 512 * c:base + 512 * c + cw],
                                op0=ALU.mult, op1=ALU.add),
                                waits=[ss[("mm", i)], ss.get(("exp", i - 2))] if c == 0 else [],
                                incs=[s_dve.inc()] if c == nch - 1 else [])
                        ss[("sev", i)] = s_dve.tgt()
                        P.add("vector", lambda e, i=i, Ei=Ei: e.reduce_max(
                            cols[i % 2][:, 0:1], Ssb[i % 2][:, 0:Ei * 128],
                            axis=AX.X, negate=True),
                            waits=[ss.get(("diag", i - 2))],
                            incs=[s_dve.inc()])
                        mx = s_dve.tgt()
                        P.add("scalar", lambda e, i=i, Ei=Ei: e.activation(
                            out=Pbf[i % 2][:, 0:Ei * 128], in_=Ssb[i % 2][:, 0:Ei * 128],
                            func=AF.Exp, bias=cols[i % 2][:, 0:1],
                            accum_out=cols[i % 2][:, 1:2]),
                            waits=[mx, ss.get(("ptdone", i - 2))], incs=[s_act.inc()])
                        ss[("exp", i)] = s_act.tgt()
                        P.add("vector", lambda e, i=i: e.reciprocal(
                            cols[i % 2][:, 1:2], cols[i % 2][:, 1:2]),
                            waits=[ss[("exp", i)]], incs=[s_dve.inc()])
                        P.add("vector", lambda e, i=i: e.tensor_scalar_mul(
                            diag[i % 2][:, :], sb_id8[:, :], cols[i % 2][:, 1:2]),
                            waits=[s_dve.tgt(), ss.get(("ptdone", i - 2))],
                            incs=[s_dve.inc()])
                        ss[("diag", i)] = s_dve.tgt()

                    def plan_ptpv(h, i):
                        base = 2048 * (i % 2)
                        Ei = SLOT_EXT[i]
                        hp = h % 2
                        o_off = base + 1024
                        packs = []
                        kt = 0
                        while kt < Ei:
                            w = min(4, Ei - kt)
                            packs.append((kt, w))
                            kt += w
                        for pi, (kt0, w) in enumerate(packs):
                            pp = pi % 2
                            pt_off = base + 512 * pp
                            for u in range(w):
                                kt = kt0 + u
                                P.add("tensor", lambda e, i=i, kt=kt, u=u, pt_off=pt_off: e.matmul(
                                    ps[:, pt_off + 128 * u:pt_off + 128 * (u + 1)],
                                    Pbf[i % 2][:, 128 * kt:128 * (kt + 1)],
                                    diag[i % 2][:, :], start=True, stop=True),
                                    waits=([ss[("diag", i)]] if pi == 0 and u == 0 else []) +
                                          ([ss.get(("ptev", i, pi - 2))] if pi >= 2 and u == 0 else []),
                                    incs=[s_pe.inc()] if u == w - 1 else [],
                                    drain=(u == w - 1))
                            ptmm = s_pe.tgt()
                            P.add("scalar", lambda e, pp=pp, w=w, pt_off=pt_off: e.copy(
                                ptsb[pp][:, 0:128 * w], ps[:, pt_off:pt_off + 128 * w]),
                                waits=[ptmm, pv_last[pp]], incs=[s_act.inc()])
                            ss[("ptev", i, pi)] = s_act.tgt()
                            for u in range(w):
                                kt = kt0 + u
                                P.add("tensor", lambda e, i=i, kt=kt, u=u, pp=pp, hp=hp, Ei=Ei, o_off=o_off: e.matmul(
                                    ps[:, o_off:o_off + 128],
                                    vh[hp][:, 128 * kt:128 * (kt + 1)],
                                    ptsb[pp][:, 128 * u:128 * (u + 1)],
                                    start=(kt == 0), stop=(kt == Ei - 1)),
                                    waits=[ss[("ptev", i, pi)]] if u == 0 else [],
                                    incs=[s_pe.inc()] if kt == Ei - 1 else [],
                                    drain=(kt == Ei - 1))
                            pv_last[pp] = s_pe.tgt()
                        ss[("ptdone", i)] = s_pe.tgt()
                        P.add("vector", lambda e, h=h, i=i, o_off=o_off: e.tensor_copy(
                            sb_oT[:, NT * h + 128 * i:NT * h + 128 * (i + 1)],
                            ps[:, o_off:o_off + 128]),
                            waits=[ss[("ptdone", i)]], incs=[s_dve.inc()])
                        ss[("oev", i)] = s_dve.tgt()

                    for h in range(H):
                        hp = h % 2
                        P.add("sync", lambda e, h=h, hp=hp: e.dma_start(
                            out=kh[hp][:, :], in_=kT_d.ap()[h]),
                            waits=[last_use[hp]], incs=[LD[hp].inc(16)])
                        P.add("sync", lambda e, h=h, hp=hp: e.dma_start(
                            out=vh[hp][:, :].rearrange("p (a c) -> p a c", a=16, c=128),
                            in_=v_d.ap()[h].transpose([1, 0, 2])),
                            incs=[LD[hp].inc(16)])
                        P.add("sync", lambda e, h=h, hp=hp: e.dma_start(
                            out=qh[hp][:, :], in_=q_d.ap()[h]),
                            incs=[LD[hp].inc(16)])
                        hrdy = LD[hp].tgt()
                        carry = {("oev", -2): ss.get(("oev", 6)),
                                 ("oev", -1): ss.get(("oev", 7)),
                                 ("sev", -2): ss.get(("sev", 6)),
                                 ("sev", -1): ss.get(("sev", 7)),
                                 ("exp", -2): ss.get(("exp", 6)),
                                 ("exp", -1): ss.get(("exp", 7)),
                                 ("diag", -2): ss.get(("diag", 6)),
                                 ("diag", -1): ss.get(("diag", 7)),
                                 ("ptdone", -2): ss.get(("ptdone", 6)),
                                 ("ptdone", -1): ss.get(("ptdone", 7))}
                        ss.clear()
                        ss.update(carry)
                        plan_scores(h, 0, hrdy)
                        plan_scores(h, 1, hrdy)
                        for i in range(8):
                            plan_ptpv(h, i)
                            if i + 2 < 8:
                                plan_scores(h, i + 2, hrdy)
                        last_use[hp] = s_pe.tgt()
                    P.add("sync", lambda e: e.nop(),
                          waits=[LD[0].tgt(), LD[1].tgt()])
                    emit(nc, P)

                if TAPS:
                    P = Plan()
                    P.add("sync", lambda e: e.dma_start(out=oT_tap.ap(), in_=sb_oT[:, :]),
                          incs=[ml.inc(16)])
                    P.add("sync", lambda e: e.nop(), waits=[ml.tgt()])
                    emit(nc, P)

                with ExitStack() as st:
                    E = st.enter_context
                    wblk = [E(nc.sbuf_tensor(f"wblkC{i}", [128, 32 * 128], BF16)) for i in range(2)]
                    xtile = [E(nc.sbuf_tensor(f"xtC{i}", [128, NT], F32)) for i in range(2)]
                    r1t = [E(nc.sbuf_tensor(f"r1C{i}", [128, NT], F32)) for i in range(2)]
                    r1b = E(nc.sbuf_tensor("r1bC", [128, NT], BF16))
                    rows4 = E(nc.sbuf_tensor("rows4", [1, 6 * NT], F32))
                    r1sq = E(nc.sbuf_tensor("r1sqC", [128, NT], BF16))
                    ps = E(nc.psum_tensor("psP4", [128, 4096], F32))
                    P = Plan()
                    wfree = [None, None]
                    xfree = [None, None]
                    r1free = [[], []]
                    mm_evd = [None, None]
                    sq_free = None
                    cast_free = None
                    for dt in range(32):
                        par = dt % 2
                        P.add("gpsimd", lambda e, dt=dt, par=par: e.dma_start(
                            out=wblk[par][:, :], in_=wout_st.ap()[dt]),
                            waits=[wfree[par]], incs=[W[par].inc(16)])
                        w_rdy = W[par].tgt()
                        P.add("sync", lambda e, dt=dt, par=par: e.dma_start(
                            out=xtile[par][:, :],
                            in_=x_own.ap()[128 * dt:128 * (dt + 1), :]),
                            waits=[xfree[par]], incs=[X[par].inc(16)])
                        x_rdy = X[par].tgt()
                        base = 1024 * par
                        for et in range(32):
                            for c in range(2):
                                P.add("tensor", lambda e, et=et, c=c, par=par, base=base: e.matmul(
                                    ps[:, base + 512 * c:base + 512 * (c + 1)],
                                    wblk[par][:, 128 * et:128 * (et + 1)],
                                    sb_oT[:, NT * et + 512 * c:NT * et + 512 * (c + 1)],
                                    start=(et == 0), stop=(et == 31)),
                                    waits=[w_rdy, mm_evd[par]] if (et == 0 and c == 0) else [],
                                    incs=[s_pe.inc()] if (et == 31 and c == 1) else [],
                                    drain=(et == 31 and c == 1))
                        mm_done = s_pe.tgt()
                        wfree[par] = mm_done
                        P.add("vector", lambda e, par=par, base=base: e.tensor_add(
                            r1t[par][:, :], xtile[par][:, :], ps[:, base:base + NT]),
                            waits=[mm_done, x_rdy] + r1free[par], incs=[s_dve.inc()])
                        r1_done = s_dve.tgt()
                        mm_evd[par] = r1_done
                        xfree[par] = r1_done
                        P.add("sync", lambda e, dt=dt, par=par: e.dma_start(
                            out=r1_d.ap()[dt], in_=r1t[par][:, :]),
                            waits=[r1_done], incs=[ST[par].inc(16)])
                        r1_dma = ST[par].tgt()
                        P.add("vector", lambda e, par=par: e.tensor_copy(
                            r1b[:, :], r1t[par][:, :]),
                            waits=[cast_free], incs=[s_dve.inc()])
                        b_rdy = s_dve.tgt()
                        P.add("scalar", lambda e, par=par: e.activation(
                            out=r1sq[:, :], in_=r1t[par][:, :], func=AF.Square),
                            waits=[r1_done, sq_free], incs=[s_act.inc()])
                        s_rdy = s_act.tgt()
                        for c in range(2):
                            P.add("tensor", lambda e, c=c, dt=dt: e.matmul(
                                ps[0:1, 2048 + 512 * c:2048 + 512 * (c + 1)], sb_1bf[:, :],
                                r1b[:, 512 * c:512 * (c + 1)],
                                start=(dt == 0), stop=(dt == 31)),
                                waits=[b_rdy] if c == 0 else [])
                            P.add("tensor", lambda e, c=c, dt=dt: e.matmul(
                                ps[0:1, 3072 + 512 * c:3072 + 512 * (c + 1)], sb_1bf[:, :],
                                r1sq[:, 512 * c:512 * (c + 1)],
                                start=(dt == 0), stop=(dt == 31)),
                                waits=[s_rdy] if c == 0 else [],
                                incs=[s_pe.inc()] if c == 1 else [],
                                drain=(c == 1 and dt == 31))
                        cast_free = s_pe.tgt()
                        sq_free = s_pe.tgt()
                        r1free[par] = [s_pe.tgt(), r1_dma]
                    stats_done = s_pe.tgt()
                    ab4 = ab_chain(P, rows4, NT, stats_done,
                                   ps[0:1, 2048:2048 + NT], ps[0:1, 3072:3072 + NT])
                    ab_bounce(P, rows4, NT, ab4, ab4_sb)
                    P.add("sync", lambda e: e.nop(),
                          waits=[ST[0].tgt(), ST[1].tgt(), bn.tgt()])
                    P.add("gpsimd", lambda e: e.nop(),
                          waits=[W[0].tgt(), W[1].tgt()])
                    emit(nc, P)

            if STOP_AFTER == "P4":
                return nc
            with ExitStack() as stX:
                sb_x1bf = stX.enter_context(nc.sbuf_tensor("sb_x1bf", [128, 32 * NT], BF16))

                with ExitStack() as st:
                    E = st.enter_context
                    r1t = [E(nc.sbuf_tensor(f"r1D{i}", [128, NT], F32)) for i in range(2)]
                    P = Plan()
                    bdone = None
                    ld_free = [[], []]
                    ap_done = [None, None]
                    for dt in range(32):
                        par = dt % 2
                        P.add("sync", lambda e, dt=dt, par=par: e.dma_start(
                            out=r1t[par][:, :], in_=r1_d.ap()[dt]),
                            waits=ld_free[par], incs=[LD[par].inc(16)])
                        t_rdy = LD[par].tgt()
                        P.add("vector", lambda e, par=par: e.tensor_mul(
                            r1t[par][:, :], r1t[par][:, :], ab4_sb[:, 0:NT]),
                            waits=[t_rdy, ap_done[par]], incs=[s_dve.inc()])
                        P.add("vector", lambda e, par=par: e.tensor_add(
                            r1t[par][:, :], r1t[par][:, :], ab4_sb[:, NT:2 * NT]),
                            waits=[s_dve.tgt()], incs=[s_dve.inc()])
                        x1_done = s_dve.tgt()
                        P.add("sync", lambda e, dt=dt, par=par: e.dma_start(
                            out=x1_d.ap()[dt], in_=r1t[par][:, :]),
                            waits=[x1_done], incs=[ST[par].inc(16)])
                        x1_dma = ST[par].tgt()
                        P.add("scalar", lambda e, dt=dt, par=par: e.activation(
                            out=sb_x1bf[:, NT * dt:NT * (dt + 1)], in_=r1t[par][:, :],
                            func=AF.Copy), waits=[x1_done], incs=[s_act.inc()])
                        ap_done[par] = s_act.tgt()
                        ld_free[par] = [x1_dma, ap_done[par]]
                    P.add("sync", lambda e: e.nop(),
                          waits=[ST[0].tgt(), ST[1].tgt()])
                    emit(nc, P)

                with ExitStack() as st:
                    E = st.enter_context
                    sb_selgu = E(nc.sbuf_tensor("sb_selgu", [64, 4 * 128], BF16))
                    sb_selTgu = E(nc.sbuf_tensor("sb_selTgu", [128, 4 * 64], BF16))
                    xsn = E(nc.sbuf_tensor("xsn", [64, 1024], BF16))
                    qst = [E(nc.sbuf_tensor(f"qst{i}", [128, 256], I32)) for i in range(2)]
                    w4i = [E(nc.sbuf_tensor(f"w4i{i}", [128, 256], I32)) for i in range(2)]
                    wc = [[E(nc.sbuf_tensor(f"wc{s}{i}", [128, 32 * 256], BF16))
                           for i in range(2)] for s in range(2)]
                    silt = E(nc.sbuf_tensor("silt", [128, 512], F32))
                    hst = [E(nc.sbuf_tensor(f"hst{i}", [128, NT], BF16)) for i in range(2)]
                    ps = E(nc.psum_tensor("psP5", [128, 4096], F32))
                    P = Plan()
                    P.add("sync", lambda e: e.dma_start(out=sb_selgu[:, :], in_=sel_gu.ap()),
                          incs=[ml.inc(16)])
                    P.add("sync", lambda e: e.dma_start(out=sb_selTgu[:, :], in_=selT_gu.ap()),
                          incs=[ml.inc(16)])
                    sel_rdy = ml.tgt()
                    P.add("tensor", lambda e: e.nop(), waits=[sel_rdy])
                    for tc in range(2):
                        for kt in range(32):
                            c = kt % 4
                            P.add("tensor", lambda e, tc=tc, kt=kt, c=c: e.matmul(
                                ps[0:64, 1024 * tc:1024 * tc + 512],
                                sb_selTgu[:, 64 * c:64 * (c + 1)],
                                sb_x1bf[:, NT * kt + 512 * tc:NT * kt + 512 * (tc + 1)],
                                start=(kt == 0), stop=(kt == 31)),
                                incs=[s_pe.inc()] if (kt == 31 and tc == 1) else [],
                                drain=(kt == 31 and tc == 1))
                    xs_done = s_pe.tgt()
                    P.add("vector", lambda e: e.tensor_scalar_mul(
                        xsn[:, 0:512], ps[0:64, 0:512], -1.0), waits=[xs_done])
                    P.add("vector", lambda e: e.tensor_scalar_mul(
                        xsn[:, 512:1024], ps[0:64, 1024:1536], -1.0),
                        incs=[s_dve.inc()])
                    xs_evac = s_dve.tgt()

                    NBLK = 43
                    dq = {}
                    regfree = {}

                    def plan_bcast(r):
                        if r >= NBLK * 8:
                            return
                        blk, rem = divmod(r, 8)
                        src_, c = divmod(rem, 4)
                        n0 = 256 * blk
                        bb = 3072 + 512 * (r % 2)
                        P.add("tensor", lambda e, src_=src_, c=c, n0=n0, bb=bb: e.matmul(
                            ps[:, bb:bb + 256],
                            sb_selgu[32 * src_:32 * (src_ + 1), 128 * c:128 * (c + 1)],
                            guq_s[32 * src_:32 * (src_ + 1), n0:n0 + 256],
                            start=True, stop=True),
                            waits=[dq.get(("bfree", r % 2))], incs=[s_pe.inc()])
                        dq[("bc", r)] = s_pe.tgt()

                    def dequant_round(r):
                        blk, rem = divmod(r, 8)
                        src, c = divmod(rem, 4)
                        n0 = 256 * blk
                        par = blk % 2
                        rp = r % 2
                        bb = 3072 + 512 * (r % 2)
                        qsrc = qw_gate if src == 0 else qw_up
                        P.add("sync", lambda e, qsrc=qsrc, c=c, n0=n0, rp=rp: e.dma_start(
                            out=qst[rp][:, :],
                            in_=bass.AP(qsrc, 128 * c * INTER + n0,
                                        [[INTER, 128], [1, 256]])),
                            waits=[dq.get(("qst", rp))], incs=[LD[rp].inc(16)])
                        q_rdy = LD[rp].tgt()
                        bc_done = dq[("bc", r)]
                        for s_ in range(8):
                            kt = 4 * s_ + c
                            P.add("vector", lambda e, s_=s_, rp=rp: e.tensor_scalar(
                                out=w4i[rp][:, :], in0=qst[rp][:, :],
                                scalar1=4 * s_, scalar2=15,
                                op0=ALU.logical_shift_right, op1=ALU.bitwise_and),
                                waits=[q_rdy, dq.get(("w4", rp))], incs=[s_dve.inc()])
                            u_done = s_dve.tgt()
                            P.add("vector", lambda e, src=src, par=par, kt=kt, rp=rp, bb=bb: e.tensor_mul(
                                wc[src][par][:, 256 * kt:256 * (kt + 1)],
                                w4i[rp][:, :], ps[:, bb:bb + 256]),
                                waits=[u_done, bc_done, dq.get(("wcuse", src, par))],
                                incs=[s_dve.inc()])
                            dq[("w4", rp)] = s_dve.tgt()
                            dq[("qst", rp)] = s_dve.tgt()
                        dq[("bfree", rp)] = s_dve.tgt()
                        dq[("dqdone", blk)] = s_dve.tgt()

                    def mm_group(blk, j, gate_req):
                        nt, tc, src = j // 4, (j // 2) % 2, j % 2
                        par = blk % 2
                        n0 = 256 * blk
                        gidx = blk * 8 + j
                        reg = gidx % 6
                        rb = 512 * reg
                        hpar = (2 * blk + nt) % 2
                        w0 = [dq[("dqdone", blk)], regfree.get(reg)]
                        if j == 0:
                            w0.append(dq.get(("hfree", hpar)))
                        first_gate = None
                        for kt in range(32):
                            inc = []
                            if kt == 0 and gate_req:
                                inc = [s_pe.inc()]
                            P.add("tensor", lambda e, src=src, par=par, kt=kt, nt=nt, tc=tc, rb=rb: e.matmul(
                                ps[:, rb:rb + 512],
                                wc[src][par][:, 256 * kt + 128 * nt:256 * kt + 128 * (nt + 1)],
                                sb_x1bf[:, NT * kt + 512 * tc:NT * kt + 512 * (tc + 1)],
                                start=(kt == 0), stop=False),
                                waits=w0 if kt == 0 else [],
                                incs=inc)
                            if kt == 0 and gate_req:
                                first_gate = s_pe.tgt()
                        P.add("tensor", lambda e, src=src, nt=nt, tc=tc, rb=rb, n0=n0: e.matmul(
                            ps[:, rb:rb + 512],
                            guq_z[32 * src:32 * (src + 1),
                                  n0 + 128 * nt:n0 + 128 * (nt + 1)],
                            xsn[32 * src:32 * (src + 1), 512 * tc:512 * (tc + 1)],
                            start=False, stop=True),
                            incs=[s_pe.inc()])
                        dq[("mm", j)] = s_pe.tgt()
                        dq[("wcuse", 0, par)] = s_pe.tgt()
                        dq[("wcuse", 1, par)] = s_pe.tgt()
                        return first_gate

                    def plan_evac(blk, j, gate):
                        nt, tc = j // 4, (j // 2) % 2
                        gidx = blk * 8 + j
                        g_reg = 512 * ((gidx - 1) % 6)
                        u_reg = 512 * (gidx % 6)
                        n0 = 256 * blk
                        hpar = (2 * blk + nt) % 2
                        P.add("scalar", lambda e, g_reg=g_reg: e.activation(
                            out=silt[:, :], in_=ps[:, g_reg:g_reg + 512],
                            func=AF.Silu),
                            waits=[gate, dq.get("sil_free")],
                            incs=[s_act.inc()])
                        sil_rdy = s_act.tgt()
                        hw = [sil_rdy]
                        if tc == 0:
                            hw.append(dq.get(("hfree", hpar)))
                        P.add("vector", lambda e, u_reg=u_reg, tc=tc, hpar=hpar: e.tensor_mul(
                            hst[hpar][:, 512 * tc:512 * (tc + 1)],
                            silt[:, :], ps[:, u_reg:u_reg + 512]),
                            waits=hw, incs=[s_dve.inc()])
                        dq["sil_free"] = s_dve.tgt()
                        regfree[(gidx - 1) % 6] = s_dve.tgt()
                        regfree[gidx % 6] = s_dve.tgt()
                        if tc == 1:
                            hdone = s_dve.tgt()
                            P.add("sync", lambda e, n0=n0, nt=nt, hpar=hpar: e.dma_start(
                                out=bass.AP(h_d, (n0 + 128 * nt) * NT,
                                            [[NT, 128], [1, NT]]),
                                in_=hst[hpar][:, :]),
                                waits=[hdone], incs=[HS[hpar].inc(16)])
                            dq[("hfree", hpar)] = HS[hpar].tgt()

                    plan_bcast(0)
                    plan_bcast(1)
                    for r in range(8):
                        dequant_round(r)
                        plan_bcast(r + 2)
                    regfree[0] = xs_evac
                    regfree[1] = xs_evac
                    regfree[2] = xs_evac
                    pending = None
                    for blk in range(NBLK):
                        for j in range(8):
                            if blk + 1 < NBLK:
                                dequant_round((blk + 1) * 8 + j)
                            gate = mm_group(blk, j, pending is not None)
                            if blk + 1 < NBLK:
                                plan_bcast((blk + 1) * 8 + j + 2)
                            if pending is not None:
                                plan_evac(*pending, gate)
                                pending = None
                            if j % 2 == 1:
                                pending = (blk, j)
                    P.add("tensor", lambda e: e.nop(), incs=[s_pe.inc()], drain=True)
                    plan_evac(*pending, s_pe.tgt())
                    pending = None
                    P.add("vector", lambda e: e.memset(hst[0][:, :], 0.0),
                          waits=[dq.get(("hfree", 0))], incs=[s_dve.inc()])
                    zt = s_dve.tgt()
                    for pr in range(2):
                        P.add("sync", lambda e, pr=pr: e.dma_start(
                            out=bass.AP(h_d, (INTER + 128 * pr) * NT,
                                        [[NT, 128], [1, NT]]),
                            in_=hst[0][:, :]),
                            waits=[zt], incs=[HS[0].inc(16)])
                    P.add("sync", lambda e: e.nop(),
                          waits=[HS[0].tgt(), HS[1].tgt()])
                    emit(nc, P)

            with ExitStack() as st:
                E = st.enter_context
                sb_seldn = E(nc.sbuf_tensor("sb_seldn", [88, 11 * 128], BF16))
                scd_f = E(nc.sbuf_tensor("scd_f", [88, D], F32))
                scd_bf = E(nc.sbuf_tensor("scd_bf", [88, D], BF16))
                qst = [E(nc.sbuf_tensor(f"qstD{i}", [128, 512], I32)) for i in range(2)]
                w4i = [E(nc.sbuf_tensor(f"w4iD{i}", [128, 512], I32)) for i in range(2)]
                wds = [E(nc.sbuf_tensor(f"wdsD{i}", [128, 512], BF16)) for i in range(4)]
                ps = E(nc.psum_tensor("psP6", [128, 4096], F32))
                P = Plan()
                P.add("sync", lambda e: e.dma_start(out=sb_seldn[:, :], in_=sel_dn.ap()),
                      incs=[ml.inc(16)])
                P.add("sync", lambda e: e.dma_start(out=scd_f[:, :], in_=sc_down.ap()),
                      incs=[ml.inc(16)])
                P.add("vector", lambda e: e.tensor_copy(scd_bf[:, :], scd_f[:, :]),
                      waits=[ml.tgt()], incs=[s_dve.inc()])
                pre6 = s_dve.tgt()
                P.add("tensor", lambda e: e.nop(), waits=[pre6])
                P.add("gpsimd", lambda e: e.nop(), waits=[pre6])
                st6 = {}
                for cp in range(11):
                    for dch in range(8):
                        rp = (cp * 8 + dch) % 2
                        P.add("sync", lambda e, cp=cp, dch=dch, rp=rp: e.dma_start(
                            out=qst[rp][:, :],
                            in_=bass.AP(qw_down, 128 * cp * D + 512 * dch,
                                        [[D, 128], [1, 512]])),
                            waits=[st6.get(("qst", rp))], incs=[LD[rp].inc(16)])
                        q_rdy = LD[rp].tgt()
                        P.add("tensor", lambda e, cp=cp, dch=dch: e.matmul(
                            ps[:, 3072:3072 + 512],
                            sb_seldn[:, 128 * cp:128 * (cp + 1)],
                            scd_bf[:, 512 * dch:512 * (dch + 1)],
                            start=True, stop=True),
                            waits=[q_rdy, st6.get("aff_done")],
                            incs=[s_pe.inc()], drain=True)
                        bc_done = s_pe.tgt()
                        for s_ in range(8):
                            jt = s_ * 11 + cp
                            wp = (cp * 64 + dch * 8 + s_) % 4
                            P.add("vector", lambda e, s_=s_, rp=rp: e.tensor_scalar(
                                out=w4i[rp][:, :], in0=qst[rp][:, :],
                                scalar1=4 * s_, scalar2=15,
                                op0=ALU.logical_shift_right, op1=ALU.bitwise_and),
                                waits=[q_rdy, st6.get(("w4", rp))], incs=[s_dve.inc()])
                            u_done = s_dve.tgt()
                            P.add("vector", lambda e, rp=rp, wp=wp: e.tensor_mul(
                                wds[wp][:, :], w4i[rp][:, :], ps[:, 3072:3072 + 512]),
                                waits=[u_done, bc_done, st6.get(("wds", wp))],
                                incs=[s_dve.inc()])
                            st6[("w4", rp)] = s_dve.tgt()
                            st6[("qst", rp)] = s_dve.tgt()
                            aff = s_dve.tgt()
                            P.add("sync", lambda e, jt=jt, dch=dch, wp=wp: e.dma_start(
                                out=wd_d.ap()[jt][:, 512 * dch:512 * (dch + 1)],
                                in_=wds[wp][:, :]),
                                waits=[aff], incs=[ST[wp].inc(16)])
                            st6[("wds", wp)] = ST[wp].tgt()
                        st6["aff_done"] = s_dve.tgt()
                P.add("sync", lambda e: e.nop(),
                      waits=[ST[0].tgt(), ST[1].tgt(), ST[2].tgt(), ST[3].tgt()])
                emit(nc, P)

        guq_stack.close()

        with ExitStack() as st78:
            sb_r2 = st78.enter_context(nc.sbuf_tensor("sb_r2", [128, 32 * NT], F32))
            with ExitStack() as st:
                E = st.enter_context
                wdt = [E(nc.sbuf_tensor(f"wdt{i}", [128, 512], BF16)) for i in range(4)]
                htl = [E(nc.sbuf_tensor(f"htl{i}", [128, 512], BF16)) for i in range(4)]
                x1t = [E(nc.sbuf_tensor(f"x1t7{i}", [128, 512], F32)) for i in range(2)]
                sb_selTdn = E(nc.sbuf_tensor("sb_selTdn", [128, 11 * 88], BF16))
                zd7 = E(nc.sbuf_tensor("zd7", [88, 4096 + 512], I32))
                scd7 = E(nc.sbuf_tensor("scd7", [88, D], F32))
                zsd7 = E(nc.sbuf_tensor("zsd7", [88, D], BF16))
                hsn = E(nc.sbuf_tensor("hsn", [88, 2 * 1024], BF16))
                hsf = E(nc.sbuf_tensor("hsf", [88, 1024], F32))
                hstile = E(nc.sbuf_tensor("hstile", [128, NT], BF16))
                P = Plan()
                st7 = {}
                P.add("sync", lambda e: e.dma_start(out=sb_selTdn[:, :], in_=selT_dn.ap()),
                      incs=[ml.inc(16)])
                P.add("sync", lambda e: e.dma_start(out=scd7[:, :], in_=sc_down.ap()),
                      incs=[ml.inc(16)])
                P.add("sync", lambda e: e.dma_start(
                    out=zd7[:, 4096:4096 + 512], in_=qz_down.ap()),
                    incs=[ml.inc(16)])
                P.add("vector", lambda e: e.nop(), waits=[ml.tgt()])
                for s_ in range(8):
                    P.add("vector", lambda e, s_=s_: e.tensor_scalar(
                        out=bass.AP(zd7, s_, [[4096 + 512, 88], [8, 512]]),
                        in0=zd7[:, 4096:4096 + 512], scalar1=4 * s_, scalar2=15,
                        op0=ALU.logical_shift_right, op1=ALU.bitwise_and),
                        incs=[s_dve.inc()] if s_ == 7 else [])
                P.add("vector", lambda e: e.scalar_tensor_tensor(
                    out=zsd7[:, :],
                    in0=bass.AP(zd7, 0, [[4096 + 512, 88], [1, 8], [8, 512]]),
                    scalar=1.0, in1=scd7[:, :], op0=ALU.add, op1=ALU.mult),
                    waits=[s_dve.tgt()], incs=[s_dve.inc()])
                zsd_rdy = s_dve.tgt()
                P.add("tensor", lambda e: e.nop(), waits=[zsd_rdy])
                hs_last = None
                for jt in range(88):
                    s_, cp = jt // 11, jt % 11
                    hp7 = jt % 2
                    P.add("sync", lambda e, s_=s_, cp=cp: e.dma_start(
                        out=hstile[:, :],
                        in_=bass.AP(h_d, (1024 * cp + s_) * NT,
                                    [[8 * NT, 128], [1, NT]])),
                        waits=[hs_last], incs=[HS[0].inc(16)])
                    h_rdy = HS[0].tgt()
                    for tc in range(2):
                        P.add("tensor", lambda e, jt=jt, cp=cp, tc=tc: e.matmul(
                            ps[0:88, 1024 * tc:1024 * tc + 512],
                            sb_selTdn[:, 88 * cp:88 * (cp + 1)],
                            hstile[:, 512 * tc:512 * (tc + 1)],
                            start=(jt == 0), stop=(jt == 87)),
                            waits=[h_rdy] if tc == 0 else [],
                            incs=[s_pe.inc()] if tc == 1 else [],
                            drain=(jt == 87 and tc == 1))
                    hs_last = s_pe.tgt()
                hs_done = s_pe.tgt()
                P.add("vector", lambda e: e.tensor_scalar_mul(
                    hsf[:, 0:512], ps[0:88, 0:512], -1.0), waits=[hs_done])
                P.add("vector", lambda e: e.tensor_scalar_mul(
                    hsf[:, 512:1024], ps[0:88, 1024:1536], -1.0),
                    incs=[s_dve.inc()])
                P.add("vector", lambda e: e.tensor_copy(
                    hsn[:, 0:1024], hsf[:, :]), waits=[s_dve.tgt()],
                    incs=[s_dve.inc()])
                P.add("vector", lambda e: e.tensor_sub(
                    hsn[:, 1024:2048], hsf[:, :], hsn[:, 0:1024]),
                    waits=[s_dve.tgt()], incs=[s_dve.inc()])
                hsn_rdy = s_dve.tgt()
                st7[("evd", 0)] = hsn_rdy
                st7[("evd", 1)] = hsn_rdy

                for th in range(2):
                    for ds in range(8):
                        par = (th * 8 + ds) % 2
                        base = 2048 * par
                        for jt in range(88):
                            bp = jt % 4
                            P.add("sync", lambda e, jt=jt, ds=ds, bp=bp: e.dma_start(
                                out=wdt[bp][:, :],
                                in_=wd_d.ap()[jt][:, 512 * ds:512 * (ds + 1)]),
                                waits=[st7.get(("wdt", bp))], incs=[W[bp].inc(16)])
                            s_ = jt // 11
                            cp = jt % 11
                            P.add("sync", lambda e, s_=s_, cp=cp, th=th, bp=bp: e.dma_start(
                                out=htl[bp][:, :],
                                in_=bass.AP(h_d, (1024 * cp + s_) * NT + 512 * th,
                                            [[8 * NT, 128], [1, 512]])),
                                waits=[st7.get(("wdt", bp))], incs=[LD[bp].inc(16)])
                            t_rdy = (W[bp].tgt(), LD[bp].tgt())
                            mw = [t_rdy[0], t_rdy[1]]
                            if jt == 0:
                                mw.append(st7.get(("evd", par)))
                            for dd in range(4):
                                P.add("tensor", lambda e, dd=dd, bp=bp, base=base, jt=jt: e.matmul(
                                    ps[:, base + 512 * dd:base + 512 * (dd + 1)],
                                    wdt[bp][:, 128 * dd:128 * (dd + 1)],
                                    htl[bp][:, :],
                                    start=(jt == 0), stop=False),
                                    waits=mw if dd == 0 else [],
                                    incs=[s_pe.inc()] if dd == 3 else [])
                            st7[("wdt", bp)] = s_pe.tgt()
                        for dd in range(4):
                            P.add("tensor", lambda e, dd=dd, ds=ds, th=th, base=base: e.matmul(
                                ps[:, base + 512 * dd:base + 512 * (dd + 1)],
                                zsd7[:, 512 * ds + 128 * dd:512 * ds + 128 * (dd + 1)],
                                hsn[0:88, 512 * th:512 * (th + 1)],
                                start=False, stop=False))
                            P.add("tensor", lambda e, dd=dd, ds=ds, th=th, base=base: e.matmul(
                                ps[:, base + 512 * dd:base + 512 * (dd + 1)],
                                zsd7[:, 512 * ds + 128 * dd:512 * ds + 128 * (dd + 1)],
                                hsn[0:88, 1024 + 512 * th:1024 + 512 * (th + 1)],
                                start=False, stop=True),
                                incs=[s_pe.inc()] if dd == 3 else [],
                                drain=(dd == 3))
                        mm_done = s_pe.tgt()
                        for dd in range(4):
                            dt = 4 * ds + dd
                            xp = dd % 2
                            P.add("sync", lambda e, dt=dt, th=th, xp=xp: e.dma_start(
                                out=x1t[xp][:, :],
                                in_=x1_d.ap()[dt][:, 512 * th:512 * (th + 1)]),
                                waits=[st7.get(("x1t", xp))], incs=[X[xp].inc(16)])
                            x_rdy = X[xp].tgt()
                            P.add("vector", lambda e, dt=dt, th=th, dd=dd, xp=xp, base=base: e.tensor_add(
                                sb_r2[:, NT * dt + 512 * th:NT * dt + 512 * (th + 1)],
                                x1t[xp][:, :],
                                ps[:, base + 512 * dd:base + 512 * (dd + 1)]),
                                waits=[mm_done, x_rdy], incs=[s_dve.inc()])
                            st7[("x1t", xp)] = s_dve.tgt()
                        st7[("evd", par)] = s_dve.tgt()
                P.add("sync", lambda e: e.nop(),
                      waits=[W[0].tgt(), W[1].tgt(), W[2].tgt(), W[3].tgt(),
                             LD[0].tgt(), LD[1].tgt(), LD[2].tgt(), LD[3].tgt(),
                             X[0].tgt(), X[1].tgt()])
                emit(nc, P)

            if TAPS:
                P = Plan()
                P.add("sync", lambda e: e.dma_start(out=r2_tap.ap(), in_=sb_r2[:, :]),
                      incs=[ml.inc(16)])
                P.add("sync", lambda e: e.nop(), waits=[ml.tgt()])
                emit(nc, P)

            with ExitStack() as st:
                E = st.enter_context
                r2b = E(nc.sbuf_tensor("r2b8", [128, NT], BF16))
                r2sq = E(nc.sbuf_tensor("r2sq8", [128, NT], BF16))
                rows8 = E(nc.sbuf_tensor("rows8", [1, 6 * NT], F32))
                outst = [E(nc.sbuf_tensor(f"outst{i}", [128, NT], F32)) for i in range(2)]
                ab8_sb = E(nc.sbuf_tensor("ab8_sb", [128, 2 * NT], F32))
                ps = E(nc.psum_tensor("psP8", [128, 4096], F32))
                P = Plan()
                cast_free = None
                sq_free = None
                for dt in range(32):
                    P.add("vector", lambda e, dt=dt: e.tensor_copy(
                        r2b[:, :], sb_r2[:, NT * dt:NT * (dt + 1)]),
                        waits=[cast_free], incs=[s_dve.inc()])
                    b_rdy = s_dve.tgt()
                    P.add("scalar", lambda e, dt=dt: e.activation(
                        out=r2sq[:, :], in_=sb_r2[:, NT * dt:NT * (dt + 1)],
                        func=AF.Square), waits=[sq_free], incs=[s_act.inc()])
                    s_rdy = s_act.tgt()
                    for c in range(2):
                        P.add("tensor", lambda e, c=c, dt=dt: e.matmul(
                            ps[0:1, 2048 + 512 * c:2048 + 512 * (c + 1)], sb_1bf[:, :],
                            r2b[:, 512 * c:512 * (c + 1)],
                            start=(dt == 0), stop=(dt == 31)),
                            waits=[b_rdy] if c == 0 else [])
                        P.add("tensor", lambda e, c=c, dt=dt: e.matmul(
                            ps[0:1, 3072 + 512 * c:3072 + 512 * (c + 1)], sb_1bf[:, :],
                            r2sq[:, 512 * c:512 * (c + 1)],
                            start=(dt == 0), stop=(dt == 31)),
                            waits=[s_rdy] if c == 0 else [],
                            incs=[s_pe.inc()] if c == 1 else [],
                            drain=(c == 1 and dt == 31))
                    cast_free = s_pe.tgt()
                    sq_free = s_pe.tgt()
                stats_done = s_pe.tgt()
                ab8 = ab_chain(P, rows8, NT, stats_done,
                               ps[0:1, 2048:2048 + NT], ps[0:1, 3072:3072 + NT])
                b8done = ab_bounce(P, rows8, NT, ab8, ab8_sb)
                ap_done = [None, None]
                of_free = [None, None]
                for dt in range(32):
                    par = dt % 2
                    P.add("vector", lambda e, dt=dt, par=par: e.tensor_mul(
                        outst[par][:, :], sb_r2[:, NT * dt:NT * (dt + 1)],
                        ab8_sb[:, 0:NT]),
                        waits=[b8done, ap_done[par], of_free[par]], incs=[s_dve.inc()])
                    P.add("vector", lambda e, par=par: e.tensor_add(
                        outst[par][:, :], outst[par][:, :], ab8_sb[:, NT:2 * NT]),
                        waits=[s_dve.tgt()], incs=[s_dve.inc()])
                    ap_done[par] = s_dve.tgt()
                    P.add("sync", lambda e, dt=dt, par=par: e.dma_start(
                        out=out_t.ap()[128 * dt:128 * (dt + 1), :], in_=outst[par][:, :]),
                        waits=[ap_done[par]], incs=[ST[par].inc(16)])
                    of_free[par] = ST[par].tgt()
                P.add("sync", lambda e: e.nop(),
                      waits=[ST[0].tgt(), ST[1].tgt()])
                emit(nc, P)

    return nc



def _staircase_sel(n_rows, n_c):
    sel = np.zeros((n_rows, n_c * 128), dtype=BF)
    for c in range(n_c):
        for p in range(128):
            g = 8 * c + p // 16
            if g < n_rows:
                sel[g, 128 * c + p] = 1.0
    return sel


def _staircase_selT(n_rows, n_c, double):
    w = 2 * n_rows if double else n_rows
    sel = np.zeros((128, n_c * w), dtype=BF)
    for c in range(n_c):
        for p in range(128):
            g = 8 * c + p // 16
            if g < n_rows:
                sel[p, w * c + g] = 1.0
                if double:
                    sel[p, w * c + n_rows + g] = 1.0
    return sel


def _rel_tiles(m):
    rel = np.full((N_REL, 128, 512), NEG_BIG, dtype=np.float32)
    strips = STRIPS[m]
    for i in range(8):
        q0 = 128 * strips[i]
        for c in range(SLOT_CHUNKS[i]):
            k0 = 512 * c
            qq = np.arange(128)[:, None]
            kk = np.arange(512)[None, :]
            kabs = k0 + kk
            qabs = q0 + qq
            r = (kabs - qabs).astype(np.float32)
            r[kabs > qabs] = NEG_BIG
            rel[REL_BASE[i] + c] = r
    return rel


def _prep_inputs(inputs):
    ins = {k: np.asarray(v) for k, v in inputs.items()}
    gi = np.arange(4096) // 128
    assert np.array_equal(ins["L0_ffn_gate_g_idx"], gi)
    assert np.array_equal(ins["L0_ffn_up_g_idx"], gi)
    assert np.array_equal(ins["L0_ffn_down_g_idx"], np.arange(INTER) // 128)

    x = np.asarray(ins["L0_attn_hidden_states"], dtype=np.float32)
    wqkvT = np.ascontiguousarray(ins["L0_attn_Wqkv"].T)[PERM_D]
    woutT = np.ascontiguousarray(ins["L0_attn_Wout"].T)[:, PERM_D]

    w1 = wqkvT.reshape(32, 128, 96, 128)
    wqkv_st = np.ascontiguousarray(w1.transpose(2, 1, 0, 3).reshape(96, 128, 32 * 128))
    w2 = woutT.reshape(32, 128, 32, 128)
    wout_st = np.ascontiguousarray(w2.transpose(2, 1, 0, 3).reshape(32, 128, 32 * 128))

    def col32(v):
        return np.ascontiguousarray(np.asarray(v, dtype=np.float32).reshape(32, 128).T)

    sc_guq = np.zeros((128, INTER), dtype=np.float32)
    sc_guq[0:32] = ins["L0_ffn_gate_scales"]
    sc_guq[32:64] = ins["L0_ffn_up_scales"]

    qwd = np.zeros((1408, D), dtype=np.int32)
    qwd[:1376] = ins["L0_ffn_down_qweight"]
    qwd = np.ascontiguousarray(qwd[:, PERM_D])
    scd = np.zeros((88, D), dtype=np.float32)
    scd[:86] = ins["L0_ffn_down_scales"]
    scd = np.ascontiguousarray(scd[:, PERM_D])
    qzd = np.zeros((88, 512), dtype=np.int32)
    qzd[:86] = ins["L0_ffn_down_qzeros"]

    shared = dict(
        wqkv_st=wqkv_st.astype(np.float32),
        wout_st=wout_st.astype(np.float32),
        qlnw=col32(ins["L0_attn_q_ln_w"]), qlnb=col32(ins["L0_attn_q_ln_b"]),
        klnw=col32(ins["L0_attn_k_ln_w"]), klnb=col32(ins["L0_attn_k_ln_b"]),
        qw_gate=np.ascontiguousarray(ins["L0_ffn_gate_qweight"], dtype=np.int32),
        qw_up=np.ascontiguousarray(ins["L0_ffn_up_qweight"], dtype=np.int32),
        qw_down=qwd, sc_guq=sc_guq, sc_down=scd,
        qz_gate=np.ascontiguousarray(ins["L0_ffn_gate_qzeros"], dtype=np.int32),
        qz_up=np.ascontiguousarray(ins["L0_ffn_up_qzeros"], dtype=np.int32),
        qz_down=qzd,
        sel_gu=np.tile(_staircase_sel(32, 4), (2, 1)),
        sel_dn=_staircase_sel(88, 11),
        selT_gu=_staircase_selT(32, 4, double=True),
        selT_dn=_staircase_selT(88, 11, double=False),
        ident_u8=np.eye(128, dtype=np.uint8),
        ident_bf=np.eye(128, dtype=BF),
        ones_bf=np.ones((128, 1), dtype=BF),
        ones1f=np.ones((1, 128), dtype=np.float32),
    )
    rel = {m: _rel_tiles(m) for m in (0, 1)}

    in_maps, metas = [], []
    for core in range(8):
        b, m = core // 2, core % 2
        strips = STRIPS[m]
        tok = np.concatenate([np.arange(128 * s_, 128 * s_ + 128) for s_ in strips])
        xb = x[b]
        im = dict(shared)
        im["x_own"] = np.ascontiguousarray(xb[tok].T[PERM_D])
        im["x_full"] = np.ascontiguousarray(xb.T[PERM_D])
        im["rel"] = rel[m]
        in_maps.append(im)
        metas.append((b, tok))
    return in_maps, metas


def kernel(**inputs):
    in_maps, metas = _prep_inputs(inputs)
    nc = build_nc()
    res = run_bass_kernel_spmd(nc, in_maps, core_ids=list(range(8)))
    out = np.zeros((B, S, D), dtype=np.float32)
    inv = np.empty(D, dtype=np.int64)
    inv[PERM_D] = np.arange(D)
    for core in range(8):
        b, tok = metas[core]
        o = np.asarray(res.results[core]["out"], dtype=np.float32)
        out[b, tok, :] = o[inv, :].T
    return out
